# version 14
# baseline (speedup 1.0000x reference)
import contextlib
import numpy as np
import ml_dtypes
import concourse.bass as bass
import concourse.mybir as mybir
from concourse.bass_utils import run_bass_kernel_spmd

F32 = mybir.dt.float32
BF16 = mybir.dt.bfloat16
AF = mybir.ActivationFunctionType
ALU = mybir.AluOpType
BF = ml_dtypes.bfloat16

D = 1024
DC = 8
E2 = 2048
NST = 128
HID = 4096
HC = 32
BLOC = 2
L = 4096
NTOK = BLOC * L
TA = 512
NTA = NTOK // TA
TB = 256
NTB = NTOK // TB
MAGIC = 12582912.0
EPS = 1e-5
DT_MAX = 10.0
RATE_EPS = 1e-4

ENGS = ("T", "V", "S", "G", "Y")


def build_nc(debug=False):
    nc = bass.Bass("TRN2", target_bir_lowering=False)

    xT = nc.dram_tensor("xT", [D, NTOK], F32, kind="ExternalInput").ap()
    winT = nc.dram_tensor("winT", [D, E2], F32, kind="ExternalInput").ap()
    wstT = nc.dram_tensor("wstT", [D, NST], F32, kind="ExternalInput").ap()
    wdtT = nc.dram_tensor("wdtT", [D, NST], F32, kind="ExternalInput").ap()
    wcT = nc.dram_tensor("wcT", [NST, D], F32, kind="ExternalInput").ap()
    woutT = nc.dram_tensor("woutT", [D, D], F32, kind="ExternalInput").ap()
    w1T = nc.dram_tensor("w1T", [D, HID], F32, kind="ExternalInput").ap()
    w2T = nc.dram_tensor("w2T", [HID, D], F32, kind="ExternalInput").ap()
    vecs = nc.dram_tensor("vecs", [128, 64], F32, kind="ExternalInput").ap()
    outT = nc.dram_tensor("outT", [D, NTOK], F32, kind="ExternalOutput").ap()
    r1s = nc.dram_tensor("r1s", [D, NTOK], BF16, kind="Internal").ap()
    w1qs = nc.dram_tensor("w1qs", [128, DC, HID], BF16, kind="Internal").ap()
    w2qs = nc.dram_tensor("w2qs", [128, HC, D], BF16, kind="Internal").ap()

    dbg = {}
    if debug:
        dbg["u0"] = nc.dram_tensor("dbg_u0", [D, TA], BF16, kind="ExternalOutput").ap()
        dbg["dec0"] = nc.dram_tensor("dbg_dec0", [NST, TA], F32, kind="ExternalOutput").ap()
        dbg["h0"] = nc.dram_tensor("dbg_h0", [NST, TA], F32, kind="ExternalOutput").ap()
        dbg["z0"] = nc.dram_tensor("dbg_z0", [D, TA], BF16, kind="ExternalOutput").ap()
        dbg["x10"] = nc.dram_tensor("dbg_x10", [D, TB], F32, kind="ExternalOutput").ap()
        dbg["hq0"] = nc.dram_tensor("dbg_hq0", [HID, TB], BF16, kind="ExternalOutput").ap()
        dbg["xb0"] = nc.dram_tensor("dbg_xb0", [D, TA], BF16, kind="ExternalOutput").ap()
        dbg["u0e"] = nc.dram_tensor("dbg_u0e", [D, TA], BF16, kind="ExternalOutput").ap()
        dbg["w0"] = nc.dram_tensor("dbg_w0", [128, 2048], BF16, kind="ExternalOutput").ap()

    SB_TOP = 229344
    P_BASE = 16512
    ptr = {"P": P_BASE}
    handles = {}

    def nbytes_pp(shape, dtype):
        n = 1
        for s in shape[1:]:
            n *= s
        return n * (2 if dtype == BF16 else 4)

    def sb(name, shape, dtype, region, at=None):
        if at is None:
            off = (ptr[region] + 31) // 32 * 32
            ptr[region] = off + nbytes_pp(shape, dtype)
        else:
            off = at
        assert off + nbytes_pp(shape, dtype) <= SB_TOP, (name, off)
        h = nc.alloc_sbuf_tensor_at(name, shape, dtype, offset=off)
        handles[name] = (h, off)
        return h

    vecs_sb = sb("vecs_sb", [128, 64], F32, "P")
    negrate = sb("negrate", [128, 1], F32, "P")
    tmpc1 = sb("tmpc1", [128, 1], F32, "P")
    tmpc2 = sb("tmpc2", [128, 1], F32, "P")
    b1q16 = sb("b1q16", [128, 32], F32, "P")
    b2q = sb("b2q", [128, 8], F32, "P")
    ones_bf = sb("ones_bf", [128, 1], BF16, "P")
    cb_one = sb("cb_one", [128, 1], F32, "P")
    cb_eps = sb("cb_eps", [128, 1], F32, "P")
    ones_row = sb("ones_row", [1, 128], F32, "P")
    rows1 = tuple(sb("row1_%d" % i, [1, 256], F32, "P") for i in range(4))
    rows2 = rows1

    overlay = (ptr["P"] + 31) // 32 * 32
    ptr["A"] = overlay
    ptr["B"] = overlay

    winT_sb = sb("winT_sb", [128, DC, E2], BF16, "A")
    wstT_sb = sb("wstT_sb", [128, DC, NST], BF16, "A")
    wdtT_sb = sb("wdtT_sb", [128, DC, NST], BF16, "A")
    wcT_sb = sb("wcT_sb", [128, D], BF16, "A")
    woutT_sb = sb("woutT_sb", [128, DC, D], BF16, "A")
    xb = sb("xb", [128, 3, DC, TA], BF16, "A")
    u_bf = sb("u_bf", [128, 2, DC, TA], BF16, "A")
    sg = sb("sg", [128, 2, DC, TA], BF16, "A")
    z_sb = sb("z_sb", [128, DC, TA], BF16, "A")
    ztmp8 = sb("ztmp8", [128, DC, TA], F32, "A")
    r1_sb = sb("r1_sb", [128, 2, DC, TA], BF16, "A")
    sp_t = sb("sp_t", [128, TA], F32, "A")
    e1_t = sb("e1_t", [128, TA], F32, "A")
    ld_t = sb("ld_t", [128, TA], F32, "A")
    dec_t = sb("dec_t", [128, 2, TA], F32, "A")
    omd_t = sb("omd_t", [128, TA], F32, "A")
    c_t = sb("c_t", [128, TA], F32, "A")
    h_t = sb("h_t", [128, 2, TA], F32, "A")
    hb_t = sb("hb_t", [128, 2, TA], BF16, "A")

    w1q = sb("w1q", [128, DC, HID], BF16, "B")
    w2q = sb("w2q", [128, HC, D], BF16, "B")
    r1ld = sb("r1ld", [128, 2, DC, TB], BF16, "B")
    x1_sb = sb("x1_sb", [128, 2, DC, TB], F32, "B")
    xq_sb = sb("xq_sb", [128, 2, DC, TB], BF16, "B")
    hq_sb = sb("hq_sb", [128, HC, TB], BF16, "B")
    r2_sb = sb("r2_sb", [128, DC, TB], BF16, "B")
    ob = sb("ob", [128, DC, TB], F32, "B")
    htmp8b = sb("htmp8b", [128, 4, TB], F32, "B")
    sqb1 = sb("sqb1", [128, DC, TB], BF16, "B")
    sqb2 = sb("sqb2", [128, DC, TB], BF16, "B")

    _wsbase = handles["sp_t"][1]
    wstg = [
        sb("wstg_a", [128, 512], F32, "A", at=handles["sqb1"][1]),
        sb("wstg_b", [128, 512], F32, "A", at=handles["sqb2"][1]),
    ]
    woutq = [
        sb("woutq_a", [128, 512], BF16, "A", at=handles["htmp8b"][1]),
        sb("woutq_b", [128, 512], BF16, "A", at=handles["htmp8b"][1] + 2048),
    ]
    assert ptr["A"] <= SB_TOP and ptr["B"] <= SB_TOP

    with contextlib.ExitStack() as st:
        en = st.enter_context
        pb = [en(nc.psum_tensor(f"pb{i}", [128, 512], F32)) for i in range(8)]
        sems = {k: en(nc.semaphore(k)) for k in (
            "qw", "qx0", "qx1", "qx2", "qa0", "qa1", "qy0", "qy1", "qo",
            "qt0", "qt1", "qws0", "qws1", "qwr", "qw2r", "qd",
            "tm", "sd", "vd")}

        prog = {e: [] for e in ENGS}
        cnt = {k: 0 for k in sems}
        waited = {e: {} for e in ENGS}
        own_sem = {"T": "tm", "V": "vd", "S": "sd"}

        def emit(eng, fn):
            prog[eng].append(fn)

        def wait(eng, sem, val, force=False, selfdep=False):
            if val is None or val <= 0:
                return
            if not force and not selfdep and own_sem.get(eng) == sem:
                return
            if not force and waited[eng].get(sem, 0) >= val:
                return
            waited[eng][sem] = max(waited[eng].get(sem, 0), val)
            emit(eng, lambda E, s=sems[sem], v=val: E.wait_ge(s, v))

        def drn(eng):
            emit(eng, lambda E: E.drain())

        def op(eng, fn, inc=None, inck=1):
            if inc is None:
                emit(eng, fn)
                return None
            cnt[inc] += inck
            emit(eng, lambda E, f=fn, s=sems[inc], k=inck: f(E).then_inc(s, k))
            return cnt[inc]

        bank_owner = [None] * 8
        bank_rr = [0]

        def take_bank():
            b = bank_rr[0] % 8
            bank_rr[0] += 1
            if bank_owner[b] is not None:
                assert bank_owner[b][0] != "__taken__", f"bank {b} re-taken before consumer recorded"
                wait("T", *bank_owner[b])
            bank_owner[b] = ("__taken__", 0)
            return b

        def free_bank(b, sem, val):
            bank_owner[b] = (sem, val)

        def mm_group(lhs_rhs, out_ap, pre_waits=()):
            b = take_bank()
            for s_name, v in pre_waits:
                wait("T", s_name, v)
            n = len(lhs_rhs)
            for i, (lhs, rhs) in enumerate(lhs_rhs):
                kw = dict(start=(i == 0), stop=(i == n - 1))
                if i == n - 1:
                    tv = op("T", lambda E, l=lhs, r=rhs, o=out_ap(b), kw=kw:
                            E.matmul(o, l, r, **kw), inc="tm")
                else:
                    op("T", lambda E, l=lhs, r=rhs, o=out_ap(b), kw=kw:
                       E.matmul(o, l, r, **kw))
            return b, tv

        def full(b):
            return pb[b][:, :]

        def ln_slice(b, T):
            return pb[b][:, 0:T]

        def row_slice(b, T):
            return pb[b][0:1, 0:T]

        def dma(eng, q, out_ap, in_ap, pre=()):
            for s_name, v in pre:
                wait(eng, s_name, v)
            return op(eng, lambda E, o=out_ap, i=in_ap: E.dma_start(out=o, in_=i),
                      inc=q, inck=16)

        dma("G", "qw", winT_sb[:, :, :], winT.rearrange("(k p) e -> p k e", p=128))
        dma("G", "qw", wstT_sb[:, :, :], wstT.rearrange("(k p) n -> p k n", p=128))
        dma("G", "qw", wdtT_sb[:, :, :], wdtT.rearrange("(k p) n -> p k n", p=128))
        dma("G", "qw", wcT_sb[:, :], wcT[:, :])
        dma("G", "qw", woutT_sb[:, :, :], woutT.rearrange("(k p) e -> p k e", p=128))
        qw_all = dma("G", "qw", vecs_sb[:, :], vecs[:, :])
        qs_vecs = None

        op("V", lambda E: E.memset(ones_bf[:], 1.0))
        op("V", lambda E: E.memset(ones_row[:], 1.0))
        op("V", lambda E: E.memset(cb_one[:], 1.0))
        cv = op("V", lambda E: E.memset(cb_eps[:], EPS), inc="vd")
        wait("S", "qw", qw_all)
        wait("S", "vd", cv)
        op("S", lambda E: E.activation(out=tmpc1[:], in_=vecs_sb[:, 1:2], func=AF.Exp))
        drn("S")
        sv = op("S", lambda E: E.activation(out=tmpc2[:], in_=tmpc1[:], func=AF.Ln,
                                            bias=cb_one[:]), inc="sd")
        wait("V", "sd", sv)
        negrate_vd = op("V", lambda E: E.tensor_scalar(out=negrate[:], in0=tmpc2[:],
                                          scalar1=RATE_EPS, scalar2=-1.0,
                                          op0=ALU.add, op1=ALU.mult), inc="vd")
        wait("V", "qw", qw_all)

        def quant_cols(dst, src_ap, postmul):
            v = op("V", lambda E, s=src_ap: E.tensor_scalar(
                out=dst[:], in0=s, scalar1=64.0, scalar2=MAGIC,
                op0=ALU.mult, op1=ALU.add), inc="vd")
            wait("V", "vd", v, selfdep=True)
            v = op("V", lambda E: E.tensor_scalar(
                out=dst[:], in0=dst[:], scalar1=MAGIC, scalar2=127.0,
                op0=ALU.subtract, op1=ALU.min), inc="vd")
            wait("V", "vd", v, selfdep=True)
            v = op("V", lambda E, pm=postmul: E.tensor_scalar(
                out=dst[:], in0=dst[:], scalar1=-128.0, scalar2=pm,
                op0=ALU.max, op1=ALU.mult), inc="vd")
            return v

        quant_cols(b1q16, vecs_sb[:, 10:42], 1.0 / 64.0)
        quant_cols(b2q, vecs_sb[:, 42:50], 1.0 / 64.0)
        op("V", lambda E: E.memset(ztmp8[:, 0, 0:1], 0.0), inc="vd")

        dbgA_qs = [0]
        xb_load = [None] * NTA
        xb_free_T = [None] * NTA
        xb_free_V = [None] * NTA
        r1_done = [None] * NTA
        r1_store = [None] * NTA
        scan_vd = [None] * NTA
        silu_done = [None] * NTA
        attn_groups = [None] * NTA
        z_done = [None] * NTA

        def load_xb(t):
            pre = []
            if t >= 3:
                pre = [("tm", xb_free_T[t - 3]), ("vd", xb_free_V[t - 3])]
            src = xT.rearrange("(c p) t -> p c t", p=128)[:, :, t * TA:(t + 1) * TA]
            xb_load[t] = dma("G", "qx%d" % (t % 3), xb[:, t % 3, :, :], src, pre=pre)

        load_xb(0)
        load_xb(1)
        load_xb(2)

        w1r = w1T.rearrange("(k p) h -> p k h", p=128)
        w2r = w2T.rearrange("(k p) e -> p k e", p=128)
        wq_stages = []
        for k in range(DC):
            for c0 in range(0, HID, 512):
                wq_stages.append((w1r[:, k, c0:c0 + 512], w1qs[:, k, c0:c0 + 512]))
        for k in range(HC):
            for c0 in range(0, D, 512):
                wq_stages.append((w2r[:, k, c0:c0 + 512], w2qs[:, k, c0:c0 + 512]))
        wq_state = {"i": 0, "quant_vd": [], "out_q": []}

        def emit_wq_stages(n):
            for _ in range(n):
                i = wq_state["i"]
                if i >= len(wq_stages):
                    return
                wq_state["i"] += 1
                si = i % 2
                srcap, dstap = wq_stages[i]
                pre = []
                if i >= 2:
                    pre.append(("vd", wq_state["quant_vd"][i - 2]))
                qv = dma("Y", "qt%d" % si, wstg[si][:, :], srcap, pre=pre)
                wait("V", "qt%d" % si, qv)
                if i >= 2:
                    wait("V", "qws%d" % si, wq_state["out_q"][i - 2])
                s = wstg[si][:, :]
                v = op("V", lambda E, s=s: E.tensor_scalar(
                    out=s, in0=s, scalar1=64.0, scalar2=MAGIC,
                    op0=ALU.mult, op1=ALU.add), inc="vd")
                wait("V", "vd", v, selfdep=True)
                v = op("V", lambda E, s=s: E.tensor_scalar(
                    out=s, in0=s, scalar1=MAGIC, scalar2=127.0,
                    op0=ALU.subtract, op1=ALU.min), inc="vd")
                wait("V", "vd", v, selfdep=True)
                v = op("V", lambda E, s=s, o=woutq[si][:, :]: E.tensor_scalar(
                    out=o, in0=s, scalar1=-128.0, scalar2=1.0 / 1024.0,
                    op0=ALU.max, op1=ALU.mult), inc="vd")
                wq_state["quant_vd"].append(v)
                oq = dma("Y", "qws%d" % si, dstap, woutq[si][:, :],
                         pre=[("vd", v)])
                wq_state["out_q"].append(oq)

        for it in range(NTA + 1):
            t = it
            tp = it - 1
            if t < NTA:
                col = slice(t * TA, (t + 1) * TA)
                par = t % 2
                xpar = t % 3
                if debug and t == 1:
                    for _e in ("T", "S", "V", "G"):
                        wait(_e, "qd", cnt["qd"], force=True)
                if debug and t == 2:
                    for _e in ("T", "S", "V", "G"):
                        wait(_e, "qd", dbgA_qs[0])
                wait("T", "qw", qw_all)
                wait("T", "qx%d" % (t % 3), xb_load[t])
                u_copy_done = None
                for e in range(8):
                    mlist = [(winT_sb[:, k, e * 128:(e + 1) * 128], xb[:, xpar, k, :])
                             for k in range(DC)]
                    b, tv = mm_group(mlist, full)
                    wait("S", "tm", tv)
                    sv = op("S", lambda E, b=b, e=e, p=par: E.copy(
                        out=u_bf[:, p, e, :], in_=full(b)), inc="sd")
                    free_bank(b, "sd", sv)
                    u_copy_done = sv
                if debug and t == 0:
                    dma("Y", "qd", dbg["u0e"].rearrange("(c p) t -> p c t", p=128),
                        u_bf[:, 0, :, :], pre=[("sd", u_copy_done)])
                mlist = [(wstT_sb[:, k, :], u_bf[:, par, k, :]) for k in range(DC)]
                bt_b, bt_tv = mm_group(mlist, full, pre_waits=[("sd", u_copy_done)])
                mlist = [(wdtT_sb[:, k, :], u_bf[:, par, k, :]) for k in range(DC)]
                dt_b, dt_tv = mm_group(mlist, full)
                wait("S", "tm", dt_tv)
                sv = op("S", lambda E, b=dt_b: E.activation(
                    out=e1_t[:], in_=full(b), func=AF.Exp,
                    bias=vecs_sb[:, 0:1]), inc="sd")
                free_bank(dt_b, "sd", sv)
                drn("S")
                sv = op("S", lambda E: E.activation(
                    out=sp_t[:], in_=e1_t[:], func=AF.Ln, bias=cb_one[:]), inc="sd")
                wait("V", "sd", sv)
                wait("V", "vd", negrate_vd, selfdep=True)
                vv = op("V", lambda E: E.tensor_scalar(
                    out=ld_t[:], in0=sp_t[:], scalar1=DT_MAX, scalar2=negrate[:],
                    op0=ALU.min, op1=ALU.mult), inc="vd")
                wait("S", "vd", vv)
                sv = op("S", lambda E, p=par: E.activation(
                    out=dec_t[:, p, :], in_=ld_t[:], func=AF.Exp), inc="sd")
                wait("V", "sd", sv)
                omd_vv = op("V", lambda E, p=par: E.tensor_scalar(
                    out=omd_t[:], in0=dec_t[:, p, :], scalar1=1.0, scalar2=-1.0,
                    op0=ALU.subtract, op1=ALU.mult), inc="vd")
                wait("V", "tm", bt_tv)
                wait("V", "vd", omd_vv, selfdep=True)
                c_vv = op("V", lambda E, b=bt_b: E.tensor_tensor(
                    out=c_t[:], in0=omd_t[:], in1=full(b), op=ALU.mult), inc="vd")
                free_bank(bt_b, "vd", c_vv)
                wait("V", "vd", c_vv, selfdep=True)
                if t % (L // TA) == 0:
                    init = 0.0
                else:
                    init = h_t[:, (t - 1) % 2, TA - 1:TA]
                scan_vd[t] = op("V", lambda E, p=par, i=init: E.tensor_tensor_scan(
                    out=h_t[:, p, :], data0=dec_t[:, p, :], data1=c_t[:],
                    initial=i, op0=ALU.mult, op1=ALU.add), inc="vd")
                wait("V", "vd", scan_vd[t], selfdep=True)
                hb_vv = op("V", lambda E, p=par: E.tensor_copy(
                    out=hb_t[:, p, :], in_=h_t[:, p, :]), inc="vd")
                gate_groups = []
                for e in range(8, 16):
                    mlist = [(winT_sb[:, k, e * 128:(e + 1) * 128], xb[:, xpar, k, :])
                             for k in range(DC)]
                    gate_groups.append(mm_group(mlist, full))
                xb_free_T[t] = gate_groups[-1][1]
                for e in range(8):
                    gb, gtv = gate_groups[e]
                    wait("S", "tm", gtv)
                    ssv = op("S", lambda E, b=gb, e=e, p=par: E.activation(
                        out=sg[:, p, e, :], in_=full(b), func=AF.Silu), inc="sd")
                    free_bank(gb, "sd", ssv)
                silu_done[t] = ssv

            if tp >= 0:
                colp = slice(tp * TA, (tp + 1) * TA)
                parp = tp % 2
                attn_groups[tp] = []
                for e in range(DC):
                    mlist = [(woutT_sb[:, k, e * 128:(e + 1) * 128], z_sb[:, k, :])
                             for k in range(DC)]
                    b, tv = mm_group(mlist, full,
                                     pre_waits=[("vd", z_done[tp])])
                    attn_groups[tp].append((b, tv))
                for e in range(DC):
                    b, tv = attn_groups[tp][e]
                    wait("V", "tm", tv)
                    if tp >= 2 and r1_store[tp - 2] is not None:
                        wait("V", "qa%d" % (tp % 2), r1_store[tp - 2])
                    vv = op("V", lambda E, b=b, e=e, p=parp, xp=tp % 3: E.tensor_tensor(
                        out=r1_sb[:, p, e, :], in0=full(b), in1=xb[:, xp, e, :],
                        op=ALU.add), inc="vd")
                    free_bank(b, "vd", vv)
                r1_done[tp] = vv
                xb_free_V[tp] = vv
                dst = r1s.rearrange("(c p) t -> p c t", p=128)[:, :, colp]
                r1_store[tp] = dma("S", "qa%d" % (tp % 2), dst,
                                   r1_sb[:, parp, :, :],
                                   pre=[("vd", r1_done[tp])])
                if tp + 3 < NTA:
                    load_xb(tp + 3)

            if t < NTA:
                if debug and t == 1:
                    wait("V", "qd", dbgA_qs[0])
                y_groups = []
                for e in range(DC):
                    b, tv = mm_group(
                        [(wcT_sb[:, e * 128:(e + 1) * 128], hb_t[:, t % 2, :])],
                        full, pre_waits=[("vd", hb_vv)])
                    y_groups.append((b, tv))
                wait("V", "sd", silu_done[t])
                for e in range(DC):
                    b, tv = y_groups[e]
                    wait("V", "tm", tv)
                    vv = op("V", lambda E, b=b, e=e, p=t % 2: E.tensor_tensor(
                        out=ztmp8[:, e, :], in0=full(b), in1=sg[:, p, e, :],
                        op=ALU.mult), inc="vd")
                    free_bank(b, "vd", vv)
                wait("V", "vd", vv, selfdep=True)
                for e in range(DC):
                    vv = op("V", lambda E, e=e, p=t % 3: E.scalar_tensor_tensor(
                        out=z_sb[:, e, :], in0=xb[:, p, e, :],
                        scalar=vecs_sb[:, 2 + e:3 + e], in1=ztmp8[:, e, :],
                        op0=ALU.mult, op1=ALU.add), inc="vd")
                z_done[t] = vv
                emit_wq_stages(8)
                if debug and t == 0:
                    dma("Y", "qd", dbg["xb0"].rearrange("(c p) t -> p c t", p=128),
                        xb[:, 0, :, :], pre=[("vd", z_done[0])])
                    dma("Y", "qd", dbg["w0"][:, :], winT_sb[:, 0, :],
                        pre=[("vd", z_done[0])])
                    dma("Y", "qd", dbg["u0"].rearrange("(c p) t -> p c t", p=128),
                        u_bf[:, 0, :, :], pre=[("sd", u_copy_done)])
                    dma("Y", "qd", dbg["dec0"][:, :], dec_t[:, 0, :],
                        pre=[("vd", scan_vd[0])])
                    dma("Y", "qd", dbg["h0"][:, :], h_t[:, 0, :],
                        pre=[("vd", scan_vd[0])])
                    dma("Y", "qd", dbg["z0"].rearrange("(c p) t -> p c t", p=128),
                        z_sb[:, :, :], pre=[("vd", z_done[0])])
                    dbgA_qs[0] = cnt["qd"]


        emit_wq_stages(len(wq_stages))
        endA = [("tm", cnt["tm"]), ("vd", cnt["vd"]), ("sd", cnt["sd"]),
                ("qa0", cnt["qa0"]), ("qa1", cnt["qa1"])]
        spill_done = [("qws0", cnt["qws0"]), ("qws1", cnt["qws1"])]
        w1_rl = dma("G", "qwr", w1q[:, :, :], w1qs[:, :, :],
                    pre=endA + spill_done)
        w2_rl = dma("Y", "qw2r", w2q[:, :, :], w2qs[:, :, :],
                    pre=endA + spill_done)

        r1_load = [None] * NTB
        r1ld_free_T = [None] * NTB
        r1ld_free_V = [None] * NTB
        out_store = [None] * NTB
        state = [dict() for _ in range(NTB)]

        def load_r1(tb):
            assert r1_load[tb] is None, tb
            pre = []
            if tb < 3:
                pre += endA
            if tb >= 2:
                pre += [("tm", r1ld_free_T[tb - 2]), ("vd", r1ld_free_V[tb - 2])]
            src = r1s.rearrange("(c p) t -> p c t", p=128)[:, :, tb * TB:(tb + 1) * TB]
            r1_load[tb] = dma("Y", "qy%d" % (tb % 2), r1ld[:, tb % 2, :, :], src, pre=pre)

        load_r1(0)
        load_r1(1)
        wait("V", "qws0", cnt["qws0"])
        wait("V", "qws1", cnt["qws1"])
        prev_htmp_v = [cnt["vd"]]
        prev_htmp_s = [0]

        def emit_sq1(n):
            p = n % 2
            wait("S", "qy%d" % p, r1_load[n])
            sqv = None
            for c in range(DC):
                sqv = op("S", lambda E, c=c, p=p: E.activation(
                    out=sqb1[:, c, :], in_=r1ld[:, p, c, :], func=AF.Square),
                    inc="sd")
            state[n]["sq1_sd"] = sqv

        def rows_and_s(bm, bv, tv_m, tv_v, rows):
            mr, tr, vr, rr = rows
            wait("V", "tm", tv_m)
            vv = op("V", lambda E, b=bm, mr=mr: E.tensor_scalar(
                out=mr[0:1, :], in0=row_slice(b, TB), scalar1=1.0 / D,
                scalar2=None, op0=ALU.mult, op1=ALU.bypass), inc="vd")
            free_bank(bm, "vd", vv)
            wait("V", "vd", vv, selfdep=True)
            vv = op("V", lambda E, mr=mr, tr=tr: E.tensor_tensor(
                out=tr[0:1, :], in0=mr[0:1, :], in1=mr[0:1, :],
                op=ALU.mult), inc="vd")
            wait("V", "tm", tv_v)
            wait("V", "vd", vv, selfdep=True)
            vv = op("V", lambda E, b=bv, tr=tr, vr=vr: E.scalar_tensor_tensor(
                out=vr[0:1, :], in0=row_slice(b, TB), scalar=1.0 / D,
                in1=tr[0:1, :], op0=ALU.mult, op1=ALU.subtract), inc="vd")
            free_bank(bv, "vd", vv)
            rows_vd = vv
            wait("S", "vd", rows_vd)
            op("S", lambda E, tr=tr, vr=vr: E.activation(
                out=tr[0:1, :], in_=vr[0:1, :], func=AF.Ln, bias=cb_eps[0:1, :]))
            drn("S")
            sv = op("S", lambda E, tr=tr, rr=rr: E.activation(
                out=rr[0:1, :], in_=tr[0:1, :], func=AF.Exp, scale=-0.5), inc="sd")
            return rows_vd, sv

        def emit_bcasts(rows, rows_vd, sv):
            mr, tr, vr, rr = rows
            bmb, tv_mb = mm_group([(ones_row[:], mr[0:1, :])],
                                  lambda b: ln_slice(b, TB),
                                  pre_waits=[("vd", rows_vd)])
            brs, tv_rs = mm_group([(ones_row[:], rr[0:1, :])],
                                  lambda b: ln_slice(b, TB),
                                  pre_waits=[("sd", sv)])
            return bmb, brs, tv_rs

        def emit_apply(src_chunks, dst_chunks, bmb, brs, tv_rs):
            wait("V", "tm", tv_rs)
            for c in range(DC):
                vv1 = op("V", lambda E, c=c, b=bmb, s=src_chunks[c], d=dst_chunks[c]:
                         E.tensor_tensor(out=d, in0=s, in1=ln_slice(b, TB),
                                         op=ALU.subtract), inc="vd")
            free_bank(bmb, "vd", vv1)
            wait("V", "vd", vv1, selfdep=True)
            for c in range(DC):
                vv2 = op("V", lambda E, c=c, b=brs, d=dst_chunks[c]:
                         E.tensor_tensor(out=d, in0=d, in1=ln_slice(b, TB),
                                         op=ALU.mult), inc="vd")
            free_bank(brs, "vd", vv2)
            return vv2

        for it in range(-1, NTB + 1):
            t3, t2, t1 = it - 1, it, it + 1
            do1 = 0 <= t1 < NTB
            do2 = 0 <= t2 < NTB
            do3 = 0 <= t3 < NTB
            if do1:
                p1 = t1 % 2
                wait("T", "qy%d" % p1, r1_load[t1])
                wait("T", "qwr", w1_rl)
                wait("T", "qw2r", w2_rl)
                if "sq1_sd" not in state[t1]:
                    emit_sq1(t1)
                src1 = [r1ld[:, p1, c, :] for c in range(DC)]
                dst1 = [x1_sb[:, p1, c, :] for c in range(DC)]
                bm1, tvm1 = mm_group([(ones_bf[:], src1[c]) for c in range(DC)],
                                     lambda b: row_slice(b, TB))
                bv1, tvv1 = mm_group([(ones_bf[:], sqb1[:, c, :]) for c in range(DC)],
                                     lambda b: row_slice(b, TB),
                                     pre_waits=[("sd", state[t1]["sq1_sd"])])
                r1ld_free_T[t1] = max(tvm1, tvv1)
                rows1_vd, sv1 = rows_and_s(bm1, bv1, tvm1, tvv1, rows1)
            if do3:
                p3 = t3 % 2
                st3 = state[t3]
                r2_vds = []
                sq2_sds = []
                for e in range(DC):
                    mlist = [(w2q[:, k, e * 128:(e + 1) * 128], hq_sb[:, k, :])
                             for k in range(HC)]
                    b, tv = mm_group(mlist, lambda bb: ln_slice(bb, TB),
                                     pre_waits=[("vd", st3["hq_vd"])])
                    wait("V", "tm", tv)
                    r2_vd = op("V", lambda E, b=b, e=e, p=p3: E.scalar_tensor_tensor(
                        out=r2_sb[:, e, :], in0=ln_slice(b, TB),
                        scalar=b2q[:, e:e + 1], in1=x1_sb[:, p, e, :],
                        op0=ALU.add, op1=ALU.add), inc="vd")
                    free_bank(b, "vd", r2_vd)
                    r2_vds.append(r2_vd)
                    wait("S", "vd", r2_vd)
                    sq2_sds.append(op("S", lambda E, e=e: E.activation(
                        out=sqb2[:, e, :], in_=r2_sb[:, e, :], func=AF.Square),
                        inc="sd"))
                    if e == 3 and do1:
                        bmb1, brs1, tvrs1 = emit_bcasts(rows1, rows1_vd, sv1)
                bm2 = take_bank()
                for e in range(DC):
                    wait("T", "vd", r2_vds[e])
                    kw = dict(start=(e == 0), stop=(e == DC - 1))
                    if e == DC - 1:
                        tvm2 = op("T", lambda E, e=e, b=bm2, kw=kw: E.matmul(
                            row_slice(b, TB), ones_bf[:], r2_sb[:, e, :], **kw),
                            inc="tm")
                    else:
                        op("T", lambda E, e=e, b=bm2, kw=kw: E.matmul(
                            row_slice(b, TB), ones_bf[:], r2_sb[:, e, :], **kw))
                bv2 = take_bank()
                for e in range(DC):
                    wait("T", "sd", sq2_sds[e])
                    kw = dict(start=(e == 0), stop=(e == DC - 1))
                    if e == DC - 1:
                        tvv2 = op("T", lambda E, e=e, b=bv2, kw=kw: E.matmul(
                            row_slice(b, TB), ones_bf[:], sqb2[:, e, :], **kw),
                            inc="tm")
                    else:
                        op("T", lambda E, e=e, b=bv2, kw=kw: E.matmul(
                            row_slice(b, TB), ones_bf[:], sqb2[:, e, :], **kw))
            elif do1:
                bmb1, brs1, tvrs1 = emit_bcasts(rows1, rows1_vd, sv1)
            if do1:
                if do3:
                    wait("V", "vd", r2_vd, selfdep=True)
                apply1_vd = emit_apply(src1, dst1, bmb1, brs1, tvrs1)
                r1ld_free_V[t1] = apply1_vd
                if debug and t1 == 0:
                    dma("Y", "qd", dbg["x10"].rearrange("(c p) t -> p c t", p=128),
                        x1_sb[:, 0, :, :], pre=[("vd", apply1_vd)])
                if debug and t1 == 1:
                    wait("V", "qd", cnt["qd"])
                wait("V", "vd", apply1_vd, selfdep=True)
                for half in range(2):
                    wait("V", "vd", prev_htmp_v[0], selfdep=True)
                    wait("V", "sd", prev_htmp_s[0])
                    for j in range(4):
                        c = half * 4 + j
                        v = op("V", lambda E, c=c, j=j, p=p1: E.tensor_scalar(
                            out=htmp8b[:, j, :], in0=x1_sb[:, p, c, :], scalar1=16.0,
                            scalar2=MAGIC, op0=ALU.mult, op1=ALU.add), inc="vd")
                    wait("V", "vd", v, selfdep=True)
                    for j in range(4):
                        v = op("V", lambda E, j=j: E.tensor_scalar(
                            out=htmp8b[:, j, :], in0=htmp8b[:, j, :], scalar1=MAGIC,
                            scalar2=127.0, op0=ALU.subtract, op1=ALU.min), inc="vd")
                    wait("V", "vd", v, selfdep=True)
                    for j in range(4):
                        c = half * 4 + j
                        xq_vd = op("V", lambda E, c=c, j=j, p=p1: E.tensor_scalar(
                            out=xq_sb[:, p, c, :], in0=htmp8b[:, j, :], scalar1=-128.0,
                            scalar2=None, op0=ALU.max, op1=ALU.bypass), inc="vd")
                    prev_htmp_v[0] = xq_vd
                state[t1]["xq_vd"] = xq_vd
                if t1 + 2 < NTB:
                    load_r1(t1 + 2)
            ln2_rows = [None]
            if do3:
                ln2_rows[0] = rows_and_s(bm2, bv2, tvm2, tvv2, rows2)

            def emit_ln2_rows():
                rows2_vd, sv2 = ln2_rows[0]
                return emit_bcasts(rows2, rows2_vd, sv2)

            def emit_ln2_apply(bcasts):
                bmb2, brs2, tvrs2 = bcasts
                if t3 >= 1 and out_store[t3 - 1] is not None:
                    wait("V", "qo", out_store[t3 - 1])
                src2 = [r2_sb[:, c, :] for c in range(DC)]
                dst2 = [ob[:, c, :] for c in range(DC)]
                apply2_vd = emit_apply(src2, dst2, bmb2, brs2, tvrs2)
                if debug and t3 == 0:
                    dma("Y", "qd", dbg["hq0"].rearrange("(c p) t -> p c t", p=128),
                        hq_sb[:, :, :], pre=[("vd", st3["hq_vd"])])
                colp = slice(t3 * TB, (t3 + 1) * TB)
                dstap = outT.rearrange("(c p) t -> p c t", p=128)[:, :, colp]
                out_store[t3] = dma("Y", "qo", dstap, ob[:, :, :],
                                    pre=[("vd", apply2_vd)])

            if do3 and not do2:
                emit_ln2_apply(emit_ln2_rows())
            if do2:
                p2 = t2 % 2
                hq_vd = None
                for gsub in range(HC // 4):
                    groups = []
                    for j in range(4):
                        hc = gsub * 4 + j
                        mlist = [(w1q[:, k, hc * 128:(hc + 1) * 128],
                                  xq_sb[:, p2, k, :]) for k in range(DC)]
                        groups.append(mm_group(mlist, lambda bb: ln_slice(bb, TB),
                                               pre_waits=[("vd", state[t2]["xq_vd"])]))
                    wait("V", "vd", prev_htmp_v[0], selfdep=True)
                    wait("V", "sd", prev_htmp_s[0])
                    for j in range(4):
                        b, tv = groups[j]
                        hc = gsub * 4 + j
                        wait("V", "tm", tv)
                        v = op("V", lambda E, b=b, hc=hc, j=j: E.tensor_scalar(
                            out=htmp8b[:, j, :], in0=ln_slice(b, TB),
                            scalar1=b1q16[:, hc:hc + 1], scalar2=16.0,
                            op0=ALU.add, op1=ALU.mult), inc="vd")
                        free_bank(b, "vd", v)
                    wait("V", "vd", v, selfdep=True)
                    for j in range(4):
                        v = op("V", lambda E, j=j: E.tensor_scalar(
                            out=htmp8b[:, j, :], in0=htmp8b[:, j, :],
                            scalar1=MAGIC, scalar2=MAGIC,
                            op0=ALU.add, op1=ALU.subtract), inc="vd")
                    prev_htmp_v[0] = v
                    wait("V", "vd", v, selfdep=True)
                    for j in range(4):
                        hc = gsub * 4 + j
                        v = op("V", lambda E, j=j, hc=hc: E.tensor_scalar(
                            out=hq_sb[:, hc, :], in0=htmp8b[:, j, :],
                            scalar1=127.0, scalar2=0.0,
                            op0=ALU.min, op1=ALU.max), inc="vd")
                    hq_sd = None
                    prev_htmp_v[0] = v
                    if gsub == 6 and do3:
                        ln2_bcasts = emit_ln2_rows()
                state[t2]["hq_vd"] = v
                if do3:
                    emit_ln2_apply(ln2_bcasts)
            if 0 <= it + 2 < NTB and "sq1_sd" not in state[it + 2]:
                emit_sq1(it + 2)

        wait("S", "qo", cnt["qo"], force=True)
        wait("S", "qa0", cnt["qa0"], force=True)
        wait("S", "qa1", cnt["qa1"], force=True)
        for q in ("qy0", "qy1", "qt0", "qt1", "qd"):
            if cnt[q]:
                wait("Y", q, cnt[q], force=True)
        for q in ("qw", "qx0", "qx1", "qx2"):
            wait("G", q, cnt[q], force=True)

        block = en(nc.Block())

        @block.tensor
        def _(E):
            for f in prog["T"]:
                f(E)

        @block.vector
        def _(E):
            for f in prog["V"]:
                f(E)

        @block.scalar
        def _(E):
            for f in prog["S"]:
                f(E)

        @block.gpsimd
        def _(E):
            for f in prog["G"]:
                f(E)

        @block.sync
        def _(E):
            for f in prog["Y"]:
                f(E)

    return nc


_NC_CACHE = {}


def _get_nc():
    if "nc" not in _NC_CACHE:
        _NC_CACHE["nc"] = build_nc()
    return _NC_CACHE["nc"]


def kernel(L0_attn_x, L0_attn_W_in, L0_attn_W_state, L0_attn_W_dt, L0_attn_b_dt,
           L0_attn_A_log, L0_attn_W_C, L0_attn_D, L0_attn_W_out,
           L0_ffn_w1, L0_ffn_b1, L0_ffn_w2, L0_ffn_b2):
    x = np.asarray(L0_attn_x, np.float32)
    B, Lq, Dm = x.shape
    assert (B, Lq, Dm) == (16, L, D)

    winT = np.ascontiguousarray(np.asarray(L0_attn_W_in, np.float32).T)
    wstT = np.ascontiguousarray(np.asarray(L0_attn_W_state, np.float32).T)
    wdtT = np.ascontiguousarray(np.asarray(L0_attn_W_dt, np.float32).T)
    wcT = np.ascontiguousarray(np.asarray(L0_attn_W_C, np.float32).T)
    woutT = np.ascontiguousarray(np.asarray(L0_attn_W_out, np.float32).T)
    w1T = np.ascontiguousarray(np.asarray(L0_ffn_w1, np.float32).T)
    w2T = np.ascontiguousarray(np.asarray(L0_ffn_w2, np.float32).T)

    vecs = np.zeros((128, 64), np.float32)
    vecs[:, 0] = np.asarray(L0_attn_b_dt, np.float32)
    vecs[:, 1] = np.asarray(L0_attn_A_log, np.float32)
    vecs[:, 2:10] = np.asarray(L0_attn_D, np.float32).reshape(8, 128).T
    vecs[:, 10:42] = np.asarray(L0_ffn_b1, np.float32).reshape(32, 128).T
    vecs[:, 42:50] = np.asarray(L0_ffn_b2, np.float32).reshape(8, 128).T

    nc = _get_nc()
    in_maps = []
    for c in range(8):
        xc = np.ascontiguousarray(
            x[2 * c:2 * c + 2].reshape(NTOK, D).T)
        in_maps.append({
            "xT": xc, "winT": winT, "wstT": wstT, "wdtT": wdtT, "wcT": wcT,
            "woutT": woutT, "w1T": w1T, "w2T": w2T, "vecs": vecs,
        })
    res = run_bass_kernel_spmd(nc, in_maps, core_ids=list(range(8)))
    out = np.empty((16, L, D), np.float32)
    for c in range(8):
        o = np.asarray(res.results[c]["outT"], np.float32)
        out[2 * c:2 * c + 2] = o.T.reshape(BLOC, L, D)
    return out


# version 15
# speedup vs baseline: 1.1291x; 1.1291x over previous
import contextlib
import numpy as np
import ml_dtypes
import concourse.bass as bass
import concourse.mybir as mybir
from concourse.bass_utils import run_bass_kernel_spmd

F32 = mybir.dt.float32
BF16 = mybir.dt.bfloat16
AF = mybir.ActivationFunctionType
ALU = mybir.AluOpType
BF = ml_dtypes.bfloat16

D = 1024
DC = 8
E2 = 2048
NST = 128
HID = 4096
HC = 32
BLOC = 2
L = 4096
NTOK = BLOC * L
TA = 512
NTA = NTOK // TA
TB = 256
NTB = NTOK // TB
MAGIC = 12582912.0
EPS = 1e-5
DT_MAX = 10.0
RATE_EPS = 1e-4

ENGS = ("T", "V", "S", "G", "Y")


def build_nc(debug=False):
    nc = bass.Bass("TRN2", target_bir_lowering=False)

    xT = nc.dram_tensor("xT", [D, NTOK], F32, kind="ExternalInput").ap()
    winT = nc.dram_tensor("winT", [D, E2], F32, kind="ExternalInput").ap()
    wstT = nc.dram_tensor("wstT", [D, NST], F32, kind="ExternalInput").ap()
    wdtT = nc.dram_tensor("wdtT", [D, NST], F32, kind="ExternalInput").ap()
    wcT = nc.dram_tensor("wcT", [NST, D], F32, kind="ExternalInput").ap()
    woutT = nc.dram_tensor("woutT", [D, D], F32, kind="ExternalInput").ap()
    w1T = nc.dram_tensor("w1T", [D, HID], F32, kind="ExternalInput").ap()
    w2T = nc.dram_tensor("w2T", [HID, D], F32, kind="ExternalInput").ap()
    vecs = nc.dram_tensor("vecs", [128, 64], F32, kind="ExternalInput").ap()
    outT = nc.dram_tensor("outT", [D, NTOK], F32, kind="ExternalOutput").ap()
    r1s = nc.dram_tensor("r1s", [D, NTOK], BF16, kind="Internal").ap()

    dbg = {}
    if debug:
        dbg["u0"] = nc.dram_tensor("dbg_u0", [D, TA], BF16, kind="ExternalOutput").ap()
        dbg["dec0"] = nc.dram_tensor("dbg_dec0", [NST, TA], F32, kind="ExternalOutput").ap()
        dbg["h0"] = nc.dram_tensor("dbg_h0", [NST, TA], F32, kind="ExternalOutput").ap()
        dbg["z0"] = nc.dram_tensor("dbg_z0", [D, TA], BF16, kind="ExternalOutput").ap()
        dbg["x10"] = nc.dram_tensor("dbg_x10", [D, TB], F32, kind="ExternalOutput").ap()
        dbg["hq0"] = nc.dram_tensor("dbg_hq0", [HID, TB], BF16, kind="ExternalOutput").ap()
        dbg["xb0"] = nc.dram_tensor("dbg_xb0", [D, TA], BF16, kind="ExternalOutput").ap()
        dbg["u0e"] = nc.dram_tensor("dbg_u0e", [D, TA], BF16, kind="ExternalOutput").ap()
        dbg["w0"] = nc.dram_tensor("dbg_w0", [128, 2048], BF16, kind="ExternalOutput").ap()

    SB_TOP = 229344
    P_BASE = 16512
    ptr = {"P": P_BASE}
    handles = {}

    def nbytes_pp(shape, dtype):
        n = 1
        for s in shape[1:]:
            n *= s
        return n * (2 if dtype == BF16 else 4)

    def sb(name, shape, dtype, region, at=None):
        if at is None:
            off = (ptr[region] + 31) // 32 * 32
            ptr[region] = off + nbytes_pp(shape, dtype)
        else:
            off = at
        assert off + nbytes_pp(shape, dtype) <= SB_TOP, (name, off)
        h = nc.alloc_sbuf_tensor_at(name, shape, dtype, offset=off)
        handles[name] = (h, off)
        return h

    vecs_sb = sb("vecs_sb", [128, 64], F32, "P")
    negrate = sb("negrate", [128, 1], F32, "P")
    tmpc1 = sb("tmpc1", [128, 1], F32, "P")
    tmpc2 = sb("tmpc2", [128, 1], F32, "P")
    b1q16 = sb("b1q16", [128, 32], F32, "P")
    b2q = sb("b2q", [128, 8], F32, "P")
    ones_bf = sb("ones_bf", [128, 1], BF16, "P")
    cb_one = sb("cb_one", [128, 1], F32, "P")
    cb_eps = sb("cb_eps", [128, 1], F32, "P")
    ones_row = sb("ones_row", [1, 128], F32, "P")
    rows1 = tuple(sb("row1_%d" % i, [1, 256], F32, "P") for i in range(4))
    rows2 = rows1

    overlay = (ptr["P"] + 31) // 32 * 32
    ptr["A"] = overlay
    ptr["B"] = overlay

    winT_sb = sb("winT_sb", [128, DC, E2], BF16, "A")
    wstT_sb = sb("wstT_sb", [128, DC, NST], BF16, "A")
    wdtT_sb = sb("wdtT_sb", [128, DC, NST], BF16, "A")
    wcT_sb = sb("wcT_sb", [128, D], BF16, "A")
    woutT_sb = sb("woutT_sb", [128, DC, D], BF16, "A")
    xb = sb("xb", [128, 3, DC, TA], BF16, "A")
    u_bf = sb("u_bf", [128, 2, DC, TA], BF16, "A")
    sg = sb("sg", [128, 2, DC, TA], BF16, "A")
    z_sb = sb("z_sb", [128, DC, TA], BF16, "A")
    ztmp8 = sb("ztmp8", [128, DC, TA], F32, "A")
    r1_sb = sb("r1_sb", [128, 2, DC, TA], BF16, "A")
    sp_t = sb("sp_t", [128, TA], F32, "A")
    e1_t = sb("e1_t", [128, TA], F32, "A")
    ld_t = sb("ld_t", [128, TA], F32, "A")
    dec_t = sb("dec_t", [128, 2, TA], F32, "A")
    omd_t = sb("omd_t", [128, TA], F32, "A")
    c_t = sb("c_t", [128, TA], F32, "A")
    h_t = sb("h_t", [128, 2, TA], F32, "A")
    hb_t = sb("hb_t", [128, 2, TA], BF16, "A")

    w1q = sb("w1q", [128, DC, HID], BF16, "B")
    w2q = sb("w2q", [128, HC, D], BF16, "B")
    r1ld = sb("r1ld", [128, 2, DC, TB], BF16, "B")
    x1_sb = sb("x1_sb", [128, 2, DC, TB], F32, "B")
    xq_sb = sb("xq_sb", [128, 2, DC, TB], BF16, "B")
    hq_sb = sb("hq_sb", [128, HC, TB], BF16, "B")
    r2_sb = sb("r2_sb", [128, DC, TB], BF16, "B")
    ob = sb("ob", [128, DC, TB], F32, "B")
    htmp8b = sb("htmp8b", [128, 4, TB], F32, "B")
    sqb1 = sb("sqb1", [128, DC, TB], BF16, "B")
    sqb2 = sb("sqb2", [128, DC, TB], BF16, "B")

    _wsbase = handles["sp_t"][1]
    wstage = [
        sb("wstage_a", [128, HID], F32, "A", at=_wsbase),
        sb("wstage_b", [128, HID], F32, "A", at=_wsbase + HID * 4),
    ]
    wstage4 = [
        sb("wstage4_a", [128, 4, D], F32, "A", at=_wsbase),
        sb("wstage4_b", [128, 4, D], F32, "A", at=_wsbase + HID * 4),
    ]
    assert _wsbase + 2 * HID * 4 <= SB_TOP
    assert _wsbase >= handles["w2q"][1] + HC * D * 2, "staging overlaps w2q"
    assert ptr["A"] <= SB_TOP and ptr["B"] <= SB_TOP

    with contextlib.ExitStack() as st:
        en = st.enter_context
        pb = [en(nc.psum_tensor(f"pb{i}", [128, 512], F32)) for i in range(8)]
        sems = {k: en(nc.semaphore(k)) for k in (
            "qw", "qx0", "qx1", "qx2", "qa0", "qa1", "qy0", "qy1", "qo",
            "qt0", "qt1", "qd", "tm", "sd", "vd")}

        prog = {e: [] for e in ENGS}
        cnt = {k: 0 for k in sems}
        waited = {e: {} for e in ENGS}
        own_sem = {"T": "tm", "V": "vd", "S": "sd"}

        def emit(eng, fn):
            prog[eng].append(fn)

        def wait(eng, sem, val, force=False, selfdep=False):
            if val is None or val <= 0:
                return
            if not force and not selfdep and own_sem.get(eng) == sem:
                return
            if not force and waited[eng].get(sem, 0) >= val:
                return
            waited[eng][sem] = max(waited[eng].get(sem, 0), val)
            emit(eng, lambda E, s=sems[sem], v=val: E.wait_ge(s, v))

        def drn(eng):
            emit(eng, lambda E: E.drain())

        def op(eng, fn, inc=None, inck=1):
            if inc is None:
                emit(eng, fn)
                return None
            cnt[inc] += inck
            emit(eng, lambda E, f=fn, s=sems[inc], k=inck: f(E).then_inc(s, k))
            return cnt[inc]

        bank_owner = [None] * 8
        bank_rr = [0]
        rbank_rr = [0]

        def take_rbank():
            b = 6 + rbank_rr[0] % 2
            rbank_rr[0] += 1
            if bank_owner[b] is not None:
                assert bank_owner[b][0] != "__taken__"
                wait("T", *bank_owner[b])
            bank_owner[b] = ("__taken__", 0)
            return b

        def take_bank():
            b = bank_rr[0] % 6
            bank_rr[0] += 1
            if bank_owner[b] is not None:
                assert bank_owner[b][0] != "__taken__", f"bank {b} re-taken before consumer recorded"
                wait("T", *bank_owner[b])
            bank_owner[b] = ("__taken__", 0)
            return b

        def free_bank(b, sem, val):
            bank_owner[b] = (sem, val)

        def mm_group(lhs_rhs, out_ap, pre_waits=()):
            b = take_bank()
            for s_name, v in pre_waits:
                wait("T", s_name, v)
            n = len(lhs_rhs)
            for i, (lhs, rhs) in enumerate(lhs_rhs):
                kw = dict(start=(i == 0), stop=(i == n - 1))
                if i == n - 1:
                    tv = op("T", lambda E, l=lhs, r=rhs, o=out_ap(b), kw=kw:
                            E.matmul(o, l, r, **kw), inc="tm")
                else:
                    op("T", lambda E, l=lhs, r=rhs, o=out_ap(b), kw=kw:
                       E.matmul(o, l, r, **kw))
            return b, tv

        def full(b):
            return pb[b][:, :]

        def ln_slice(b, T):
            return pb[b][:, 0:T]

        def row_slice(b, T):
            return pb[b][0:1, 0:T]

        def dma(eng, q, out_ap, in_ap, pre=()):
            for s_name, v in pre:
                wait(eng, s_name, v)
            return op(eng, lambda E, o=out_ap, i=in_ap: E.dma_start(out=o, in_=i),
                      inc=q, inck=16)

        dma("G", "qw", winT_sb[:, :, :], winT.rearrange("(k p) e -> p k e", p=128))
        dma("G", "qw", wstT_sb[:, :, :], wstT.rearrange("(k p) n -> p k n", p=128))
        dma("G", "qw", wdtT_sb[:, :, :], wdtT.rearrange("(k p) n -> p k n", p=128))
        dma("G", "qw", wcT_sb[:, :], wcT[:, :])
        dma("G", "qw", woutT_sb[:, :, :], woutT.rearrange("(k p) e -> p k e", p=128))
        qw_all = dma("G", "qw", vecs_sb[:, :], vecs[:, :])
        qs_vecs = None

        op("V", lambda E: E.memset(ones_bf[:], 1.0))
        op("V", lambda E: E.memset(ones_row[:], 1.0))
        op("V", lambda E: E.memset(cb_one[:], 1.0))
        cv = op("V", lambda E: E.memset(cb_eps[:], EPS), inc="vd")
        wait("S", "qw", qw_all)
        wait("S", "vd", cv)
        op("S", lambda E: E.activation(out=tmpc1[:], in_=vecs_sb[:, 1:2], func=AF.Exp))
        drn("S")
        sv = op("S", lambda E: E.activation(out=tmpc2[:], in_=tmpc1[:], func=AF.Ln,
                                            bias=cb_one[:]), inc="sd")
        wait("V", "sd", sv)
        negrate_vd = op("V", lambda E: E.tensor_scalar(out=negrate[:], in0=tmpc2[:],
                                          scalar1=RATE_EPS, scalar2=-1.0,
                                          op0=ALU.add, op1=ALU.mult), inc="vd")
        wait("V", "qw", qw_all)

        def quant_cols(dst, src_ap, postmul):
            v = op("V", lambda E, s=src_ap: E.tensor_scalar(
                out=dst[:], in0=s, scalar1=64.0, scalar2=MAGIC,
                op0=ALU.mult, op1=ALU.add), inc="vd")
            wait("V", "vd", v, selfdep=True)
            v = op("V", lambda E: E.tensor_scalar(
                out=dst[:], in0=dst[:], scalar1=MAGIC, scalar2=127.0,
                op0=ALU.subtract, op1=ALU.min), inc="vd")
            wait("V", "vd", v, selfdep=True)
            v = op("V", lambda E, pm=postmul: E.tensor_scalar(
                out=dst[:], in0=dst[:], scalar1=-128.0, scalar2=pm,
                op0=ALU.max, op1=ALU.mult), inc="vd")
            return v

        quant_cols(b1q16, vecs_sb[:, 10:42], 1.0 / 64.0)
        quant_cols(b2q, vecs_sb[:, 42:50], 1.0 / 64.0)
        op("V", lambda E: E.memset(ztmp8[:, 0, 0:1], 0.0), inc="vd")

        dbgA_qs = [0]
        xb_load = [None] * NTA
        xb_free_T = [None] * NTA
        xb_free_V = [None] * NTA
        r1_done = [None] * NTA
        r1_store = [None] * NTA
        scan_vd = [None] * NTA
        silu_done = [None] * NTA
        z_done = [None] * NTA

        def load_xb(t):
            pre = []
            if t >= 3:
                pre = [("tm", xb_free_T[t - 3]), ("vd", xb_free_V[t - 3])]
            src = xT.rearrange("(c p) t -> p c t", p=128)[:, :, t * TA:(t + 1) * TA]
            xb_load[t] = dma("G", "qx%d" % (t % 3), xb[:, t % 3, :, :], src, pre=pre)

        load_xb(0)
        load_xb(1)
        load_xb(2)

        for it in range(NTA + 1):
            t = it
            tp = it - 1
            if t < NTA:
                col = slice(t * TA, (t + 1) * TA)
                par = t % 2
                xpar = t % 3
                if debug and t == 1:
                    for _e in ("T", "S", "V", "G"):
                        wait(_e, "qd", cnt["qd"], force=True)
                if debug and t == 2:
                    for _e in ("T", "S", "V", "G"):
                        wait(_e, "qd", dbgA_qs[0])
                wait("T", "qw", qw_all)
                wait("T", "qx%d" % (t % 3), xb_load[t])
                u_copy_done = None
                for e in range(8):
                    mlist = [(winT_sb[:, k, e * 128:(e + 1) * 128], xb[:, xpar, k, :])
                             for k in range(DC)]
                    b, tv = mm_group(mlist, full)
                    wait("S", "tm", tv)
                    sv = op("S", lambda E, b=b, e=e, p=par: E.copy(
                        out=u_bf[:, p, e, :], in_=full(b)), inc="sd")
                    free_bank(b, "sd", sv)
                    u_copy_done = sv
                if debug and t == 0:
                    dma("Y", "qd", dbg["u0e"].rearrange("(c p) t -> p c t", p=128),
                        u_bf[:, 0, :, :], pre=[("sd", u_copy_done)])
                mlist = [(wstT_sb[:, k, :], u_bf[:, par, k, :]) for k in range(DC)]
                bt_b, bt_tv = mm_group(mlist, full, pre_waits=[("sd", u_copy_done)])
                mlist = [(wdtT_sb[:, k, :], u_bf[:, par, k, :]) for k in range(DC)]
                dt_b, dt_tv = mm_group(mlist, full)
                wait("S", "tm", dt_tv)
                sv = op("S", lambda E, b=dt_b: E.activation(
                    out=e1_t[:], in_=full(b), func=AF.Exp,
                    bias=vecs_sb[:, 0:1]), inc="sd")
                free_bank(dt_b, "sd", sv)
                drn("S")
                sv = op("S", lambda E: E.activation(
                    out=sp_t[:], in_=e1_t[:], func=AF.Ln, bias=cb_one[:]), inc="sd")
                wait("V", "sd", sv)
                wait("V", "vd", negrate_vd, selfdep=True)
                vv = op("V", lambda E: E.tensor_scalar(
                    out=ld_t[:], in0=sp_t[:], scalar1=DT_MAX, scalar2=negrate[:],
                    op0=ALU.min, op1=ALU.mult), inc="vd")
                wait("S", "vd", vv)
                sv = op("S", lambda E, p=par: E.activation(
                    out=dec_t[:, p, :], in_=ld_t[:], func=AF.Exp), inc="sd")
                wait("V", "sd", sv)
                omd_vv = op("V", lambda E, p=par: E.tensor_scalar(
                    out=omd_t[:], in0=dec_t[:, p, :], scalar1=1.0, scalar2=-1.0,
                    op0=ALU.subtract, op1=ALU.mult), inc="vd")
                wait("V", "tm", bt_tv)
                wait("V", "vd", omd_vv, selfdep=True)
                c_vv = op("V", lambda E, b=bt_b: E.tensor_tensor(
                    out=c_t[:], in0=omd_t[:], in1=full(b), op=ALU.mult), inc="vd")
                free_bank(bt_b, "vd", c_vv)
                wait("V", "vd", c_vv, selfdep=True)
                if t % (L // TA) == 0:
                    init = 0.0
                else:
                    init = h_t[:, (t - 1) % 2, TA - 1:TA]
                scan_vd[t] = op("V", lambda E, p=par, i=init: E.tensor_tensor_scan(
                    out=h_t[:, p, :], data0=dec_t[:, p, :], data1=c_t[:],
                    initial=i, op0=ALU.mult, op1=ALU.add), inc="vd")
                wait("V", "vd", scan_vd[t], selfdep=True)
                hb_vv = op("V", lambda E, p=par: E.tensor_copy(
                    out=hb_t[:, p, :], in_=h_t[:, p, :]), inc="vd")
                for e in range(8):
                    mlist = [(winT_sb[:, k, (8 + e) * 128:(9 + e) * 128],
                              xb[:, xpar, k, :]) for k in range(DC)]
                    gb, gtv = mm_group(mlist, full)
                    wait("S", "tm", gtv)
                    ssv = op("S", lambda E, b=gb, e=e, p=par: E.activation(
                        out=sg[:, p, e, :], in_=full(b), func=AF.Silu), inc="sd")
                    free_bank(gb, "sd", ssv)
                xb_free_T[t] = gtv
                silu_done[t] = ssv

            if tp >= 0:
                colp = slice(tp * TA, (tp + 1) * TA)
                parp = tp % 2
                for e in range(DC):
                    mlist = [(woutT_sb[:, k, e * 128:(e + 1) * 128], z_sb[:, k, :])
                             for k in range(DC)]
                    b, tv = mm_group(mlist, full,
                                     pre_waits=[("vd", z_done[tp])])
                    wait("V", "tm", tv)
                    if tp >= 2 and r1_store[tp - 2] is not None:
                        wait("V", "qa%d" % (tp % 2), r1_store[tp - 2])
                    vv = op("V", lambda E, b=b, e=e, p=parp, xp=tp % 3: E.tensor_tensor(
                        out=r1_sb[:, p, e, :], in0=full(b), in1=xb[:, xp, e, :],
                        op=ALU.add), inc="vd")
                    free_bank(b, "vd", vv)
                r1_done[tp] = vv
                xb_free_V[tp] = vv
                dst = r1s.rearrange("(c p) t -> p c t", p=128)[:, :, colp]
                r1_store[tp] = dma("S", "qa%d" % (tp % 2), dst,
                                   r1_sb[:, parp, :, :],
                                   pre=[("vd", r1_done[tp])])
                if tp + 3 < NTA:
                    load_xb(tp + 3)

            if t < NTA:
                if debug and t == 1:
                    wait("V", "qd", dbgA_qs[0])
                wait("V", "sd", silu_done[t])
                for e in range(DC):
                    b, tv = mm_group(
                        [(wcT_sb[:, e * 128:(e + 1) * 128], hb_t[:, t % 2, :])],
                        full, pre_waits=[("vd", hb_vv)])
                    wait("V", "tm", tv)
                    vv = op("V", lambda E, b=b, e=e, p=t % 2: E.tensor_tensor(
                        out=ztmp8[:, e, :], in0=full(b), in1=sg[:, p, e, :],
                        op=ALU.mult), inc="vd")
                    free_bank(b, "vd", vv)
                wait("V", "vd", vv, selfdep=True)
                for e in range(DC):
                    vv = op("V", lambda E, e=e, p=t % 3: E.scalar_tensor_tensor(
                        out=z_sb[:, e, :], in0=xb[:, p, e, :],
                        scalar=vecs_sb[:, 2 + e:3 + e], in1=ztmp8[:, e, :],
                        op0=ALU.mult, op1=ALU.add), inc="vd")
                z_done[t] = vv
                if debug and t == 0:
                    dma("Y", "qd", dbg["xb0"].rearrange("(c p) t -> p c t", p=128),
                        xb[:, 0, :, :], pre=[("vd", z_done[0])])
                    dma("Y", "qd", dbg["w0"][:, :], winT_sb[:, 0, :],
                        pre=[("vd", z_done[0])])
                    dma("Y", "qd", dbg["u0"].rearrange("(c p) t -> p c t", p=128),
                        u_bf[:, 0, :, :], pre=[("sd", u_copy_done)])
                    dma("Y", "qd", dbg["dec0"][:, :], dec_t[:, 0, :],
                        pre=[("vd", scan_vd[0])])
                    dma("Y", "qd", dbg["h0"][:, :], h_t[:, 0, :],
                        pre=[("vd", scan_vd[0])])
                    dma("Y", "qd", dbg["z0"].rearrange("(c p) t -> p c t", p=128),
                        z_sb[:, :, :], pre=[("vd", z_done[0])])
                    dbgA_qs[0] = cnt["qd"]


        endA = [("tm", cnt["tm"]), ("vd", cnt["vd"]), ("sd", cnt["sd"]),
                ("qa0", cnt["qa0"]), ("qa1", cnt["qa1"])]

        wq_loads = []
        first_stage = True

        def stage_quant(dst_ap, src_ap, ncols, si, shape4=False):
            nonlocal first_stage
            pre = list(endA) if first_stage else []
            if len(wq_loads) >= 2:
                pre.append(("vd", wq_loads[-2][1]))
            first_stage = False
            stg = wstage4[si] if shape4 else wstage[si]
            qv = dma("Y", "qt%d" % si, stg[:, 0:ncols] if not shape4 else stg[:, :, :],
                     src_ap, pre=pre)
            wait("V", "qt%d" % si, qv)
            sap = stg[:, :, :] if shape4 else stg[:, 0:ncols]
            v = op("V", lambda E, s=sap: E.tensor_scalar(
                out=s, in0=s, scalar1=64.0, scalar2=MAGIC,
                op0=ALU.mult, op1=ALU.add), inc="vd")
            wait("V", "vd", v, selfdep=True)
            v = op("V", lambda E, s=sap: E.tensor_scalar(
                out=s, in0=s, scalar1=MAGIC, scalar2=127.0,
                op0=ALU.subtract, op1=ALU.min), inc="vd")
            wait("V", "vd", v, selfdep=True)
            vv = op("V", lambda E, s=sap, d=dst_ap: E.tensor_scalar(
                out=d, in0=s, scalar1=-128.0, scalar2=1.0 / 1024.0,
                op0=ALU.max, op1=ALU.mult), inc="vd")
            wq_loads.append((qv, vv))

        for k in range(DC):
            stage_quant(w1q[:, k, :],
                        w1T.rearrange("(k p) h -> p k h", p=128)[:, k, :],
                        HID, k % 2)
        for k4 in range(HC // 4):
            stage_quant(w2q[:, 4 * k4:4 * k4 + 4, :],
                        w2T.rearrange("(k p) e -> p k e", p=128)[:, 4 * k4:4 * k4 + 4, :],
                        HID, k4 % 2, shape4=True)
        wq_done = wq_loads[-1][1]

        r1_load = [None] * NTB
        r1ld_free_T = [None] * NTB
        r1ld_free_V = [None] * NTB
        out_store = [None] * NTB
        state = [dict() for _ in range(NTB)]

        def load_r1(tb):
            assert r1_load[tb] is None, tb
            pre = []
            if tb < 3:
                pre += endA + [("vd", wq_loads[-1][1])]
            if tb >= 2:
                pre += [("tm", r1ld_free_T[tb - 2]), ("vd", r1ld_free_V[tb - 2])]
            src = r1s.rearrange("(c p) t -> p c t", p=128)[:, :, tb * TB:(tb + 1) * TB]
            r1_load[tb] = dma("Y", "qy%d" % (tb % 2), r1ld[:, tb % 2, :, :], src, pre=pre)

        load_r1(0)
        load_r1(1)
        prev_htmp_v = [wq_loads[-1][1]]
        prev_htmp_s = [0]

        def emit_sq1(n):
            p = n % 2
            wait("S", "qy%d" % p, r1_load[n])
            sqv = None
            for c in range(DC):
                sqv = op("S", lambda E, c=c, p=p: E.activation(
                    out=sqb1[:, c, :], in_=r1ld[:, p, c, :], func=AF.Square),
                    inc="sd")
            state[n]["sq1_sd"] = sqv

        def rows_and_s(bm, bv, tv_m, tv_v, rows):
            mr, tr, vr, rr = rows
            wait("V", "tm", tv_m)
            vv = op("V", lambda E, b=bm, mr=mr: E.tensor_scalar(
                out=mr[0:1, :], in0=row_slice(b, TB), scalar1=1.0 / D,
                scalar2=None, op0=ALU.mult, op1=ALU.bypass), inc="vd")
            free_bank(bm, "vd", vv)
            wait("V", "vd", vv, selfdep=True)
            vv = op("V", lambda E, mr=mr, tr=tr: E.tensor_tensor(
                out=tr[0:1, :], in0=mr[0:1, :], in1=mr[0:1, :],
                op=ALU.mult), inc="vd")
            wait("V", "tm", tv_v)
            wait("V", "vd", vv, selfdep=True)
            vv = op("V", lambda E, b=bv, tr=tr, vr=vr: E.scalar_tensor_tensor(
                out=vr[0:1, :], in0=row_slice(b, TB), scalar=1.0 / D,
                in1=tr[0:1, :], op0=ALU.mult, op1=ALU.subtract), inc="vd")
            free_bank(bv, "vd", vv)
            rows_vd = vv
            wait("S", "vd", rows_vd)
            op("S", lambda E, tr=tr, vr=vr: E.activation(
                out=tr[0:1, :], in_=vr[0:1, :], func=AF.Ln, bias=cb_eps[0:1, :]))
            drn("S")
            sv = op("S", lambda E, tr=tr, rr=rr: E.activation(
                out=rr[0:1, :], in_=tr[0:1, :], func=AF.Exp, scale=-0.5), inc="sd")
            return rows_vd, sv

        def emit_bcasts(rows, rows_vd, sv):
            mr, tr, vr, rr = rows
            bmb = take_rbank()
            wait("T", "vd", rows_vd)
            op("T", lambda E, b=bmb, mr=mr: E.matmul(
                ln_slice(b, TB), ones_row[:], mr[0:1, :], start=True, stop=True),
                inc="tm")
            brs = take_rbank()
            wait("T", "sd", sv)
            tv_rs = op("T", lambda E, b=brs, rr=rr: E.matmul(
                ln_slice(b, TB), ones_row[:], rr[0:1, :], start=True, stop=True),
                inc="tm")
            return bmb, brs, tv_rs

        def emit_apply(src_chunks, dst_chunks, bmb, brs, tv_rs):
            wait("V", "tm", tv_rs)
            for c in range(DC):
                vv1 = op("V", lambda E, c=c, b=bmb, s=src_chunks[c], d=dst_chunks[c]:
                         E.tensor_tensor(out=d, in0=s, in1=ln_slice(b, TB),
                                         op=ALU.subtract), inc="vd")
            free_bank(bmb, "vd", vv1)
            wait("V", "vd", vv1, selfdep=True)
            for c in range(DC):
                vv2 = op("V", lambda E, c=c, b=brs, d=dst_chunks[c]:
                         E.tensor_tensor(out=d, in0=d, in1=ln_slice(b, TB),
                                         op=ALU.mult), inc="vd")
            free_bank(brs, "vd", vv2)
            return vv2

        for it in range(-1, NTB + 1):
            t3, t2, t1 = it - 1, it, it + 1
            do1 = 0 <= t1 < NTB
            do2 = 0 <= t2 < NTB
            do3 = 0 <= t3 < NTB
            if do1:
                p1 = t1 % 2
                wait("T", "qy%d" % p1, r1_load[t1])
                wait("T", "vd", wq_done)
                if "sq1_sd" not in state[t1]:
                    emit_sq1(t1)
                src1 = [r1ld[:, p1, c, :] for c in range(DC)]
                dst1 = [x1_sb[:, p1, c, :] for c in range(DC)]
                bm1, tvm1 = mm_group([(ones_bf[:], src1[c]) for c in range(DC)],
                                     lambda b: row_slice(b, TB))
                bv1, tvv1 = mm_group([(ones_bf[:], sqb1[:, c, :]) for c in range(DC)],
                                     lambda b: row_slice(b, TB),
                                     pre_waits=[("sd", state[t1]["sq1_sd"])])
                r1ld_free_T[t1] = max(tvm1, tvv1)
                rows1_vd, sv1 = rows_and_s(bm1, bv1, tvm1, tvv1, rows1)
            if do3:
                p3 = t3 % 2
                st3 = state[t3]
                r2_vds = []
                sq2_sds = []
                for e in range(DC):
                    mlist = [(w2q[:, k, e * 128:(e + 1) * 128], hq_sb[:, k, :])
                             for k in range(HC)]
                    b, tv = mm_group(mlist, lambda bb: ln_slice(bb, TB),
                                     pre_waits=[("vd", st3["hq_vd"])])
                    wait("V", "tm", tv)
                    r2_vd = op("V", lambda E, b=b, e=e, p=p3: E.scalar_tensor_tensor(
                        out=r2_sb[:, e, :], in0=ln_slice(b, TB),
                        scalar=b2q[:, e:e + 1], in1=x1_sb[:, p, e, :],
                        op0=ALU.add, op1=ALU.add), inc="vd")
                    free_bank(b, "vd", r2_vd)
                    r2_vds.append(r2_vd)
                    wait("S", "vd", r2_vd)
                    sq2_sds.append(op("S", lambda E, e=e: E.activation(
                        out=sqb2[:, e, :], in_=r2_sb[:, e, :], func=AF.Square),
                        inc="sd"))
                    if e == 3 and do1:
                        bmb1, brs1, tvrs1 = emit_bcasts(rows1, rows1_vd, sv1)
                bm2 = take_bank()
                for e in range(DC):
                    wait("T", "vd", r2_vds[e])
                    kw = dict(start=(e == 0), stop=(e == DC - 1))
                    if e == DC - 1:
                        tvm2 = op("T", lambda E, e=e, b=bm2, kw=kw: E.matmul(
                            row_slice(b, TB), ones_bf[:], r2_sb[:, e, :], **kw),
                            inc="tm")
                    else:
                        op("T", lambda E, e=e, b=bm2, kw=kw: E.matmul(
                            row_slice(b, TB), ones_bf[:], r2_sb[:, e, :], **kw))
                bv2 = take_bank()
                for e in range(DC):
                    wait("T", "sd", sq2_sds[e])
                    kw = dict(start=(e == 0), stop=(e == DC - 1))
                    if e == DC - 1:
                        tvv2 = op("T", lambda E, e=e, b=bv2, kw=kw: E.matmul(
                            row_slice(b, TB), ones_bf[:], sqb2[:, e, :], **kw),
                            inc="tm")
                    else:
                        op("T", lambda E, e=e, b=bv2, kw=kw: E.matmul(
                            row_slice(b, TB), ones_bf[:], sqb2[:, e, :], **kw))
            elif do1:
                bmb1, brs1, tvrs1 = emit_bcasts(rows1, rows1_vd, sv1)
            if do1:
                if do3:
                    wait("V", "vd", r2_vd, selfdep=True)
                apply1_vd = emit_apply(src1, dst1, bmb1, brs1, tvrs1)
                r1ld_free_V[t1] = apply1_vd
                if debug and t1 == 0:
                    dma("Y", "qd", dbg["x10"].rearrange("(c p) t -> p c t", p=128),
                        x1_sb[:, 0, :, :], pre=[("vd", apply1_vd)])
                if debug and t1 == 1:
                    wait("V", "qd", cnt["qd"])
                wait("V", "vd", apply1_vd, selfdep=True)
                for half in range(2):
                    wait("V", "vd", prev_htmp_v[0], selfdep=True)
                    wait("V", "sd", prev_htmp_s[0])
                    for j in range(4):
                        c = half * 4 + j
                        v = op("V", lambda E, c=c, j=j, p=p1: E.tensor_scalar(
                            out=htmp8b[:, j, :], in0=x1_sb[:, p, c, :], scalar1=16.0,
                            scalar2=MAGIC, op0=ALU.mult, op1=ALU.add), inc="vd")
                    wait("V", "vd", v, selfdep=True)
                    for j in range(4):
                        v = op("V", lambda E, j=j: E.tensor_scalar(
                            out=htmp8b[:, j, :], in0=htmp8b[:, j, :], scalar1=MAGIC,
                            scalar2=127.0, op0=ALU.subtract, op1=ALU.min), inc="vd")
                    wait("V", "vd", v, selfdep=True)
                    for j in range(4):
                        c = half * 4 + j
                        xq_vd = op("V", lambda E, c=c, j=j, p=p1: E.tensor_scalar(
                            out=xq_sb[:, p, c, :], in0=htmp8b[:, j, :], scalar1=-128.0,
                            scalar2=None, op0=ALU.max, op1=ALU.bypass), inc="vd")
                    prev_htmp_v[0] = xq_vd
                state[t1]["xq_vd"] = xq_vd
                if t1 + 2 < NTB:
                    load_r1(t1 + 2)
            ln2_rows = [None]
            if do3:
                ln2_rows[0] = rows_and_s(bm2, bv2, tvm2, tvv2, rows2)

            def emit_ln2_rows():
                rows2_vd, sv2 = ln2_rows[0]
                return emit_bcasts(rows2, rows2_vd, sv2)

            def emit_ln2_apply(bcasts):
                bmb2, brs2, tvrs2 = bcasts
                if t3 >= 1 and out_store[t3 - 1] is not None:
                    wait("V", "qo", out_store[t3 - 1])
                src2 = [r2_sb[:, c, :] for c in range(DC)]
                dst2 = [ob[:, c, :] for c in range(DC)]
                apply2_vd = emit_apply(src2, dst2, bmb2, brs2, tvrs2)
                if debug and t3 == 0:
                    dma("Y", "qd", dbg["hq0"].rearrange("(c p) t -> p c t", p=128),
                        hq_sb[:, :, :], pre=[("vd", st3["hq_vd"])])
                colp = slice(t3 * TB, (t3 + 1) * TB)
                dstap = outT.rearrange("(c p) t -> p c t", p=128)[:, :, colp]
                out_store[t3] = dma("Y", "qo", dstap, ob[:, :, :],
                                    pre=[("vd", apply2_vd)])

            if do3 and not do2:
                emit_ln2_apply(emit_ln2_rows())
            if do2:
                p2 = t2 % 2
                hq_vd = None
                for gsub in range(HC // 4):
                    groups = []
                    for j in range(4):
                        hc = gsub * 4 + j
                        mlist = [(w1q[:, k, hc * 128:(hc + 1) * 128],
                                  xq_sb[:, p2, k, :]) for k in range(DC)]
                        groups.append(mm_group(mlist, lambda bb: ln_slice(bb, TB),
                                               pre_waits=[("vd", state[t2]["xq_vd"])]))
                    wait("V", "vd", prev_htmp_v[0], selfdep=True)
                    wait("V", "sd", prev_htmp_s[0])
                    for j in range(4):
                        b, tv = groups[j]
                        hc = gsub * 4 + j
                        wait("V", "tm", tv)
                        v = op("V", lambda E, b=b, hc=hc, j=j: E.tensor_scalar(
                            out=htmp8b[:, j, :], in0=ln_slice(b, TB),
                            scalar1=b1q16[:, hc:hc + 1], scalar2=16.0,
                            op0=ALU.add, op1=ALU.mult), inc="vd")
                        free_bank(b, "vd", v)
                    wait("V", "vd", v, selfdep=True)
                    for j in range(4):
                        v = op("V", lambda E, j=j: E.tensor_scalar(
                            out=htmp8b[:, j, :], in0=htmp8b[:, j, :],
                            scalar1=MAGIC, scalar2=MAGIC,
                            op0=ALU.add, op1=ALU.subtract), inc="vd")
                    prev_htmp_v[0] = v
                    wait("V", "vd", v, selfdep=True)
                    for j in range(4):
                        hc = gsub * 4 + j
                        v = op("V", lambda E, j=j, hc=hc: E.tensor_scalar(
                            out=hq_sb[:, hc, :], in0=htmp8b[:, j, :],
                            scalar1=127.0, scalar2=0.0,
                            op0=ALU.min, op1=ALU.max), inc="vd")
                    hq_sd = None
                    prev_htmp_v[0] = v
                    if gsub == 6 and do3:
                        ln2_bcasts = emit_ln2_rows()
                state[t2]["hq_vd"] = v
                if do3:
                    emit_ln2_apply(ln2_bcasts)
            if 0 <= it + 2 < NTB and "sq1_sd" not in state[it + 2]:
                emit_sq1(it + 2)

        wait("S", "qo", cnt["qo"], force=True)
        wait("S", "qa0", cnt["qa0"], force=True)
        wait("S", "qa1", cnt["qa1"], force=True)
        for q in ("qy0", "qy1", "qt0", "qt1", "qd"):
            if cnt[q]:
                wait("Y", q, cnt[q], force=True)
        for q in ("qw", "qx0", "qx1", "qx2"):
            wait("G", q, cnt[q], force=True)

        block = en(nc.Block())

        @block.tensor
        def _(E):
            for f in prog["T"]:
                f(E)

        @block.vector
        def _(E):
            for f in prog["V"]:
                f(E)

        @block.scalar
        def _(E):
            for f in prog["S"]:
                f(E)

        @block.gpsimd
        def _(E):
            for f in prog["G"]:
                f(E)

        @block.sync
        def _(E):
            for f in prog["Y"]:
                f(E)

    return nc


_NC_CACHE = {}


def _get_nc():
    if "nc" not in _NC_CACHE:
        _NC_CACHE["nc"] = build_nc()
    return _NC_CACHE["nc"]


def kernel(L0_attn_x, L0_attn_W_in, L0_attn_W_state, L0_attn_W_dt, L0_attn_b_dt,
           L0_attn_A_log, L0_attn_W_C, L0_attn_D, L0_attn_W_out,
           L0_ffn_w1, L0_ffn_b1, L0_ffn_w2, L0_ffn_b2):
    x = np.asarray(L0_attn_x, np.float32)
    B, Lq, Dm = x.shape
    assert (B, Lq, Dm) == (16, L, D)

    winT = np.ascontiguousarray(np.asarray(L0_attn_W_in, np.float32).T)
    wstT = np.ascontiguousarray(np.asarray(L0_attn_W_state, np.float32).T)
    wdtT = np.ascontiguousarray(np.asarray(L0_attn_W_dt, np.float32).T)
    wcT = np.ascontiguousarray(np.asarray(L0_attn_W_C, np.float32).T)
    woutT = np.ascontiguousarray(np.asarray(L0_attn_W_out, np.float32).T)
    w1T = np.ascontiguousarray(np.asarray(L0_ffn_w1, np.float32).T)
    w2T = np.ascontiguousarray(np.asarray(L0_ffn_w2, np.float32).T)

    vecs = np.zeros((128, 64), np.float32)
    vecs[:, 0] = np.asarray(L0_attn_b_dt, np.float32)
    vecs[:, 1] = np.asarray(L0_attn_A_log, np.float32)
    vecs[:, 2:10] = np.asarray(L0_attn_D, np.float32).reshape(8, 128).T
    vecs[:, 10:42] = np.asarray(L0_ffn_b1, np.float32).reshape(32, 128).T
    vecs[:, 42:50] = np.asarray(L0_ffn_b2, np.float32).reshape(8, 128).T

    nc = _get_nc()
    in_maps = []
    for c in range(8):
        xc = np.ascontiguousarray(
            x[2 * c:2 * c + 2].reshape(NTOK, D).T)
        in_maps.append({
            "xT": xc, "winT": winT, "wstT": wstT, "wdtT": wdtT, "wcT": wcT,
            "woutT": woutT, "w1T": w1T, "w2T": w2T, "vecs": vecs,
        })
    res = run_bass_kernel_spmd(nc, in_maps, core_ids=list(range(8)))
    out = np.empty((16, L, D), np.float32)
    for c in range(8):
        o = np.asarray(res.results[c]["outT"], np.float32)
        out[2 * c:2 * c + 2] = o.T.reshape(BLOC, L, D)
    return out


# version 16
# speedup vs baseline: 1.1472x; 1.0160x over previous
import contextlib
import numpy as np
import ml_dtypes
import concourse.bass as bass
import concourse.mybir as mybir
from concourse.bass_utils import run_bass_kernel_spmd

F32 = mybir.dt.float32
BF16 = mybir.dt.bfloat16
AF = mybir.ActivationFunctionType
ALU = mybir.AluOpType
BF = ml_dtypes.bfloat16

D = 1024
DC = 8
E2 = 2048
NST = 128
HID = 4096
HC = 32
BLOC = 2
L = 4096
NTOK = BLOC * L
TA = 512
NTA = NTOK // TA
TB = 256
NTB = NTOK // TB
MAGIC = 12582912.0
EPS = 1e-5
DT_MAX = 10.0
RATE_EPS = 1e-4

ENGS = ("T", "V", "S", "G", "Y")


def build_nc(debug=False):
    nc = bass.Bass("TRN2", target_bir_lowering=False)

    xT = nc.dram_tensor("xT", [D, NTOK], F32, kind="ExternalInput").ap()
    winT = nc.dram_tensor("winT", [D, E2], F32, kind="ExternalInput").ap()
    wstT = nc.dram_tensor("wstT", [D, NST], F32, kind="ExternalInput").ap()
    wdtT = nc.dram_tensor("wdtT", [D, NST], F32, kind="ExternalInput").ap()
    wcT = nc.dram_tensor("wcT", [NST, D], F32, kind="ExternalInput").ap()
    woutT = nc.dram_tensor("woutT", [D, D], F32, kind="ExternalInput").ap()
    w1T = nc.dram_tensor("w1T", [D, HID], F32, kind="ExternalInput").ap()
    w2T = nc.dram_tensor("w2T", [HID, D], F32, kind="ExternalInput").ap()
    vecs = nc.dram_tensor("vecs", [128, 64], F32, kind="ExternalInput").ap()
    outT = nc.dram_tensor("outT", [D, NTOK], F32, kind="ExternalOutput").ap()
    r1s = nc.dram_tensor("r1s", [D, NTOK], BF16, kind="Internal").ap()

    dbg = {}
    if debug:
        dbg["u0"] = nc.dram_tensor("dbg_u0", [D, TA], BF16, kind="ExternalOutput").ap()
        dbg["dec0"] = nc.dram_tensor("dbg_dec0", [NST, TA], F32, kind="ExternalOutput").ap()
        dbg["h0"] = nc.dram_tensor("dbg_h0", [NST, TA], F32, kind="ExternalOutput").ap()
        dbg["z0"] = nc.dram_tensor("dbg_z0", [D, TA], BF16, kind="ExternalOutput").ap()
        dbg["x10"] = nc.dram_tensor("dbg_x10", [D, TB], F32, kind="ExternalOutput").ap()
        dbg["hq0"] = nc.dram_tensor("dbg_hq0", [HID, TB], BF16, kind="ExternalOutput").ap()
        dbg["xb0"] = nc.dram_tensor("dbg_xb0", [D, TA], BF16, kind="ExternalOutput").ap()
        dbg["u0e"] = nc.dram_tensor("dbg_u0e", [D, TA], BF16, kind="ExternalOutput").ap()
        dbg["w0"] = nc.dram_tensor("dbg_w0", [128, 2048], BF16, kind="ExternalOutput").ap()

    SB_TOP = 229344
    P_BASE = 16512
    ptr = {"P": P_BASE}
    handles = {}

    def nbytes_pp(shape, dtype):
        n = 1
        for s in shape[1:]:
            n *= s
        return n * (2 if dtype == BF16 else 4)

    def sb(name, shape, dtype, region, at=None):
        if at is None:
            off = (ptr[region] + 31) // 32 * 32
            ptr[region] = off + nbytes_pp(shape, dtype)
        else:
            off = at
        assert off + nbytes_pp(shape, dtype) <= SB_TOP, (name, off)
        h = nc.alloc_sbuf_tensor_at(name, shape, dtype, offset=off)
        handles[name] = (h, off)
        return h

    vecs_sb = sb("vecs_sb", [128, 64], F32, "P")
    negrate = sb("negrate", [128, 1], F32, "P")
    tmpc1 = sb("tmpc1", [128, 1], F32, "P")
    tmpc2 = sb("tmpc2", [128, 1], F32, "P")
    b1q16 = sb("b1q16", [128, 32], F32, "P")
    b2q = sb("b2q", [128, 8], F32, "P")
    ones_bf = sb("ones_bf", [128, 1], BF16, "P")
    cb_one = sb("cb_one", [128, 1], F32, "P")
    cb_eps = sb("cb_eps", [128, 1], F32, "P")
    cb_eps256 = sb("cb_eps256", [128, 1], F32, "P")
    cb_ln16 = sb("cb_ln16", [128, 1], F32, "P")
    ones_row = sb("ones_row", [1, 128], F32, "P")
    rows1 = tuple(sb("row1_%d" % i, [1, 256], F32, "P") for i in range(4))
    rows2 = rows1

    overlay = (ptr["P"] + 31) // 32 * 32
    ptr["A"] = overlay
    ptr["B"] = overlay

    winT_sb = sb("winT_sb", [128, DC, E2], BF16, "A")
    wstT_sb = sb("wstT_sb", [128, DC, NST], BF16, "A")
    wdtT_sb = sb("wdtT_sb", [128, DC, NST], BF16, "A")
    wcT_sb = sb("wcT_sb", [128, D], BF16, "A")
    woutT_sb = sb("woutT_sb", [128, DC, D], BF16, "A")
    xb = sb("xb", [128, 3, DC, TA], BF16, "A")
    u_bf = sb("u_bf", [128, 2, DC, TA], BF16, "A")
    sg = sb("sg", [128, 2, DC, TA], BF16, "A")
    z_sb = sb("z_sb", [128, DC, TA], BF16, "A")
    ztmp8 = sb("ztmp8", [128, DC, TA], F32, "A")
    r1_sb = sb("r1_sb", [128, 2, DC, TA], BF16, "A")
    sp_t = sb("sp_t", [128, TA], F32, "A")
    e1_t = sb("e1_t", [128, TA], F32, "A")
    ld_t = sb("ld_t", [128, TA], F32, "A")
    dec_t = sb("dec_t", [128, 2, TA], F32, "A")
    omd_t = sb("omd_t", [128, TA], F32, "A")
    c_t = sb("c_t", [128, TA], F32, "A")
    h_t = sb("h_t", [128, 2, TA], F32, "A")
    hb_t = sb("hb_t", [128, 2, TA], BF16, "A")

    w1q = sb("w1q", [128, DC, HID], BF16, "B")
    w2q = sb("w2q", [128, HC, D], BF16, "B")
    r1ld = sb("r1ld", [128, 2, DC, TB], BF16, "B")
    x1_sb = sb("x1_sb", [128, 2, DC, TB], F32, "B")
    xq_sb = sb("xq_sb", [128, 2, DC, TB], BF16, "B")
    hq_sb = sb("hq_sb", [128, HC, TB], BF16, "B")
    r2_sb = sb("r2_sb", [128, DC, TB], BF16, "B")
    ob = sb("ob", [128, DC, TB], F32, "B")
    htmp8b = sb("htmp8b", [128, 4, TB], F32, "B")
    sqb1 = sb("sqb1", [128, DC, TB], BF16, "B")
    sqb2 = sb("sqb2", [128, DC, TB], BF16, "B")

    _wsbase = handles["sp_t"][1]
    wstage = [
        sb("wstage_a", [128, HID], F32, "A", at=_wsbase),
        sb("wstage_b", [128, HID], F32, "A", at=_wsbase + HID * 4),
    ]
    wstage4 = [
        sb("wstage4_a", [128, 4, D], F32, "A", at=_wsbase),
        sb("wstage4_b", [128, 4, D], F32, "A", at=_wsbase + HID * 4),
    ]
    assert _wsbase + 2 * HID * 4 <= SB_TOP
    assert _wsbase >= handles["w2q"][1] + HC * D * 2, "staging overlaps w2q"
    assert ptr["A"] <= SB_TOP and ptr["B"] <= SB_TOP

    with contextlib.ExitStack() as st:
        en = st.enter_context
        pb = [en(nc.psum_tensor(f"pb{i}", [128, 512], F32)) for i in range(8)]
        sems = {k: en(nc.semaphore(k)) for k in (
            "qw", "qx0", "qx1", "qx2", "qa0", "qa1", "qy0", "qy1", "qo",
            "qt0", "qt1", "qd", "tm", "sd", "vd")}

        prog = {e: [] for e in ENGS}
        cnt = {k: 0 for k in sems}
        waited = {e: {} for e in ENGS}
        own_sem = {"T": "tm", "V": "vd", "S": "sd"}

        def emit(eng, fn):
            prog[eng].append(fn)

        def wait(eng, sem, val, force=False, selfdep=False):
            if val is None or val <= 0:
                return
            if not force and not selfdep and own_sem.get(eng) == sem:
                return
            if not force and waited[eng].get(sem, 0) >= val:
                return
            waited[eng][sem] = max(waited[eng].get(sem, 0), val)
            emit(eng, lambda E, s=sems[sem], v=val: E.wait_ge(s, v))

        def drn(eng):
            emit(eng, lambda E: E.drain())

        def op(eng, fn, inc=None, inck=1):
            if inc is None:
                emit(eng, fn)
                return None
            cnt[inc] += inck
            emit(eng, lambda E, f=fn, s=sems[inc], k=inck: f(E).then_inc(s, k))
            return cnt[inc]

        bank_owner = [None] * 8
        bank_rr = [0]
        rbank_rr = [0]

        def take_rbank():
            b = 6 + rbank_rr[0] % 2
            rbank_rr[0] += 1
            if bank_owner[b] is not None:
                assert bank_owner[b][0] != "__taken__"
                wait("T", *bank_owner[b])
            bank_owner[b] = ("__taken__", 0)
            return b

        def take_bank():
            b = bank_rr[0] % 6
            bank_rr[0] += 1
            if bank_owner[b] is not None:
                assert bank_owner[b][0] != "__taken__", f"bank {b} re-taken before consumer recorded"
                wait("T", *bank_owner[b])
            bank_owner[b] = ("__taken__", 0)
            return b

        def free_bank(b, sem, val):
            bank_owner[b] = (sem, val)

        def mm_group(lhs_rhs, out_ap, pre_waits=()):
            b = take_bank()
            for s_name, v in pre_waits:
                wait("T", s_name, v)
            n = len(lhs_rhs)
            for i, (lhs, rhs) in enumerate(lhs_rhs):
                kw = dict(start=(i == 0), stop=(i == n - 1))
                if i == n - 1:
                    tv = op("T", lambda E, l=lhs, r=rhs, o=out_ap(b), kw=kw:
                            E.matmul(o, l, r, **kw), inc="tm")
                else:
                    op("T", lambda E, l=lhs, r=rhs, o=out_ap(b), kw=kw:
                       E.matmul(o, l, r, **kw))
            return b, tv

        def full(b):
            return pb[b][:, :]

        def ln_slice(b, T):
            return pb[b][:, 0:T]

        def row_slice(b, T):
            return pb[b][0:1, 0:T]

        def dma(eng, q, out_ap, in_ap, pre=()):
            for s_name, v in pre:
                wait(eng, s_name, v)
            return op(eng, lambda E, o=out_ap, i=in_ap: E.dma_start(out=o, in_=i),
                      inc=q, inck=16)

        dma("G", "qw", winT_sb[:, :, :], winT.rearrange("(k p) e -> p k e", p=128))
        dma("G", "qw", wstT_sb[:, :, :], wstT.rearrange("(k p) n -> p k n", p=128))
        dma("G", "qw", wdtT_sb[:, :, :], wdtT.rearrange("(k p) n -> p k n", p=128))
        dma("G", "qw", wcT_sb[:, :], wcT[:, :])
        dma("G", "qw", woutT_sb[:, :, :], woutT.rearrange("(k p) e -> p k e", p=128))
        qw_all = dma("G", "qw", vecs_sb[:, :], vecs[:, :])
        qs_vecs = None

        op("V", lambda E: E.memset(ones_bf[:], 1.0))
        op("V", lambda E: E.memset(ones_row[:], 1.0))
        op("V", lambda E: E.memset(cb_one[:], 1.0))
        op("V", lambda E: E.memset(cb_eps[:], EPS))
        op("V", lambda E: E.memset(cb_eps256[:], 256.0 * EPS))
        cv = op("V", lambda E: E.memset(cb_ln16[:], float(np.log(16.0))), inc="vd")
        wait("S", "qw", qw_all)
        wait("S", "vd", cv)
        op("S", lambda E: E.activation(out=tmpc1[:], in_=vecs_sb[:, 1:2], func=AF.Exp))
        drn("S")
        sv = op("S", lambda E: E.activation(out=tmpc2[:], in_=tmpc1[:], func=AF.Ln,
                                            bias=cb_one[:]), inc="sd")
        wait("V", "sd", sv)
        negrate_vd = op("V", lambda E: E.tensor_scalar(out=negrate[:], in0=tmpc2[:],
                                          scalar1=RATE_EPS, scalar2=-1.0,
                                          op0=ALU.add, op1=ALU.mult), inc="vd")
        wait("V", "qw", qw_all)

        def quant_cols(dst, src_ap, postmul):
            v = op("V", lambda E, s=src_ap: E.tensor_scalar(
                out=dst[:], in0=s, scalar1=64.0, scalar2=MAGIC,
                op0=ALU.mult, op1=ALU.add), inc="vd")
            wait("V", "vd", v, selfdep=True)
            v = op("V", lambda E: E.tensor_scalar(
                out=dst[:], in0=dst[:], scalar1=MAGIC, scalar2=127.0,
                op0=ALU.subtract, op1=ALU.min), inc="vd")
            wait("V", "vd", v, selfdep=True)
            v = op("V", lambda E, pm=postmul: E.tensor_scalar(
                out=dst[:], in0=dst[:], scalar1=-128.0, scalar2=pm,
                op0=ALU.max, op1=ALU.mult), inc="vd")
            return v

        quant_cols(b1q16, vecs_sb[:, 10:42], 1.0 / 64.0)
        quant_cols(b2q, vecs_sb[:, 42:50], 16.0 / 64.0)
        op("V", lambda E: E.memset(ztmp8[:, 0, 0:1], 0.0), inc="vd")

        dbgA_qs = [0]
        xb_load = [None] * NTA
        xb_free_T = [None] * NTA
        xb_free_V = [None] * NTA
        r1_done = [None] * NTA
        r1_store = [None] * NTA
        scan_vd = [None] * NTA
        silu_done = [None] * NTA
        z_done = [None] * NTA

        def load_xb(t):
            pre = []
            if t >= 3:
                pre = [("tm", xb_free_T[t - 3]), ("vd", xb_free_V[t - 3])]
            src = xT.rearrange("(c p) t -> p c t", p=128)[:, :, t * TA:(t + 1) * TA]
            xb_load[t] = dma("G", "qx%d" % (t % 3), xb[:, t % 3, :, :], src, pre=pre)

        load_xb(0)
        load_xb(1)
        load_xb(2)

        for it in range(NTA + 1):
            t = it
            tp = it - 1
            if t < NTA:
                col = slice(t * TA, (t + 1) * TA)
                par = t % 2
                xpar = t % 3
                if debug and t == 1:
                    for _e in ("T", "S", "V", "G"):
                        wait(_e, "qd", cnt["qd"], force=True)
                if debug and t == 2:
                    for _e in ("T", "S", "V", "G"):
                        wait(_e, "qd", dbgA_qs[0])
                wait("T", "qw", qw_all)
                wait("T", "qx%d" % (t % 3), xb_load[t])
                u_copy_done = None
                for e in range(8):
                    mlist = [(winT_sb[:, k, e * 128:(e + 1) * 128], xb[:, xpar, k, :])
                             for k in range(DC)]
                    b, tv = mm_group(mlist, full)
                    wait("S", "tm", tv)
                    sv = op("S", lambda E, b=b, e=e, p=par: E.copy(
                        out=u_bf[:, p, e, :], in_=full(b)), inc="sd")
                    free_bank(b, "sd", sv)
                    u_copy_done = sv
                if debug and t == 0:
                    dma("Y", "qd", dbg["u0e"].rearrange("(c p) t -> p c t", p=128),
                        u_bf[:, 0, :, :], pre=[("sd", u_copy_done)])
                mlist = [(wstT_sb[:, k, :], u_bf[:, par, k, :]) for k in range(DC)]
                bt_b, bt_tv = mm_group(mlist, full, pre_waits=[("sd", u_copy_done)])
                mlist = [(wdtT_sb[:, k, :], u_bf[:, par, k, :]) for k in range(DC)]
                dt_b, dt_tv = mm_group(mlist, full)
                wait("S", "tm", dt_tv)
                sv = op("S", lambda E, b=dt_b: E.activation(
                    out=e1_t[:], in_=full(b), func=AF.Exp,
                    bias=vecs_sb[:, 0:1]), inc="sd")
                free_bank(dt_b, "sd", sv)
                drn("S")
                sv = op("S", lambda E: E.activation(
                    out=sp_t[:], in_=e1_t[:], func=AF.Ln, bias=cb_one[:]), inc="sd")
                wait("V", "sd", sv)
                wait("V", "vd", negrate_vd, selfdep=True)
                vv = op("V", lambda E: E.tensor_scalar(
                    out=ld_t[:], in0=sp_t[:], scalar1=DT_MAX, scalar2=negrate[:],
                    op0=ALU.min, op1=ALU.mult), inc="vd")
                wait("S", "vd", vv)
                sv = op("S", lambda E, p=par: E.activation(
                    out=dec_t[:, p, :], in_=ld_t[:], func=AF.Exp), inc="sd")
                wait("V", "sd", sv)
                omd_vv = op("V", lambda E, p=par: E.tensor_scalar(
                    out=omd_t[:], in0=dec_t[:, p, :], scalar1=1.0, scalar2=-1.0,
                    op0=ALU.subtract, op1=ALU.mult), inc="vd")
                wait("V", "tm", bt_tv)
                wait("V", "vd", omd_vv, selfdep=True)
                c_vv = op("V", lambda E, b=bt_b: E.tensor_tensor(
                    out=c_t[:], in0=omd_t[:], in1=full(b), op=ALU.mult), inc="vd")
                free_bank(bt_b, "vd", c_vv)
                wait("V", "vd", c_vv, selfdep=True)
                if t % (L // TA) == 0:
                    init = 0.0
                else:
                    init = h_t[:, (t - 1) % 2, TA - 1:TA]
                scan_vd[t] = op("V", lambda E, p=par, i=init: E.tensor_tensor_scan(
                    out=h_t[:, p, :], data0=dec_t[:, p, :], data1=c_t[:],
                    initial=i, op0=ALU.mult, op1=ALU.add), inc="vd")
                wait("V", "vd", scan_vd[t], selfdep=True)
                hb_vv = op("V", lambda E, p=par: E.tensor_copy(
                    out=hb_t[:, p, :], in_=h_t[:, p, :]), inc="vd")
                for e in range(8):
                    mlist = [(winT_sb[:, k, (8 + e) * 128:(9 + e) * 128],
                              xb[:, xpar, k, :]) for k in range(DC)]
                    gb, gtv = mm_group(mlist, full)
                    wait("S", "tm", gtv)
                    ssv = op("S", lambda E, b=gb, e=e, p=par: E.activation(
                        out=sg[:, p, e, :], in_=full(b), func=AF.Silu), inc="sd")
                    free_bank(gb, "sd", ssv)
                xb_free_T[t] = gtv
                silu_done[t] = ssv

            if tp >= 0:
                colp = slice(tp * TA, (tp + 1) * TA)
                parp = tp % 2
                for e in range(DC):
                    mlist = [(woutT_sb[:, k, e * 128:(e + 1) * 128], z_sb[:, k, :])
                             for k in range(DC)]
                    b, tv = mm_group(mlist, full,
                                     pre_waits=[("vd", z_done[tp])])
                    wait("V", "tm", tv)
                    if tp >= 2 and r1_store[tp - 2] is not None:
                        wait("V", "qa%d" % (tp % 2), r1_store[tp - 2])
                    vv = op("V", lambda E, b=b, e=e, p=parp, xp=tp % 3: E.tensor_tensor(
                        out=r1_sb[:, p, e, :], in0=full(b), in1=xb[:, xp, e, :],
                        op=ALU.add), inc="vd")
                    free_bank(b, "vd", vv)
                r1_done[tp] = vv
                xb_free_V[tp] = vv
                dst = r1s.rearrange("(c p) t -> p c t", p=128)[:, :, colp]
                r1_store[tp] = dma("S", "qa%d" % (tp % 2), dst,
                                   r1_sb[:, parp, :, :],
                                   pre=[("vd", r1_done[tp])])
                if tp + 3 < NTA:
                    load_xb(tp + 3)

            if t < NTA:
                if debug and t == 1:
                    wait("V", "qd", dbgA_qs[0])
                wait("V", "sd", silu_done[t])
                for e in range(DC):
                    b, tv = mm_group(
                        [(wcT_sb[:, e * 128:(e + 1) * 128], hb_t[:, t % 2, :])],
                        full, pre_waits=[("vd", hb_vv)])
                    wait("V", "tm", tv)
                    vv = op("V", lambda E, b=b, e=e, p=t % 2: E.tensor_tensor(
                        out=ztmp8[:, e, :], in0=full(b), in1=sg[:, p, e, :],
                        op=ALU.mult), inc="vd")
                    free_bank(b, "vd", vv)
                wait("V", "vd", vv, selfdep=True)
                for e in range(DC):
                    vv = op("V", lambda E, e=e, p=t % 3: E.scalar_tensor_tensor(
                        out=z_sb[:, e, :], in0=xb[:, p, e, :],
                        scalar=vecs_sb[:, 2 + e:3 + e], in1=ztmp8[:, e, :],
                        op0=ALU.mult, op1=ALU.add), inc="vd")
                z_done[t] = vv
                if debug and t == 0:
                    dma("Y", "qd", dbg["xb0"].rearrange("(c p) t -> p c t", p=128),
                        xb[:, 0, :, :], pre=[("vd", z_done[0])])
                    dma("Y", "qd", dbg["w0"][:, :], winT_sb[:, 0, :],
                        pre=[("vd", z_done[0])])
                    dma("Y", "qd", dbg["u0"].rearrange("(c p) t -> p c t", p=128),
                        u_bf[:, 0, :, :], pre=[("sd", u_copy_done)])
                    dma("Y", "qd", dbg["dec0"][:, :], dec_t[:, 0, :],
                        pre=[("vd", scan_vd[0])])
                    dma("Y", "qd", dbg["h0"][:, :], h_t[:, 0, :],
                        pre=[("vd", scan_vd[0])])
                    dma("Y", "qd", dbg["z0"].rearrange("(c p) t -> p c t", p=128),
                        z_sb[:, :, :], pre=[("vd", z_done[0])])
                    dbgA_qs[0] = cnt["qd"]


        endA = [("tm", cnt["tm"]), ("vd", cnt["vd"]), ("sd", cnt["sd"]),
                ("qa0", cnt["qa0"]), ("qa1", cnt["qa1"])]

        wq_loads = []
        first_stage = True

        def stage_quant(dst_ap, src_ap, ncols, si, shape4=False,
                        postmul=1.0 / 1024.0):
            nonlocal first_stage
            pre = list(endA) if first_stage else []
            if len(wq_loads) >= 2:
                pre.append(("vd", wq_loads[-2][1]))
            first_stage = False
            stg = wstage4[si] if shape4 else wstage[si]
            qv = dma("Y", "qt%d" % si, stg[:, 0:ncols] if not shape4 else stg[:, :, :],
                     src_ap, pre=pre)
            wait("V", "qt%d" % si, qv)
            sap = stg[:, :, :] if shape4 else stg[:, 0:ncols]
            v = op("V", lambda E, s=sap: E.tensor_scalar(
                out=s, in0=s, scalar1=64.0, scalar2=MAGIC,
                op0=ALU.mult, op1=ALU.add), inc="vd")
            wait("V", "vd", v, selfdep=True)
            v = op("V", lambda E, s=sap: E.tensor_scalar(
                out=s, in0=s, scalar1=MAGIC, scalar2=127.0,
                op0=ALU.subtract, op1=ALU.min), inc="vd")
            wait("V", "vd", v, selfdep=True)
            vv = op("V", lambda E, s=sap, d=dst_ap, pm=postmul: E.tensor_scalar(
                out=d, in0=s, scalar1=-128.0, scalar2=pm,
                op0=ALU.max, op1=ALU.mult), inc="vd")
            wq_loads.append((qv, vv))

        for k in range(DC):
            stage_quant(w1q[:, k, :],
                        w1T.rearrange("(k p) h -> p k h", p=128)[:, k, :],
                        HID, k % 2)
        for k4 in range(HC // 4):
            stage_quant(w2q[:, 4 * k4:4 * k4 + 4, :],
                        w2T.rearrange("(k p) e -> p k e", p=128)[:, 4 * k4:4 * k4 + 4, :],
                        HID, k4 % 2, shape4=True, postmul=16.0 / 1024.0)
        wq_done = wq_loads[-1][1]

        r1_load = [None] * NTB
        r1ld_free_T = [None] * NTB
        r1ld_free_V = [None] * NTB
        out_store = [None] * NTB
        state = [dict() for _ in range(NTB)]

        def load_r1(tb):
            assert r1_load[tb] is None, tb
            pre = []
            if tb < 3:
                pre += endA + [("vd", wq_loads[-1][1])]
            if tb >= 2:
                pre += [("tm", r1ld_free_T[tb - 2]), ("vd", r1ld_free_V[tb - 2])]
            src = r1s.rearrange("(c p) t -> p c t", p=128)[:, :, tb * TB:(tb + 1) * TB]
            r1_load[tb] = dma("Y", "qy%d" % (tb % 2), r1ld[:, tb % 2, :, :], src, pre=pre)

        load_r1(0)
        load_r1(1)
        prev_htmp_v = [wq_loads[-1][1]]
        prev_htmp_s = [0]

        def emit_sq1(n):
            p = n % 2
            wait("S", "qy%d" % p, r1_load[n])
            sqv = None
            for c in range(DC):
                sqv = op("S", lambda E, c=c, p=p: E.activation(
                    out=sqb1[:, c, :], in_=r1ld[:, p, c, :], func=AF.Square),
                    inc="sd")
            state[n]["sq1_sd"] = sqv

        def rows_and_s(bm, bv, tv_m, tv_v, rows, eps_ap=None, exp_bias=None):
            if eps_ap is None:
                eps_ap = cb_eps
            """V rows chain + S ln/exp; returns (rows_vd, sv, free info)."""
            mr, tr, vr, rr = rows
            wait("V", "tm", tv_m)
            vv = op("V", lambda E, b=bm, mr=mr: E.tensor_scalar(
                out=mr[0:1, :], in0=row_slice(b, TB), scalar1=1.0 / D,
                scalar2=None, op0=ALU.mult, op1=ALU.bypass), inc="vd")
            free_bank(bm, "vd", vv)
            wait("V", "vd", vv, selfdep=True)
            vv = op("V", lambda E, mr=mr, tr=tr: E.tensor_tensor(
                out=tr[0:1, :], in0=mr[0:1, :], in1=mr[0:1, :],
                op=ALU.mult), inc="vd")
            wait("V", "tm", tv_v)
            wait("V", "vd", vv, selfdep=True)
            vv = op("V", lambda E, b=bv, tr=tr, vr=vr: E.scalar_tensor_tensor(
                out=vr[0:1, :], in0=row_slice(b, TB), scalar=1.0 / D,
                in1=tr[0:1, :], op0=ALU.mult, op1=ALU.subtract), inc="vd")
            free_bank(bv, "vd", vv)
            rows_vd = vv
            wait("S", "vd", rows_vd)
            op("S", lambda E, tr=tr, vr=vr, ep=eps_ap: E.activation(
                out=tr[0:1, :], in_=vr[0:1, :], func=AF.Ln, bias=ep[0:1, :]))
            drn("S")
            if exp_bias is None:
                sv = op("S", lambda E, tr=tr, rr=rr: E.activation(
                    out=rr[0:1, :], in_=tr[0:1, :], func=AF.Exp,
                    scale=-0.5), inc="sd")
            else:
                sv = op("S", lambda E, tr=tr, rr=rr, eb=exp_bias: E.activation(
                    out=rr[0:1, :], in_=tr[0:1, :], func=AF.Exp,
                    scale=-0.5, bias=eb[0:1, :]), inc="sd")
            return rows_vd, sv

        def emit_bcasts(rows, rows_vd, sv):
            mr, tr, vr, rr = rows
            bmb = take_rbank()
            wait("T", "vd", rows_vd)
            op("T", lambda E, b=bmb, mr=mr: E.matmul(
                ln_slice(b, TB), ones_row[:], mr[0:1, :], start=True, stop=True),
                inc="tm")
            brs = take_rbank()
            wait("T", "sd", sv)
            tv_rs = op("T", lambda E, b=brs, rr=rr: E.matmul(
                ln_slice(b, TB), ones_row[:], rr[0:1, :], start=True, stop=True),
                inc="tm")
            return bmb, brs, tv_rs

        def emit_apply(src_chunks, dst_chunks, bmb, brs, tv_rs):
            wait("V", "tm", tv_rs)
            for c in range(DC):
                vv1 = op("V", lambda E, c=c, b=bmb, s=src_chunks[c], d=dst_chunks[c]:
                         E.tensor_tensor(out=d, in0=s, in1=ln_slice(b, TB),
                                         op=ALU.subtract), inc="vd")
            free_bank(bmb, "vd", vv1)
            wait("V", "vd", vv1, selfdep=True)
            for c in range(DC):
                vv2 = op("V", lambda E, c=c, b=brs, d=dst_chunks[c]:
                         E.tensor_tensor(out=d, in0=d, in1=ln_slice(b, TB),
                                         op=ALU.mult), inc="vd")
            free_bank(brs, "vd", vv2)
            return vv2

        for it in range(-1, NTB + 1):
            t3, t2, t1 = it - 1, it, it + 1
            do1 = 0 <= t1 < NTB
            do2 = 0 <= t2 < NTB
            do3 = 0 <= t3 < NTB
            if do1:
                p1 = t1 % 2
                wait("T", "qy%d" % p1, r1_load[t1])
                wait("T", "vd", wq_done)
                if "sq1_sd" not in state[t1]:
                    emit_sq1(t1)
                src1 = [r1ld[:, p1, c, :] for c in range(DC)]
                dst1 = [x1_sb[:, p1, c, :] for c in range(DC)]
                bm1, tvm1 = mm_group([(ones_bf[:], src1[c]) for c in range(DC)],
                                     lambda b: row_slice(b, TB))
                bv1, tvv1 = mm_group([(ones_bf[:], sqb1[:, c, :]) for c in range(DC)],
                                     lambda b: row_slice(b, TB),
                                     pre_waits=[("sd", state[t1]["sq1_sd"])])
                r1ld_free_T[t1] = max(tvm1, tvv1)
                rows1_vd, sv1 = rows_and_s(bm1, bv1, tvm1, tvv1, rows1,
                                           exp_bias=cb_ln16)
            if do3:
                p3 = t3 % 2
                st3 = state[t3]
                r2_vds = []
                sq2_sds = []
                for e in range(DC):
                    mlist = [(w2q[:, k, e * 128:(e + 1) * 128], hq_sb[:, k, :])
                             for k in range(HC)]
                    b, tv = mm_group(mlist, lambda bb: ln_slice(bb, TB),
                                     pre_waits=[("vd", st3["hq_vd"])])
                    wait("V", "tm", tv)
                    r2_vd = op("V", lambda E, b=b, e=e, p=p3: E.scalar_tensor_tensor(
                        out=r2_sb[:, e, :], in0=ln_slice(b, TB),
                        scalar=b2q[:, e:e + 1], in1=x1_sb[:, p, e, :],
                        op0=ALU.add, op1=ALU.add), inc="vd")
                    free_bank(b, "vd", r2_vd)
                    r2_vds.append(r2_vd)
                    wait("S", "vd", r2_vd)
                    sq2_sds.append(op("S", lambda E, e=e: E.activation(
                        out=sqb2[:, e, :], in_=r2_sb[:, e, :], func=AF.Square),
                        inc="sd"))
                    if e == 3 and do1:
                        bmb1, brs1, tvrs1 = emit_bcasts(rows1, rows1_vd, sv1)
                bm2 = take_bank()
                for e in range(DC):
                    wait("T", "vd", r2_vds[e])
                    kw = dict(start=(e == 0), stop=(e == DC - 1))
                    if e == DC - 1:
                        tvm2 = op("T", lambda E, e=e, b=bm2, kw=kw: E.matmul(
                            row_slice(b, TB), ones_bf[:], r2_sb[:, e, :], **kw),
                            inc="tm")
                    else:
                        op("T", lambda E, e=e, b=bm2, kw=kw: E.matmul(
                            row_slice(b, TB), ones_bf[:], r2_sb[:, e, :], **kw))
                bv2 = take_bank()
                for e in range(DC):
                    wait("T", "sd", sq2_sds[e])
                    kw = dict(start=(e == 0), stop=(e == DC - 1))
                    if e == DC - 1:
                        tvv2 = op("T", lambda E, e=e, b=bv2, kw=kw: E.matmul(
                            row_slice(b, TB), ones_bf[:], sqb2[:, e, :], **kw),
                            inc="tm")
                    else:
                        op("T", lambda E, e=e, b=bv2, kw=kw: E.matmul(
                            row_slice(b, TB), ones_bf[:], sqb2[:, e, :], **kw))
            elif do1:
                bmb1, brs1, tvrs1 = emit_bcasts(rows1, rows1_vd, sv1)
            if do1:
                if do3:
                    wait("V", "vd", r2_vd, selfdep=True)
                apply1_vd = emit_apply(src1, dst1, bmb1, brs1, tvrs1)
                r1ld_free_V[t1] = apply1_vd
                if debug and t1 == 0:
                    dma("Y", "qd", dbg["x10"].rearrange("(c p) t -> p c t", p=128),
                        x1_sb[:, 0, :, :], pre=[("vd", apply1_vd)])
                if debug and t1 == 1:
                    wait("V", "qd", cnt["qd"])
                wait("V", "vd", apply1_vd, selfdep=True)
                for half in range(2):
                    wait("V", "vd", prev_htmp_v[0], selfdep=True)
                    wait("V", "sd", prev_htmp_s[0])
                    for j in range(4):
                        c = half * 4 + j
                        v = op("V", lambda E, c=c, j=j, p=p1: E.tensor_scalar(
                            out=htmp8b[:, j, :], in0=x1_sb[:, p, c, :],
                            scalar1=MAGIC, scalar2=MAGIC,
                            op0=ALU.add, op1=ALU.subtract), inc="vd")
                    wait("V", "vd", v, selfdep=True)
                    for j in range(4):
                        c = half * 4 + j
                        xq_vd = op("V", lambda E, c=c, j=j, p=p1: E.tensor_scalar(
                            out=xq_sb[:, p, c, :], in0=htmp8b[:, j, :],
                            scalar1=127.0, scalar2=-128.0,
                            op0=ALU.min, op1=ALU.max), inc="vd")
                    prev_htmp_v[0] = xq_vd
                state[t1]["xq_vd"] = xq_vd
                if t1 + 2 < NTB:
                    load_r1(t1 + 2)
            ln2_rows = [None]
            if do3:
                ln2_rows[0] = rows_and_s(bm2, bv2, tvm2, tvv2, rows2,
                                         eps_ap=cb_eps256)

            def emit_ln2_rows():
                rows2_vd, sv2 = ln2_rows[0]
                return emit_bcasts(rows2, rows2_vd, sv2)

            def emit_ln2_apply(bcasts):
                bmb2, brs2, tvrs2 = bcasts
                if t3 >= 1 and out_store[t3 - 1] is not None:
                    wait("V", "qo", out_store[t3 - 1])
                src2 = [r2_sb[:, c, :] for c in range(DC)]
                dst2 = [ob[:, c, :] for c in range(DC)]
                apply2_vd = emit_apply(src2, dst2, bmb2, brs2, tvrs2)
                if debug and t3 == 0:
                    dma("Y", "qd", dbg["hq0"].rearrange("(c p) t -> p c t", p=128),
                        hq_sb[:, :, :], pre=[("vd", st3["hq_vd"])])
                colp = slice(t3 * TB, (t3 + 1) * TB)
                dstap = outT.rearrange("(c p) t -> p c t", p=128)[:, :, colp]
                out_store[t3] = dma("Y", "qo", dstap, ob[:, :, :],
                                    pre=[("vd", apply2_vd)])

            if do3 and not do2:
                emit_ln2_apply(emit_ln2_rows())
            if do2:
                p2 = t2 % 2
                hq_vd = None
                for gsub in range(HC // 4):
                    groups = []
                    for j in range(4):
                        hc = gsub * 4 + j
                        mlist = [(w1q[:, k, hc * 128:(hc + 1) * 128],
                                  xq_sb[:, p2, k, :]) for k in range(DC)]
                        groups.append(mm_group(mlist, lambda bb: ln_slice(bb, TB),
                                               pre_waits=[("vd", state[t2]["xq_vd"])]))
                    wait("V", "vd", prev_htmp_v[0], selfdep=True)
                    wait("V", "sd", prev_htmp_s[0])
                    for j in range(4):
                        b, tv = groups[j]
                        hc = gsub * 4 + j
                        wait("V", "tm", tv)
                        v = op("V", lambda E, b=b, hc=hc, j=j: E.tensor_scalar(
                            out=htmp8b[:, j, :], in0=ln_slice(b, TB),
                            scalar1=b1q16[:, hc:hc + 1], scalar2=16.0,
                            op0=ALU.add, op1=ALU.mult), inc="vd")
                        free_bank(b, "vd", v)
                    wait("V", "vd", v, selfdep=True)
                    for j in range(4):
                        v = op("V", lambda E, j=j: E.tensor_scalar(
                            out=htmp8b[:, j, :], in0=htmp8b[:, j, :],
                            scalar1=MAGIC, scalar2=MAGIC,
                            op0=ALU.add, op1=ALU.subtract), inc="vd")
                    prev_htmp_v[0] = v
                    wait("V", "vd", v, selfdep=True)
                    for j in range(4):
                        hc = gsub * 4 + j
                        v = op("V", lambda E, j=j, hc=hc: E.tensor_scalar(
                            out=hq_sb[:, hc, :], in0=htmp8b[:, j, :],
                            scalar1=127.0, scalar2=0.0,
                            op0=ALU.min, op1=ALU.max), inc="vd")
                    hq_sd = None
                    prev_htmp_v[0] = v
                    if gsub == 6 and do3:
                        ln2_bcasts = emit_ln2_rows()
                state[t2]["hq_vd"] = v
                if do3:
                    emit_ln2_apply(ln2_bcasts)
            if 0 <= it + 2 < NTB and "sq1_sd" not in state[it + 2]:
                emit_sq1(it + 2)

        wait("S", "qo", cnt["qo"], force=True)
        wait("S", "qa0", cnt["qa0"], force=True)
        wait("S", "qa1", cnt["qa1"], force=True)
        for q in ("qy0", "qy1", "qt0", "qt1", "qd"):
            if cnt[q]:
                wait("Y", q, cnt[q], force=True)
        for q in ("qw", "qx0", "qx1", "qx2"):
            wait("G", q, cnt[q], force=True)

        block = en(nc.Block())

        @block.tensor
        def _(E):
            for f in prog["T"]:
                f(E)

        @block.vector
        def _(E):
            for f in prog["V"]:
                f(E)

        @block.scalar
        def _(E):
            for f in prog["S"]:
                f(E)

        @block.gpsimd
        def _(E):
            for f in prog["G"]:
                f(E)

        @block.sync
        def _(E):
            for f in prog["Y"]:
                f(E)

    return nc


_NC_CACHE = {}


def _get_nc():
    if "nc" not in _NC_CACHE:
        _NC_CACHE["nc"] = build_nc()
    return _NC_CACHE["nc"]


def kernel(L0_attn_x, L0_attn_W_in, L0_attn_W_state, L0_attn_W_dt, L0_attn_b_dt,
           L0_attn_A_log, L0_attn_W_C, L0_attn_D, L0_attn_W_out,
           L0_ffn_w1, L0_ffn_b1, L0_ffn_w2, L0_ffn_b2):
    x = np.asarray(L0_attn_x, np.float32)
    B, Lq, Dm = x.shape
    assert (B, Lq, Dm) == (16, L, D)

    winT = np.ascontiguousarray(np.asarray(L0_attn_W_in, np.float32).T)
    wstT = np.ascontiguousarray(np.asarray(L0_attn_W_state, np.float32).T)
    wdtT = np.ascontiguousarray(np.asarray(L0_attn_W_dt, np.float32).T)
    wcT = np.ascontiguousarray(np.asarray(L0_attn_W_C, np.float32).T)
    woutT = np.ascontiguousarray(np.asarray(L0_attn_W_out, np.float32).T)
    w1T = np.ascontiguousarray(np.asarray(L0_ffn_w1, np.float32).T)
    w2T = np.ascontiguousarray(np.asarray(L0_ffn_w2, np.float32).T)

    vecs = np.zeros((128, 64), np.float32)
    vecs[:, 0] = np.asarray(L0_attn_b_dt, np.float32)
    vecs[:, 1] = np.asarray(L0_attn_A_log, np.float32)
    vecs[:, 2:10] = np.asarray(L0_attn_D, np.float32).reshape(8, 128).T
    vecs[:, 10:42] = np.asarray(L0_ffn_b1, np.float32).reshape(32, 128).T
    vecs[:, 42:50] = np.asarray(L0_ffn_b2, np.float32).reshape(8, 128).T

    nc = _get_nc()
    in_maps = []
    for c in range(8):
        xc = np.ascontiguousarray(
            x[2 * c:2 * c + 2].reshape(NTOK, D).T)
        in_maps.append({
            "xT": xc, "winT": winT, "wstT": wstT, "wdtT": wdtT, "wcT": wcT,
            "woutT": woutT, "w1T": w1T, "w2T": w2T, "vecs": vecs,
        })
    res = run_bass_kernel_spmd(nc, in_maps, core_ids=list(range(8)))
    out = np.empty((16, L, D), np.float32)
    for c in range(8):
        o = np.asarray(res.results[c]["outT"], np.float32)
        out[2 * c:2 * c + 2] = o.T.reshape(BLOC, L, D)
    return out


# version 17
# speedup vs baseline: 1.1671x; 1.0173x over previous
import contextlib
import numpy as np
import ml_dtypes
import concourse.bass as bass
import concourse.mybir as mybir
from concourse.bass_utils import run_bass_kernel_spmd

F32 = mybir.dt.float32
BF16 = mybir.dt.bfloat16
AF = mybir.ActivationFunctionType
ALU = mybir.AluOpType
BF = ml_dtypes.bfloat16

D = 1024
DC = 8
E2 = 2048
NST = 128
HID = 4096
HC = 32
BLOC = 2
L = 4096
NTOK = BLOC * L
TA = 512
NTA = NTOK // TA
TB = 256
NTB = NTOK // TB
MAGIC = 12582912.0
EPS = 1e-5
DT_MAX = 10.0
RATE_EPS = 1e-4

ENGS = ("T", "V", "S", "G", "Y")


def build_nc(debug=False):
    nc = bass.Bass("TRN2", target_bir_lowering=False)

    xT = nc.dram_tensor("xT", [D, NTOK], F32, kind="ExternalInput").ap()
    winT = nc.dram_tensor("winT", [D, E2], F32, kind="ExternalInput").ap()
    wstT = nc.dram_tensor("wstT", [D, NST], F32, kind="ExternalInput").ap()
    wdtT = nc.dram_tensor("wdtT", [D, NST], F32, kind="ExternalInput").ap()
    wcT = nc.dram_tensor("wcT", [NST, D], F32, kind="ExternalInput").ap()
    woutT = nc.dram_tensor("woutT", [D, D], F32, kind="ExternalInput").ap()
    w1T = nc.dram_tensor("w1T", [D, HID], F32, kind="ExternalInput").ap()
    w2T = nc.dram_tensor("w2T", [HID, D], F32, kind="ExternalInput").ap()
    vecs = nc.dram_tensor("vecs", [128, 64], F32, kind="ExternalInput").ap()
    outT = nc.dram_tensor("outT", [D, NTOK], F32, kind="ExternalOutput").ap()
    r1s = nc.dram_tensor("r1s", [D, NTOK], BF16, kind="Internal").ap()

    dbg = {}
    if debug:
        dbg["u0"] = nc.dram_tensor("dbg_u0", [D, TA], BF16, kind="ExternalOutput").ap()
        dbg["dec0"] = nc.dram_tensor("dbg_dec0", [NST, TA], F32, kind="ExternalOutput").ap()
        dbg["h0"] = nc.dram_tensor("dbg_h0", [NST, TA], F32, kind="ExternalOutput").ap()
        dbg["z0"] = nc.dram_tensor("dbg_z0", [D, TA], BF16, kind="ExternalOutput").ap()
        dbg["x10"] = nc.dram_tensor("dbg_x10", [D, TB], F32, kind="ExternalOutput").ap()
        dbg["hq0"] = nc.dram_tensor("dbg_hq0", [HID, TB], BF16, kind="ExternalOutput").ap()
        dbg["xb0"] = nc.dram_tensor("dbg_xb0", [D, TA], BF16, kind="ExternalOutput").ap()
        dbg["u0e"] = nc.dram_tensor("dbg_u0e", [D, TA], BF16, kind="ExternalOutput").ap()
        dbg["w0"] = nc.dram_tensor("dbg_w0", [128, 2048], BF16, kind="ExternalOutput").ap()

    SB_TOP = 229344
    P_BASE = 16512
    ptr = {"P": P_BASE}
    handles = {}

    def nbytes_pp(shape, dtype):
        n = 1
        for s in shape[1:]:
            n *= s
        return n * (2 if dtype == BF16 else 4)

    def sb(name, shape, dtype, region, at=None):
        if at is None:
            off = (ptr[region] + 31) // 32 * 32
            ptr[region] = off + nbytes_pp(shape, dtype)
        else:
            off = at
        assert off + nbytes_pp(shape, dtype) <= SB_TOP, (name, off)
        h = nc.alloc_sbuf_tensor_at(name, shape, dtype, offset=off)
        handles[name] = (h, off)
        return h

    vecs_sb = sb("vecs_sb", [128, 64], F32, "P")
    negrate = sb("negrate", [128, 1], F32, "P")
    tmpc1 = sb("tmpc1", [128, 1], F32, "P")
    tmpc2 = sb("tmpc2", [128, 1], F32, "P")
    b1q16 = sb("b1q16", [128, 32], F32, "P")
    b2q = sb("b2q", [128, 8], F32, "P")
    ones_bf = sb("ones_bf", [128, 1], BF16, "P")
    cb_one = sb("cb_one", [128, 1], F32, "P")
    cb_eps = sb("cb_eps", [128, 1], F32, "P")
    cb_eps256 = sb("cb_eps256", [128, 1], F32, "P")
    cb_ln16 = sb("cb_ln16", [128, 1], F32, "P")
    ones_row = sb("ones_row", [1, 128], F32, "P")
    rows1 = tuple(sb("row1_%d" % i, [1, 256], F32, "P") for i in range(4))
    rows2 = rows1

    overlay = (ptr["P"] + 31) // 32 * 32
    ptr["A"] = overlay
    ptr["B"] = overlay

    winT_sb = sb("winT_sb", [128, DC, E2], BF16, "A")
    wstT_sb = sb("wstT_sb", [128, DC, NST], BF16, "A")
    wdtT_sb = sb("wdtT_sb", [128, DC, NST], BF16, "A")
    wcT_sb = sb("wcT_sb", [128, D], BF16, "A")
    woutT_sb = sb("woutT_sb", [128, DC, D], BF16, "A")
    xb = sb("xb", [128, 3, DC, TA], BF16, "A")
    u_bf = sb("u_bf", [128, 2, DC, TA], BF16, "A")
    sg = sb("sg", [128, 2, DC, TA], BF16, "A")
    z_sb = sb("z_sb", [128, DC, TA], BF16, "A")
    ztmp8 = sb("ztmp8", [128, DC, TA], F32, "A")
    r1_sb = sb("r1_sb", [128, 2, DC, TA], BF16, "A")
    sp_t = sb("sp_t", [128, TA], F32, "A")
    e1_t = sb("e1_t", [128, TA], F32, "A")
    ld_t = sb("ld_t", [128, TA], F32, "A")
    dec_t = sb("dec_t", [128, 2, TA], F32, "A")
    omd_t = sb("omd_t", [128, TA], F32, "A")
    c_t = sb("c_t", [128, TA], F32, "A")
    h_t = sb("h_t", [128, 2, TA], F32, "A")
    hb_t = sb("hb_t", [128, 2, TA], BF16, "A")

    w1q = sb("w1q", [128, DC, HID], BF16, "B")
    w2q = sb("w2q", [128, HC, D], BF16, "B")
    r1ld = sb("r1ld", [128, 2, DC, TB], BF16, "B")
    x1_sb = sb("x1_sb", [128, 2, DC, TB], F32, "B")
    xq_sb = sb("xq_sb", [128, 2, DC, TB], BF16, "B")
    hq_sb = sb("hq_sb", [128, HC, TB], BF16, "B")
    r2_sb = sb("r2_sb", [128, DC, TB], BF16, "B")
    ob = sb("ob", [128, DC, TB], F32, "B")
    htmp8b = sb("htmp8b", [128, 4, TB], F32, "B")
    sqb1 = sb("sqb1", [128, DC, TB], BF16, "B")
    sqb2 = sb("sqb2", [128, DC, TB], BF16, "B")

    _wsbase = handles["sp_t"][1]
    wstage = [
        sb("wstage_a", [128, HID], F32, "A", at=_wsbase),
        sb("wstage_b", [128, HID], F32, "A", at=_wsbase + HID * 4),
    ]
    wstage4 = [
        sb("wstage4_a", [128, 4, D], F32, "A", at=_wsbase),
        sb("wstage4_b", [128, 4, D], F32, "A", at=_wsbase + HID * 4),
    ]
    assert _wsbase + 2 * HID * 4 <= SB_TOP
    assert _wsbase >= handles["w2q"][1] + HC * D * 2, "staging overlaps w2q"
    assert ptr["A"] <= SB_TOP and ptr["B"] <= SB_TOP

    with contextlib.ExitStack() as st:
        en = st.enter_context
        pb = [en(nc.psum_tensor(f"pb{i}", [128, 512], F32)) for i in range(8)]
        sems = {k: en(nc.semaphore(k)) for k in (
            "qw", "qx0", "qx1", "qx2", "qa0", "qa1", "qy0", "qy1", "qo",
            "qt0", "qt1", "qd", "tm", "sd", "vd")}

        prog = {e: [] for e in ENGS}
        cnt = {k: 0 for k in sems}
        waited = {e: {} for e in ENGS}
        own_sem = {"T": "tm", "V": "vd", "S": "sd"}

        def emit(eng, fn):
            prog[eng].append(fn)

        def wait(eng, sem, val, force=False, selfdep=False):
            if val is None or val <= 0:
                return
            if not force and not selfdep and own_sem.get(eng) == sem:
                return
            if not force and waited[eng].get(sem, 0) >= val:
                return
            waited[eng][sem] = max(waited[eng].get(sem, 0), val)
            emit(eng, lambda E, s=sems[sem], v=val: E.wait_ge(s, v))

        def drn(eng):
            emit(eng, lambda E: E.drain())

        def op(eng, fn, inc=None, inck=1):
            if inc is None:
                emit(eng, fn)
                return None
            cnt[inc] += inck
            emit(eng, lambda E, f=fn, s=sems[inc], k=inck: f(E).then_inc(s, k))
            return cnt[inc]

        bank_owner = [None] * 8
        bank_rr = [0]
        rbank_rr = [0]

        def take_rbank():
            b = 6 + rbank_rr[0] % 2
            rbank_rr[0] += 1
            if bank_owner[b] is not None:
                assert bank_owner[b][0] != "__taken__"
                wait("T", *bank_owner[b])
            bank_owner[b] = ("__taken__", 0)
            return b

        def take_bank():
            b = bank_rr[0] % 6
            bank_rr[0] += 1
            if bank_owner[b] is not None:
                assert bank_owner[b][0] != "__taken__", f"bank {b} re-taken before consumer recorded"
                wait("T", *bank_owner[b])
            bank_owner[b] = ("__taken__", 0)
            return b

        def free_bank(b, sem, val):
            bank_owner[b] = (sem, val)

        def mm_group(lhs_rhs, out_ap, pre_waits=()):
            b = take_bank()
            for s_name, v in pre_waits:
                wait("T", s_name, v)
            n = len(lhs_rhs)
            for i, (lhs, rhs) in enumerate(lhs_rhs):
                kw = dict(start=(i == 0), stop=(i == n - 1))
                if i == n - 1:
                    tv = op("T", lambda E, l=lhs, r=rhs, o=out_ap(b), kw=kw:
                            E.matmul(o, l, r, **kw), inc="tm")
                else:
                    op("T", lambda E, l=lhs, r=rhs, o=out_ap(b), kw=kw:
                       E.matmul(o, l, r, **kw))
            return b, tv

        def full(b):
            return pb[b][:, :]

        def ln_slice(b, T):
            return pb[b][:, 0:T]

        def row_slice(b, T):
            return pb[b][0:1, 0:T]

        def dma(eng, q, out_ap, in_ap, pre=()):
            for s_name, v in pre:
                wait(eng, s_name, v)
            return op(eng, lambda E, o=out_ap, i=in_ap: E.dma_start(out=o, in_=i),
                      inc=q, inck=16)

        dma("G", "qw", winT_sb[:, :, :], winT.rearrange("(k p) e -> p k e", p=128))
        dma("G", "qw", wstT_sb[:, :, :], wstT.rearrange("(k p) n -> p k n", p=128))
        dma("G", "qw", wdtT_sb[:, :, :], wdtT.rearrange("(k p) n -> p k n", p=128))
        dma("G", "qw", wcT_sb[:, :], wcT[:, :])
        dma("G", "qw", woutT_sb[:, :, :], woutT.rearrange("(k p) e -> p k e", p=128))
        qw_all = dma("G", "qw", vecs_sb[:, :], vecs[:, :])
        qs_vecs = None

        op("V", lambda E: E.memset(ones_bf[:], 1.0))
        op("V", lambda E: E.memset(ones_row[:], 1.0))
        op("V", lambda E: E.memset(cb_one[:], 1.0))
        op("V", lambda E: E.memset(cb_eps[:], EPS))
        op("V", lambda E: E.memset(cb_eps256[:], 256.0 * EPS))
        cv = op("V", lambda E: E.memset(cb_ln16[:], float(np.log(16.0))), inc="vd")
        wait("S", "qw", qw_all)
        wait("S", "vd", cv)
        op("S", lambda E: E.activation(out=tmpc1[:], in_=vecs_sb[:, 1:2], func=AF.Exp))
        drn("S")
        sv = op("S", lambda E: E.activation(out=tmpc2[:], in_=tmpc1[:], func=AF.Ln,
                                            bias=cb_one[:]), inc="sd")
        wait("V", "sd", sv)
        negrate_vd = op("V", lambda E: E.tensor_scalar(out=negrate[:], in0=tmpc2[:],
                                          scalar1=RATE_EPS, scalar2=-1.0,
                                          op0=ALU.add, op1=ALU.mult), inc="vd")
        wait("V", "qw", qw_all)

        def quant_cols(dst, src_ap, postmul):
            v = op("V", lambda E, s=src_ap: E.tensor_scalar(
                out=dst[:], in0=s, scalar1=64.0, scalar2=MAGIC,
                op0=ALU.mult, op1=ALU.add), inc="vd")
            wait("V", "vd", v, selfdep=True)
            v = op("V", lambda E: E.tensor_scalar(
                out=dst[:], in0=dst[:], scalar1=MAGIC, scalar2=127.0,
                op0=ALU.subtract, op1=ALU.min), inc="vd")
            wait("V", "vd", v, selfdep=True)
            v = op("V", lambda E, pm=postmul: E.tensor_scalar(
                out=dst[:], in0=dst[:], scalar1=-128.0, scalar2=pm,
                op0=ALU.max, op1=ALU.mult), inc="vd")
            return v

        quant_cols(b1q16, vecs_sb[:, 10:42], 1.0 / 64.0)
        quant_cols(b2q, vecs_sb[:, 42:50], 16.0 / 64.0)
        op("V", lambda E: E.memset(ztmp8[:, 0, 0:1], 0.0), inc="vd")

        dbgA_qs = [0]
        xb_load = [None] * NTA
        xb_free_T = [None] * NTA
        xb_free_V = [None] * NTA
        r1_done = [None] * NTA
        r1_store = [None] * NTA
        scan_vd = [None] * NTA
        silu_done = [None] * NTA
        z_done = [None] * NTA

        def load_xb(t):
            pre = []
            if t >= 3:
                pre = [("tm", xb_free_T[t - 3]), ("vd", xb_free_V[t - 3])]
            src = xT.rearrange("(c p) t -> p c t", p=128)[:, :, t * TA:(t + 1) * TA]
            xb_load[t] = dma("G", "qx%d" % (t % 3), xb[:, t % 3, :, :], src, pre=pre)

        load_xb(0)
        load_xb(1)
        load_xb(2)

        for it in range(NTA + 1):
            t = it
            tp = it - 1
            if t < NTA:
                col = slice(t * TA, (t + 1) * TA)
                par = t % 2
                xpar = t % 3
                if debug and t == 1:
                    for _e in ("T", "S", "V", "G"):
                        wait(_e, "qd", cnt["qd"], force=True)
                if debug and t == 2:
                    for _e in ("T", "S", "V", "G"):
                        wait(_e, "qd", dbgA_qs[0])
                wait("T", "qw", qw_all)
                wait("T", "qx%d" % (t % 3), xb_load[t])
                u_copy_done = None
                for e in range(8):
                    mlist = [(winT_sb[:, k, e * 128:(e + 1) * 128], xb[:, xpar, k, :])
                             for k in range(DC)]
                    b, tv = mm_group(mlist, full)
                    wait("S", "tm", tv)
                    sv = op("S", lambda E, b=b, e=e, p=par: E.copy(
                        out=u_bf[:, p, e, :], in_=full(b)), inc="sd")
                    free_bank(b, "sd", sv)
                    u_copy_done = sv
                if debug and t == 0:
                    dma("Y", "qd", dbg["u0e"].rearrange("(c p) t -> p c t", p=128),
                        u_bf[:, 0, :, :], pre=[("sd", u_copy_done)])
                mlist = [(wstT_sb[:, k, :], u_bf[:, par, k, :]) for k in range(DC)]
                bt_b, bt_tv = mm_group(mlist, full, pre_waits=[("sd", u_copy_done)])
                mlist = [(wdtT_sb[:, k, :], u_bf[:, par, k, :]) for k in range(DC)]
                dt_b, dt_tv = mm_group(mlist, full)
                wait("S", "tm", dt_tv)
                sv = op("S", lambda E, b=dt_b: E.activation(
                    out=e1_t[:], in_=full(b), func=AF.Exp,
                    bias=vecs_sb[:, 0:1]), inc="sd")
                free_bank(dt_b, "sd", sv)
                drn("S")
                sv = op("S", lambda E: E.activation(
                    out=sp_t[:], in_=e1_t[:], func=AF.Ln, bias=cb_one[:]), inc="sd")
                wait("V", "sd", sv)
                wait("V", "vd", negrate_vd, selfdep=True)
                vv = op("V", lambda E: E.tensor_scalar(
                    out=ld_t[:], in0=sp_t[:], scalar1=DT_MAX, scalar2=negrate[:],
                    op0=ALU.min, op1=ALU.mult), inc="vd")
                wait("S", "vd", vv)
                sv = op("S", lambda E, p=par: E.activation(
                    out=dec_t[:, p, :], in_=ld_t[:], func=AF.Exp), inc="sd")
                wait("V", "sd", sv)
                omd_vv = op("V", lambda E, p=par: E.tensor_scalar(
                    out=omd_t[:], in0=dec_t[:, p, :], scalar1=1.0, scalar2=-1.0,
                    op0=ALU.subtract, op1=ALU.mult), inc="vd")
                wait("V", "tm", bt_tv)
                wait("V", "vd", omd_vv, selfdep=True)
                c_vv = op("V", lambda E, b=bt_b: E.tensor_tensor(
                    out=c_t[:], in0=omd_t[:], in1=full(b), op=ALU.mult), inc="vd")
                free_bank(bt_b, "vd", c_vv)
                wait("V", "vd", c_vv, selfdep=True)
                if t % (L // TA) == 0:
                    init = 0.0
                else:
                    init = h_t[:, (t - 1) % 2, TA - 1:TA]
                scan_vd[t] = op("V", lambda E, p=par, i=init: E.tensor_tensor_scan(
                    out=h_t[:, p, :], data0=dec_t[:, p, :], data1=c_t[:],
                    initial=i, op0=ALU.mult, op1=ALU.add), inc="vd")
                wait("V", "vd", scan_vd[t], selfdep=True)
                hb_vv = op("V", lambda E, p=par: E.tensor_copy(
                    out=hb_t[:, p, :], in_=h_t[:, p, :]), inc="vd")
                for e in range(8):
                    mlist = [(winT_sb[:, k, (8 + e) * 128:(9 + e) * 128],
                              xb[:, xpar, k, :]) for k in range(DC)]
                    gb, gtv = mm_group(mlist, full)
                    wait("S", "tm", gtv)
                    ssv = op("S", lambda E, b=gb, e=e, p=par: E.activation(
                        out=sg[:, p, e, :], in_=full(b), func=AF.Silu), inc="sd")
                    free_bank(gb, "sd", ssv)
                xb_free_T[t] = gtv
                silu_done[t] = ssv

            if tp >= 0:
                colp = slice(tp * TA, (tp + 1) * TA)
                parp = tp % 2
                for e in range(DC):
                    mlist = [(woutT_sb[:, k, e * 128:(e + 1) * 128], z_sb[:, k, :])
                             for k in range(DC)]
                    b, tv = mm_group(mlist, full,
                                     pre_waits=[("vd", z_done[tp])])
                    wait("V", "tm", tv)
                    if tp >= 2 and r1_store[tp - 2] is not None:
                        wait("V", "qa%d" % (tp % 2), r1_store[tp - 2])
                    vv = op("V", lambda E, b=b, e=e, p=parp, xp=tp % 3: E.tensor_tensor(
                        out=r1_sb[:, p, e, :], in0=full(b), in1=xb[:, xp, e, :],
                        op=ALU.add), inc="vd")
                    free_bank(b, "vd", vv)
                r1_done[tp] = vv
                xb_free_V[tp] = vv
                dst = r1s.rearrange("(c p) t -> p c t", p=128)[:, :, colp]
                r1_store[tp] = dma("S", "qa%d" % (tp % 2), dst,
                                   r1_sb[:, parp, :, :],
                                   pre=[("vd", r1_done[tp])])
                if tp + 3 < NTA:
                    load_xb(tp + 3)

            if t < NTA:
                if debug and t == 1:
                    wait("V", "qd", dbgA_qs[0])
                wait("V", "sd", silu_done[t])
                for e in range(DC):
                    b, tv = mm_group(
                        [(wcT_sb[:, e * 128:(e + 1) * 128], hb_t[:, t % 2, :])],
                        full, pre_waits=[("vd", hb_vv)])
                    wait("V", "tm", tv)
                    vv = op("V", lambda E, b=b, e=e, p=t % 2: E.tensor_tensor(
                        out=ztmp8[:, e, :], in0=full(b), in1=sg[:, p, e, :],
                        op=ALU.mult), inc="vd")
                    free_bank(b, "vd", vv)
                wait("V", "vd", vv, selfdep=True)
                for e in range(DC):
                    vv = op("V", lambda E, e=e, p=t % 3: E.scalar_tensor_tensor(
                        out=z_sb[:, e, :], in0=xb[:, p, e, :],
                        scalar=vecs_sb[:, 2 + e:3 + e], in1=ztmp8[:, e, :],
                        op0=ALU.mult, op1=ALU.add), inc="vd")
                z_done[t] = vv
                if debug and t == 0:
                    dma("Y", "qd", dbg["xb0"].rearrange("(c p) t -> p c t", p=128),
                        xb[:, 0, :, :], pre=[("vd", z_done[0])])
                    dma("Y", "qd", dbg["w0"][:, :], winT_sb[:, 0, :],
                        pre=[("vd", z_done[0])])
                    dma("Y", "qd", dbg["u0"].rearrange("(c p) t -> p c t", p=128),
                        u_bf[:, 0, :, :], pre=[("sd", u_copy_done)])
                    dma("Y", "qd", dbg["dec0"][:, :], dec_t[:, 0, :],
                        pre=[("vd", scan_vd[0])])
                    dma("Y", "qd", dbg["h0"][:, :], h_t[:, 0, :],
                        pre=[("vd", scan_vd[0])])
                    dma("Y", "qd", dbg["z0"].rearrange("(c p) t -> p c t", p=128),
                        z_sb[:, :, :], pre=[("vd", z_done[0])])
                    dbgA_qs[0] = cnt["qd"]


        endA = [("tm", cnt["tm"]), ("vd", cnt["vd"]), ("sd", cnt["sd"]),
                ("qa0", cnt["qa0"]), ("qa1", cnt["qa1"])]

        wq_loads = []
        first_stage = True

        def stage_quant(dst_ap, src_ap, ncols, si, shape4=False,
                        postmul=1.0 / 1024.0):
            nonlocal first_stage
            pre = list(endA) if first_stage else []
            if len(wq_loads) >= 2:
                pre.append(("vd", wq_loads[-2][1]))
            first_stage = False
            stg = wstage4[si] if shape4 else wstage[si]
            qv = dma("Y", "qt%d" % si, stg[:, 0:ncols] if not shape4 else stg[:, :, :],
                     src_ap, pre=pre)
            wait("V", "qt%d" % si, qv)
            sap = stg[:, :, :] if shape4 else stg[:, 0:ncols]
            v = op("V", lambda E, s=sap: E.tensor_scalar(
                out=s, in0=s, scalar1=64.0, scalar2=MAGIC,
                op0=ALU.mult, op1=ALU.add), inc="vd")
            wait("V", "vd", v, selfdep=True)
            v = op("V", lambda E, s=sap: E.tensor_scalar(
                out=s, in0=s, scalar1=MAGIC, scalar2=127.0,
                op0=ALU.subtract, op1=ALU.min), inc="vd")
            wait("V", "vd", v, selfdep=True)
            vv = op("V", lambda E, s=sap, d=dst_ap, pm=postmul: E.tensor_scalar(
                out=d, in0=s, scalar1=-128.0, scalar2=pm,
                op0=ALU.max, op1=ALU.mult), inc="vd")
            wq_loads.append((qv, vv))

        for k in range(DC):
            stage_quant(w1q[:, k, :],
                        w1T.rearrange("(k p) h -> p k h", p=128)[:, k, :],
                        HID, k % 2)
        for k4 in range(HC // 4):
            stage_quant(w2q[:, 4 * k4:4 * k4 + 4, :],
                        w2T.rearrange("(k p) e -> p k e", p=128)[:, 4 * k4:4 * k4 + 4, :],
                        HID, k4 % 2, shape4=True, postmul=16.0 / 1024.0)
        wq_done = wq_loads[-1][1]

        r1_load = [None] * NTB
        r1ld_free_T = [None] * NTB
        r1ld_free_V = [None] * NTB
        out_store = [None] * NTB
        state = [dict() for _ in range(NTB)]

        def load_r1(tb):
            assert r1_load[tb] is None, tb
            pre = []
            if tb < 3:
                pre += endA + [("vd", wq_loads[-1][1])]
            if tb >= 2:
                pre += [("tm", r1ld_free_T[tb - 2]), ("vd", r1ld_free_V[tb - 2])]
            src = r1s.rearrange("(c p) t -> p c t", p=128)[:, :, tb * TB:(tb + 1) * TB]
            r1_load[tb] = dma("Y", "qy%d" % (tb % 2), r1ld[:, tb % 2, :, :], src, pre=pre)

        load_r1(0)
        load_r1(1)
        prev_htmp_v = [wq_loads[-1][1]]
        prev_htmp_s = [0]

        def emit_sq1(n):
            p = n % 2
            wait("S", "qy%d" % p, r1_load[n])
            sqv = None
            for c in range(DC):
                sqv = op("S", lambda E, c=c, p=p: E.activation(
                    out=sqb1[:, c, :], in_=r1ld[:, p, c, :], func=AF.Square),
                    inc="sd")
            state[n]["sq1_sd"] = sqv

        def rows_and_s(bm, bv, tv_m, tv_v, rows, eps_ap=None, exp_bias=None):
            if eps_ap is None:
                eps_ap = cb_eps
            """V rows chain + S ln/exp; returns (rows_vd, sv, free info)."""
            mr, tr, vr, rr = rows
            wait("V", "tm", tv_m)
            vv = op("V", lambda E, b=bm, mr=mr: E.tensor_scalar(
                out=mr[0:1, :], in0=row_slice(b, TB), scalar1=1.0 / D,
                scalar2=None, op0=ALU.mult, op1=ALU.bypass), inc="vd")
            free_bank(bm, "vd", vv)
            wait("V", "vd", vv, selfdep=True)
            vv = op("V", lambda E, mr=mr, tr=tr: E.tensor_tensor(
                out=tr[0:1, :], in0=mr[0:1, :], in1=mr[0:1, :],
                op=ALU.mult), inc="vd")
            wait("V", "tm", tv_v)
            wait("V", "vd", vv, selfdep=True)
            vv = op("V", lambda E, b=bv, tr=tr, vr=vr: E.scalar_tensor_tensor(
                out=vr[0:1, :], in0=row_slice(b, TB), scalar=1.0 / D,
                in1=tr[0:1, :], op0=ALU.mult, op1=ALU.subtract), inc="vd")
            free_bank(bv, "vd", vv)
            rows_vd = vv
            wait("S", "vd", rows_vd)
            op("S", lambda E, tr=tr, vr=vr, ep=eps_ap: E.activation(
                out=tr[0:1, :], in_=vr[0:1, :], func=AF.Ln, bias=ep[0:1, :]))
            drn("S")
            if exp_bias is None:
                sv = op("S", lambda E, tr=tr, rr=rr: E.activation(
                    out=rr[0:1, :], in_=tr[0:1, :], func=AF.Exp,
                    scale=-0.5), inc="sd")
            else:
                sv = op("S", lambda E, tr=tr, rr=rr, eb=exp_bias: E.activation(
                    out=rr[0:1, :], in_=tr[0:1, :], func=AF.Exp,
                    scale=-0.5, bias=eb[0:1, :]), inc="sd")
            return rows_vd, sv

        def emit_bcasts(rows, rows_vd, sv):
            mr, tr, vr, rr = rows
            bmb = take_rbank()
            wait("T", "vd", rows_vd)
            op("T", lambda E, b=bmb, mr=mr: E.matmul(
                ln_slice(b, TB), ones_row[:], mr[0:1, :], start=True, stop=True),
                inc="tm")
            brs = take_rbank()
            wait("T", "sd", sv)
            tv_rs = op("T", lambda E, b=brs, rr=rr: E.matmul(
                ln_slice(b, TB), ones_row[:], rr[0:1, :], start=True, stop=True),
                inc="tm")
            return bmb, brs, tv_rs

        def emit_apply(src_chunks, dst_chunks, bmb, brs, tv_rs):
            wait("V", "tm", tv_rs)
            for c in range(DC):
                vv1 = op("V", lambda E, c=c, b=bmb, s=src_chunks[c], d=dst_chunks[c]:
                         E.tensor_tensor(out=d, in0=s, in1=ln_slice(b, TB),
                                         op=ALU.subtract), inc="vd")
            free_bank(bmb, "vd", vv1)
            wait("V", "vd", vv1, selfdep=True)
            for c in range(DC):
                vv2 = op("V", lambda E, c=c, b=brs, d=dst_chunks[c]:
                         E.tensor_tensor(out=d, in0=d, in1=ln_slice(b, TB),
                                         op=ALU.mult), inc="vd")
            free_bank(brs, "vd", vv2)
            return vv2

        for it in range(-1, NTB + 1):
            t3, t2, t1 = it - 1, it, it + 1
            do1 = 0 <= t1 < NTB
            do2 = 0 <= t2 < NTB
            do3 = 0 <= t3 < NTB
            if do1:
                p1 = t1 % 2
                wait("T", "qy%d" % p1, r1_load[t1])
                wait("T", "vd", wq_done)
                if "sq1_sd" not in state[t1]:
                    emit_sq1(t1)
                src1 = [r1ld[:, p1, c, :] for c in range(DC)]
                dst1 = [x1_sb[:, p1, c, :] for c in range(DC)]
                ap1_state = {"last": None}

                def ap1_chunk(c, guard=None, _s1=src1, _d1=dst1,
                              _st=ap1_state):
                    bmb1_, brs1_, tvrs1_ = ap1_state["bc"]
                    wait("V", "tm", tvrs1_)
                    if guard:
                        wait("V", "vd", guard, selfdep=True)
                    v1 = op("V", lambda E, c=c, b=bmb1_, s=_s1[c], d=_d1[c]:
                            E.tensor_tensor(out=d, in0=s, in1=ln_slice(b, TB),
                                            op=ALU.subtract), inc="vd")
                    wait("V", "vd", v1, selfdep=True)
                    v2 = op("V", lambda E, c=c, b=brs1_, d=_d1[c]:
                            E.tensor_tensor(out=d, in0=d, in1=ln_slice(b, TB),
                                            op=ALU.mult), inc="vd")
                    _st["last"] = (v1, v2)
                bm1, tvm1 = mm_group([(ones_bf[:], src1[c]) for c in range(DC)],
                                     lambda b: row_slice(b, TB))
                bv1, tvv1 = mm_group([(ones_bf[:], sqb1[:, c, :]) for c in range(DC)],
                                     lambda b: row_slice(b, TB),
                                     pre_waits=[("sd", state[t1]["sq1_sd"])])
                r1ld_free_T[t1] = max(tvm1, tvv1)
                rows1_vd, sv1 = rows_and_s(bm1, bv1, tvm1, tvv1, rows1,
                                           exp_bias=cb_ln16)
            if do3:
                p3 = t3 % 2
                st3 = state[t3]
                r2_vds = []
                sq2_sds = []
                for e in range(DC):
                    mlist = [(w2q[:, k, e * 128:(e + 1) * 128], hq_sb[:, k, :])
                             for k in range(HC)]
                    b, tv = mm_group(mlist, lambda bb: ln_slice(bb, TB),
                                     pre_waits=[("vd", st3["hq_vd"])])
                    wait("V", "tm", tv)
                    r2_vd = op("V", lambda E, b=b, e=e, p=p3: E.scalar_tensor_tensor(
                        out=r2_sb[:, e, :], in0=ln_slice(b, TB),
                        scalar=b2q[:, e:e + 1], in1=x1_sb[:, p, e, :],
                        op0=ALU.add, op1=ALU.add), inc="vd")
                    free_bank(b, "vd", r2_vd)
                    r2_vds.append(r2_vd)
                    wait("S", "vd", r2_vd)
                    sq2_sds.append(op("S", lambda E, e=e: E.activation(
                        out=sqb2[:, e, :], in_=r2_sb[:, e, :], func=AF.Square),
                        inc="sd"))
                    if e == 3 and do1:
                        ap1_state["bc"] = emit_bcasts(rows1, rows1_vd, sv1)
                    if e >= 4 and do1:
                        ap1_chunk(e - 4, guard=r2_vds[e - 4])
                bm2 = take_bank()
                for e in range(DC):
                    wait("T", "vd", r2_vds[e])
                    kw = dict(start=(e == 0), stop=(e == DC - 1))
                    if e == DC - 1:
                        tvm2 = op("T", lambda E, e=e, b=bm2, kw=kw: E.matmul(
                            row_slice(b, TB), ones_bf[:], r2_sb[:, e, :], **kw),
                            inc="tm")
                    else:
                        op("T", lambda E, e=e, b=bm2, kw=kw: E.matmul(
                            row_slice(b, TB), ones_bf[:], r2_sb[:, e, :], **kw))
                bv2 = take_bank()
                for e in range(DC):
                    wait("T", "sd", sq2_sds[e])
                    kw = dict(start=(e == 0), stop=(e == DC - 1))
                    if e == DC - 1:
                        tvv2 = op("T", lambda E, e=e, b=bv2, kw=kw: E.matmul(
                            row_slice(b, TB), ones_bf[:], sqb2[:, e, :], **kw),
                            inc="tm")
                    else:
                        op("T", lambda E, e=e, b=bv2, kw=kw: E.matmul(
                            row_slice(b, TB), ones_bf[:], sqb2[:, e, :], **kw))
            elif do1:
                ap1_state["bc"] = emit_bcasts(rows1, rows1_vd, sv1)
            if do1:
                first = 4 if do3 else 0
                for c in range(first, DC):
                    ap1_chunk(c, guard=(r2_vds[c] if do3 else None))
                bmb1_, brs1_, _ = ap1_state["bc"]
                free_bank(bmb1_, "vd", ap1_state["last"][0])
                free_bank(brs1_, "vd", ap1_state["last"][1])
                apply1_vd = ap1_state["last"][1]
                r1ld_free_V[t1] = apply1_vd
                if debug and t1 == 0:
                    dma("Y", "qd", dbg["x10"].rearrange("(c p) t -> p c t", p=128),
                        x1_sb[:, 0, :, :], pre=[("vd", apply1_vd)])
                if debug and t1 == 1:
                    wait("V", "qd", cnt["qd"])
                wait("V", "vd", apply1_vd, selfdep=True)
                for half in range(2):
                    wait("V", "vd", prev_htmp_v[0], selfdep=True)
                    wait("V", "sd", prev_htmp_s[0])
                    for j in range(4):
                        c = half * 4 + j
                        v = op("V", lambda E, c=c, j=j, p=p1: E.tensor_scalar(
                            out=htmp8b[:, j, :], in0=x1_sb[:, p, c, :],
                            scalar1=MAGIC, scalar2=MAGIC,
                            op0=ALU.add, op1=ALU.subtract), inc="vd")
                    wait("V", "vd", v, selfdep=True)
                    for j in range(4):
                        c = half * 4 + j
                        xq_vd = op("V", lambda E, c=c, j=j, p=p1: E.tensor_scalar(
                            out=xq_sb[:, p, c, :], in0=htmp8b[:, j, :],
                            scalar1=127.0, scalar2=-128.0,
                            op0=ALU.min, op1=ALU.max), inc="vd")
                    prev_htmp_v[0] = xq_vd
                state[t1]["xq_vd"] = xq_vd
                if t1 + 2 < NTB:
                    load_r1(t1 + 2)
            ln2_rows = [None]
            if do3:
                ln2_rows[0] = rows_and_s(bm2, bv2, tvm2, tvv2, rows2,
                                         eps_ap=cb_eps256)

            def emit_ln2_rows():
                rows2_vd, sv2 = ln2_rows[0]
                return emit_bcasts(rows2, rows2_vd, sv2)

            def emit_ln2_apply(bcasts):
                bmb2, brs2, tvrs2 = bcasts
                if t3 >= 1 and out_store[t3 - 1] is not None:
                    wait("V", "qo", out_store[t3 - 1])
                src2 = [r2_sb[:, c, :] for c in range(DC)]
                dst2 = [ob[:, c, :] for c in range(DC)]
                apply2_vd = emit_apply(src2, dst2, bmb2, brs2, tvrs2)
                if debug and t3 == 0:
                    dma("Y", "qd", dbg["hq0"].rearrange("(c p) t -> p c t", p=128),
                        hq_sb[:, :, :], pre=[("vd", st3["hq_vd"])])
                colp = slice(t3 * TB, (t3 + 1) * TB)
                dstap = outT.rearrange("(c p) t -> p c t", p=128)[:, :, colp]
                out_store[t3] = dma("Y", "qo", dstap, ob[:, :, :],
                                    pre=[("vd", apply2_vd)])

            if do3 and not do2:
                emit_ln2_apply(emit_ln2_rows())
            if do2:
                p2 = t2 % 2
                hq_vd = None
                for gsub in range(HC // 4):
                    groups = []
                    for j in range(4):
                        hc = gsub * 4 + j
                        mlist = [(w1q[:, k, hc * 128:(hc + 1) * 128],
                                  xq_sb[:, p2, k, :]) for k in range(DC)]
                        groups.append(mm_group(mlist, lambda bb: ln_slice(bb, TB),
                                               pre_waits=[("vd", state[t2]["xq_vd"])]))
                    wait("V", "vd", prev_htmp_v[0], selfdep=True)
                    wait("V", "sd", prev_htmp_s[0])
                    for j in range(4):
                        b, tv = groups[j]
                        hc = gsub * 4 + j
                        wait("V", "tm", tv)
                        v = op("V", lambda E, b=b, hc=hc, j=j: E.tensor_scalar(
                            out=htmp8b[:, j, :], in0=ln_slice(b, TB),
                            scalar1=b1q16[:, hc:hc + 1], scalar2=16.0,
                            op0=ALU.add, op1=ALU.mult), inc="vd")
                        free_bank(b, "vd", v)
                    wait("V", "vd", v, selfdep=True)
                    for j in range(4):
                        v = op("V", lambda E, j=j: E.tensor_scalar(
                            out=htmp8b[:, j, :], in0=htmp8b[:, j, :],
                            scalar1=MAGIC, scalar2=MAGIC,
                            op0=ALU.add, op1=ALU.subtract), inc="vd")
                    prev_htmp_v[0] = v
                    wait("V", "vd", v, selfdep=True)
                    for j in range(4):
                        hc = gsub * 4 + j
                        v = op("V", lambda E, j=j, hc=hc: E.tensor_scalar(
                            out=hq_sb[:, hc, :], in0=htmp8b[:, j, :],
                            scalar1=127.0, scalar2=0.0,
                            op0=ALU.min, op1=ALU.max), inc="vd")
                    hq_sd = None
                    prev_htmp_v[0] = v
                    if gsub == 6 and do3:
                        ln2_bcasts = emit_ln2_rows()
                state[t2]["hq_vd"] = v
                if do3:
                    emit_ln2_apply(ln2_bcasts)
            if 0 <= it + 2 < NTB and "sq1_sd" not in state[it + 2]:
                emit_sq1(it + 2)

        wait("S", "qo", cnt["qo"], force=True)
        wait("S", "qa0", cnt["qa0"], force=True)
        wait("S", "qa1", cnt["qa1"], force=True)
        for q in ("qy0", "qy1", "qt0", "qt1", "qd"):
            if cnt[q]:
                wait("Y", q, cnt[q], force=True)
        for q in ("qw", "qx0", "qx1", "qx2"):
            wait("G", q, cnt[q], force=True)

        block = en(nc.Block())

        @block.tensor
        def _(E):
            for f in prog["T"]:
                f(E)

        @block.vector
        def _(E):
            for f in prog["V"]:
                f(E)

        @block.scalar
        def _(E):
            for f in prog["S"]:
                f(E)

        @block.gpsimd
        def _(E):
            for f in prog["G"]:
                f(E)

        @block.sync
        def _(E):
            for f in prog["Y"]:
                f(E)

    return nc


_NC_CACHE = {}


def _get_nc():
    if "nc" not in _NC_CACHE:
        _NC_CACHE["nc"] = build_nc()
    return _NC_CACHE["nc"]


def kernel(L0_attn_x, L0_attn_W_in, L0_attn_W_state, L0_attn_W_dt, L0_attn_b_dt,
           L0_attn_A_log, L0_attn_W_C, L0_attn_D, L0_attn_W_out,
           L0_ffn_w1, L0_ffn_b1, L0_ffn_w2, L0_ffn_b2):
    x = np.asarray(L0_attn_x, np.float32)
    B, Lq, Dm = x.shape
    assert (B, Lq, Dm) == (16, L, D)

    winT = np.ascontiguousarray(np.asarray(L0_attn_W_in, np.float32).T)
    wstT = np.ascontiguousarray(np.asarray(L0_attn_W_state, np.float32).T)
    wdtT = np.ascontiguousarray(np.asarray(L0_attn_W_dt, np.float32).T)
    wcT = np.ascontiguousarray(np.asarray(L0_attn_W_C, np.float32).T)
    woutT = np.ascontiguousarray(np.asarray(L0_attn_W_out, np.float32).T)
    w1T = np.ascontiguousarray(np.asarray(L0_ffn_w1, np.float32).T)
    w2T = np.ascontiguousarray(np.asarray(L0_ffn_w2, np.float32).T)

    vecs = np.zeros((128, 64), np.float32)
    vecs[:, 0] = np.asarray(L0_attn_b_dt, np.float32)
    vecs[:, 1] = np.asarray(L0_attn_A_log, np.float32)
    vecs[:, 2:10] = np.asarray(L0_attn_D, np.float32).reshape(8, 128).T
    vecs[:, 10:42] = np.asarray(L0_ffn_b1, np.float32).reshape(32, 128).T
    vecs[:, 42:50] = np.asarray(L0_ffn_b2, np.float32).reshape(8, 128).T

    nc = _get_nc()
    in_maps = []
    for c in range(8):
        xc = np.ascontiguousarray(
            x[2 * c:2 * c + 2].reshape(NTOK, D).T)
        in_maps.append({
            "xT": xc, "winT": winT, "wstT": wstT, "wdtT": wdtT, "wcT": wcT,
            "woutT": woutT, "w1T": w1T, "w2T": w2T, "vecs": vecs,
        })
    res = run_bass_kernel_spmd(nc, in_maps, core_ids=list(range(8)))
    out = np.empty((16, L, D), np.float32)
    for c in range(8):
        o = np.asarray(res.results[c]["outT"], np.float32)
        out[2 * c:2 * c + 2] = o.T.reshape(BLOC, L, D)
    return out


# version 18
# speedup vs baseline: 1.1919x; 1.0213x over previous
import contextlib
import numpy as np
import ml_dtypes
import concourse.bass as bass
import concourse.mybir as mybir
from concourse.bass_utils import run_bass_kernel_spmd

F32 = mybir.dt.float32
BF16 = mybir.dt.bfloat16
AF = mybir.ActivationFunctionType
ALU = mybir.AluOpType
BF = ml_dtypes.bfloat16

D = 1024
DC = 8
E2 = 2048
NST = 128
HID = 4096
HC = 32
BLOC = 2
L = 4096
NTOK = BLOC * L
TA = 512
NTA = NTOK // TA
TB = 256
NTB = NTOK // TB
MAGIC = 12582912.0
EPS = 1e-5
DT_MAX = 10.0
RATE_EPS = 1e-4

ENGS = ("T", "V", "S", "G", "Y")


def build_nc(debug=False):
    nc = bass.Bass("TRN2", target_bir_lowering=False)

    xT = nc.dram_tensor("xT", [D, NTOK], F32, kind="ExternalInput").ap()
    winT = nc.dram_tensor("winT", [D, E2], F32, kind="ExternalInput").ap()
    wstT = nc.dram_tensor("wstT", [D, NST], F32, kind="ExternalInput").ap()
    wdtT = nc.dram_tensor("wdtT", [D, NST], F32, kind="ExternalInput").ap()
    wcT = nc.dram_tensor("wcT", [NST, D], F32, kind="ExternalInput").ap()
    woutT = nc.dram_tensor("woutT", [D, D], F32, kind="ExternalInput").ap()
    w1T = nc.dram_tensor("w1T", [D, HID], F32, kind="ExternalInput").ap()
    w2T = nc.dram_tensor("w2T", [HID, D], F32, kind="ExternalInput").ap()
    vecs = nc.dram_tensor("vecs", [128, 64], F32, kind="ExternalInput").ap()
    outT = nc.dram_tensor("outT", [D, NTOK], F32, kind="ExternalOutput").ap()
    r1s = nc.dram_tensor("r1s", [D, NTOK], BF16, kind="Internal").ap()
    w1qs = nc.dram_tensor("w1qs", [128, DC, HID], BF16, kind="Internal").ap()
    w2qs = nc.dram_tensor("w2qs", [128, HC, D], BF16, kind="Internal").ap()

    dbg = {}
    if debug:
        dbg["u0"] = nc.dram_tensor("dbg_u0", [D, TA], BF16, kind="ExternalOutput").ap()
        dbg["dec0"] = nc.dram_tensor("dbg_dec0", [NST, TA], F32, kind="ExternalOutput").ap()
        dbg["h0"] = nc.dram_tensor("dbg_h0", [NST, TA], F32, kind="ExternalOutput").ap()
        dbg["z0"] = nc.dram_tensor("dbg_z0", [D, TA], BF16, kind="ExternalOutput").ap()
        dbg["x10"] = nc.dram_tensor("dbg_x10", [D, TB], F32, kind="ExternalOutput").ap()
        dbg["hq0"] = nc.dram_tensor("dbg_hq0", [HID, TB], BF16, kind="ExternalOutput").ap()
        dbg["xb0"] = nc.dram_tensor("dbg_xb0", [D, TA], BF16, kind="ExternalOutput").ap()
        dbg["u0e"] = nc.dram_tensor("dbg_u0e", [D, TA], BF16, kind="ExternalOutput").ap()
        dbg["w0"] = nc.dram_tensor("dbg_w0", [128, 2048], BF16, kind="ExternalOutput").ap()

    SB_TOP = 229344
    P_BASE = 16512
    ptr = {"P": P_BASE}
    handles = {}

    def nbytes_pp(shape, dtype):
        n = 1
        for s in shape[1:]:
            n *= s
        return n * (2 if dtype == BF16 else 4)

    def sb(name, shape, dtype, region, at=None):
        if at is None:
            off = (ptr[region] + 31) // 32 * 32
            ptr[region] = off + nbytes_pp(shape, dtype)
        else:
            off = at
        assert off + nbytes_pp(shape, dtype) <= SB_TOP, (name, off)
        h = nc.alloc_sbuf_tensor_at(name, shape, dtype, offset=off)
        handles[name] = (h, off)
        return h

    vecs_sb = sb("vecs_sb", [128, 64], F32, "P")
    negrate = sb("negrate", [128, 1], F32, "P")
    tmpc1 = sb("tmpc1", [128, 1], F32, "P")
    tmpc2 = sb("tmpc2", [128, 1], F32, "P")
    b1q16 = sb("b1q16", [128, 32], F32, "P")
    b2q = sb("b2q", [128, 8], F32, "P")
    ones_bf = sb("ones_bf", [128, 1], BF16, "P")
    cb_one = sb("cb_one", [128, 1], F32, "P")
    cb_eps = sb("cb_eps", [128, 1], F32, "P")
    cb_eps256 = sb("cb_eps256", [128, 1], F32, "P")
    cb_ln16 = sb("cb_ln16", [128, 1], F32, "P")
    ones_row = sb("ones_row", [1, 128], F32, "P")
    rows1 = tuple(sb("row1_%d" % i, [1, 256], F32, "P") for i in range(4))
    rows2 = rows1

    overlay = (ptr["P"] + 31) // 32 * 32
    ptr["A"] = overlay
    ptr["B"] = overlay

    winT_sb = sb("winT_sb", [128, DC, E2], BF16, "A")
    wstT_sb = sb("wstT_sb", [128, DC, NST], BF16, "A")
    wdtT_sb = sb("wdtT_sb", [128, DC, NST], BF16, "A")
    wcT_sb = sb("wcT_sb", [128, D], BF16, "A")
    woutT_sb = sb("woutT_sb", [128, DC, D], BF16, "A")
    xb = sb("xb", [128, 3, DC, TA], BF16, "A")
    u_bf = sb("u_bf", [128, 2, DC, TA], BF16, "A")
    sg = sb("sg", [128, 2, DC, TA], BF16, "A")
    z_sb = sb("z_sb", [128, DC, TA], BF16, "A")
    ztmp8 = sb("ztmp8", [128, DC, TA], F32, "A")
    r1_sb = sb("r1_sb", [128, 2, DC, TA], BF16, "A")
    sp_t = sb("sp_t", [128, TA], F32, "A")
    e1_t = sb("e1_t", [128, TA], F32, "A")
    ld_t = sb("ld_t", [128, TA], F32, "A")
    dec_t = sb("dec_t", [128, 2, TA], F32, "A")
    omd_t = sb("omd_t", [128, TA], F32, "A")
    c_t = sb("c_t", [128, TA], F32, "A")
    h_t = sb("h_t", [128, 2, TA], F32, "A")
    hb_t = sb("hb_t", [128, 2, TA], BF16, "A")

    w1q = sb("w1q", [128, DC, HID], BF16, "B")
    w2q = sb("w2q", [128, HC, D], BF16, "B")
    r1ld = sb("r1ld", [128, 2, DC, TB], BF16, "B")
    x1_sb = sb("x1_sb", [128, 2, DC, TB], F32, "B")
    xq_sb = sb("xq_sb", [128, 2, DC, TB], BF16, "B")
    hq_sb = sb("hq_sb", [128, HC, TB], BF16, "B")
    r2_sb = sb("r2_sb", [128, DC, TB], BF16, "B")
    ob = sb("ob", [128, DC, TB], F32, "B")
    htmp8b = sb("htmp8b", [128, 4, TB], F32, "B")
    sqb1 = sb("sqb1", [128, DC, TB], BF16, "B")
    sqb2 = sb("sqb2", [128, DC, TB], BF16, "B")

    wstg = [sb("wstg_%d" % i, [128, 512], F32, "A",
               at=handles["sqb1" if i < 2 else "sqb2"][1] + (i % 2) * 2048)
            for i in range(4)]
    woutq = [sb("woutq_%d" % i, [128, 512], BF16, "A",
                at=handles["htmp8b"][1] + i * 1024) for i in range(4)]
    assert ptr["A"] <= SB_TOP and ptr["B"] <= SB_TOP

    with contextlib.ExitStack() as st:
        en = st.enter_context
        pb = [en(nc.psum_tensor(f"pb{i}", [128, 512], F32)) for i in range(8)]
        sems = {k: en(nc.semaphore(k)) for k in (
            "qw", "qx0", "qx1", "qx2", "qa0", "qa1", "qy0", "qy1", "qo",
            "qt0", "qt1", "qt2", "qt3", "qws0", "qws1", "qws2", "qws3",
            "qwr", "qw2r", "qd", "tm", "sd", "vd")}

        prog = {e: [] for e in ENGS}
        cnt = {k: 0 for k in sems}
        waited = {e: {} for e in ENGS}
        own_sem = {"T": "tm", "V": "vd", "S": "sd"}

        def emit(eng, fn):
            prog[eng].append(fn)

        def wait(eng, sem, val, force=False, selfdep=False):
            if val is None or val <= 0:
                return
            if not force and not selfdep and own_sem.get(eng) == sem:
                return
            if not force and waited[eng].get(sem, 0) >= val:
                return
            waited[eng][sem] = max(waited[eng].get(sem, 0), val)
            emit(eng, lambda E, s=sems[sem], v=val: E.wait_ge(s, v))

        def drn(eng):
            emit(eng, lambda E: E.drain())

        def op(eng, fn, inc=None, inck=1):
            if inc is None:
                emit(eng, fn)
                return None
            cnt[inc] += inck
            emit(eng, lambda E, f=fn, s=sems[inc], k=inck: f(E).then_inc(s, k))
            return cnt[inc]

        bank_owner = [None] * 8
        bank_rr = [0]
        rbank_rr = [0]

        def take_rbank():
            b = 6 + rbank_rr[0] % 2
            rbank_rr[0] += 1
            if bank_owner[b] is not None:
                assert bank_owner[b][0] != "__taken__"
                wait("T", *bank_owner[b])
            bank_owner[b] = ("__taken__", 0)
            return b

        def take_bank():
            b = bank_rr[0] % 6
            bank_rr[0] += 1
            if bank_owner[b] is not None:
                assert bank_owner[b][0] != "__taken__", f"bank {b} re-taken before consumer recorded"
                wait("T", *bank_owner[b])
            bank_owner[b] = ("__taken__", 0)
            return b

        def free_bank(b, sem, val):
            bank_owner[b] = (sem, val)

        def mm_group(lhs_rhs, out_ap, pre_waits=()):
            b = take_bank()
            for s_name, v in pre_waits:
                wait("T", s_name, v)
            n = len(lhs_rhs)
            for i, (lhs, rhs) in enumerate(lhs_rhs):
                kw = dict(start=(i == 0), stop=(i == n - 1))
                if i == n - 1:
                    tv = op("T", lambda E, l=lhs, r=rhs, o=out_ap(b), kw=kw:
                            E.matmul(o, l, r, **kw), inc="tm")
                else:
                    op("T", lambda E, l=lhs, r=rhs, o=out_ap(b), kw=kw:
                       E.matmul(o, l, r, **kw))
            return b, tv

        def full(b):
            return pb[b][:, :]

        def ln_slice(b, T):
            return pb[b][:, 0:T]

        def row_slice(b, T):
            return pb[b][0:1, 0:T]

        def dma(eng, q, out_ap, in_ap, pre=()):
            for s_name, v in pre:
                wait(eng, s_name, v)
            return op(eng, lambda E, o=out_ap, i=in_ap: E.dma_start(out=o, in_=i),
                      inc=q, inck=16)

        dma("G", "qw", winT_sb[:, :, :], winT.rearrange("(k p) e -> p k e", p=128))
        dma("G", "qw", wstT_sb[:, :, :], wstT.rearrange("(k p) n -> p k n", p=128))
        dma("G", "qw", wdtT_sb[:, :, :], wdtT.rearrange("(k p) n -> p k n", p=128))
        dma("G", "qw", wcT_sb[:, :], wcT[:, :])
        dma("G", "qw", woutT_sb[:, :, :], woutT.rearrange("(k p) e -> p k e", p=128))
        qw_all = dma("G", "qw", vecs_sb[:, :], vecs[:, :])
        qs_vecs = None

        op("V", lambda E: E.memset(ones_bf[:], 1.0))
        op("V", lambda E: E.memset(ones_row[:], 1.0))
        op("V", lambda E: E.memset(cb_one[:], 1.0))
        op("V", lambda E: E.memset(cb_eps[:], EPS))
        op("V", lambda E: E.memset(cb_eps256[:], 256.0 * EPS))
        cv = op("V", lambda E: E.memset(cb_ln16[:], float(np.log(16.0))), inc="vd")
        wait("S", "qw", qw_all)
        wait("S", "vd", cv)
        op("S", lambda E: E.activation(out=tmpc1[:], in_=vecs_sb[:, 1:2], func=AF.Exp))
        drn("S")
        sv = op("S", lambda E: E.activation(out=tmpc2[:], in_=tmpc1[:], func=AF.Ln,
                                            bias=cb_one[:]), inc="sd")
        wait("V", "sd", sv)
        negrate_vd = op("V", lambda E: E.tensor_scalar(out=negrate[:], in0=tmpc2[:],
                                          scalar1=RATE_EPS, scalar2=-1.0,
                                          op0=ALU.add, op1=ALU.mult), inc="vd")
        wait("V", "qw", qw_all)

        def quant_cols(dst, src_ap, postmul):
            v = op("V", lambda E, s=src_ap: E.tensor_scalar(
                out=dst[:], in0=s, scalar1=64.0, scalar2=MAGIC,
                op0=ALU.mult, op1=ALU.add), inc="vd")
            wait("V", "vd", v, selfdep=True)
            v = op("V", lambda E: E.tensor_scalar(
                out=dst[:], in0=dst[:], scalar1=MAGIC, scalar2=127.0,
                op0=ALU.subtract, op1=ALU.min), inc="vd")
            wait("V", "vd", v, selfdep=True)
            v = op("V", lambda E, pm=postmul: E.tensor_scalar(
                out=dst[:], in0=dst[:], scalar1=-128.0, scalar2=pm,
                op0=ALU.max, op1=ALU.mult), inc="vd")
            return v

        quant_cols(b1q16, vecs_sb[:, 10:42], 1.0 / 64.0)
        quant_cols(b2q, vecs_sb[:, 42:50], 16.0 / 64.0)
        op("V", lambda E: E.memset(ztmp8[:, 0, 0:1], 0.0), inc="vd")

        dbgA_qs = [0]
        xb_load = [None] * NTA
        xb_free_T = [None] * NTA
        xb_free_V = [None] * NTA
        r1_done = [None] * NTA
        r1_store = [None] * NTA
        scan_vd = [None] * NTA
        silu_done = [None] * NTA
        z_done = [None] * NTA

        def load_xb(t):
            pre = []
            if t >= 3:
                pre = [("tm", xb_free_T[t - 3]), ("vd", xb_free_V[t - 3])]
            src = xT.rearrange("(c p) t -> p c t", p=128)[:, :, t * TA:(t + 1) * TA]
            xb_load[t] = dma("G", "qx%d" % (t % 3), xb[:, t % 3, :, :], src, pre=pre)

        load_xb(0)
        load_xb(1)
        load_xb(2)

        w1rr = w1T.rearrange("(k p) h -> p k h", p=128)
        w2rr = w2T.rearrange("(k p) e -> p k e", p=128)
        wq_stages = []
        for k in range(DC):
            for c0 in range(0, HID, 512):
                wq_stages.append((w1rr[:, k, c0:c0 + 512],
                                  w1qs[:, k, c0:c0 + 512], 1.0 / 1024.0))
        for k in range(HC):
            for c0 in range(0, D, 512):
                wq_stages.append((w2rr[:, k, c0:c0 + 512],
                                  w2qs[:, k, c0:c0 + 512], 16.0 / 1024.0))
        wqs = {"issued": 0, "done": 0, "qv": [], "vv": [], "oq": []}

        def wq_issue(n):
            for _ in range(n):
                i = wqs["issued"]
                if i >= len(wq_stages):
                    return
                wqs["issued"] += 1
                si = i % 4
                srcap, _, _ = wq_stages[i]
                pre = []
                if i >= 4:
                    pre.append(("vd", wqs["vv"][i - 4]))
                wqs["qv"].append(dma("Y", "qt%d" % si, wstg[si][:, :], srcap,
                                     pre=pre))

        def wq_quant(n):
            for _ in range(n):
                i = wqs["done"]
                if i >= min(wqs["issued"], len(wq_stages)):
                    return
                wqs["done"] += 1
                si = i % 4
                _, dstap, pm = wq_stages[i]
                wait("V", "qt%d" % si, wqs["qv"][i])
                if i >= 4:
                    wait("V", "qws%d" % si, wqs["oq"][i - 4])
                s = wstg[si][:, :]
                v = op("V", lambda E, s=s: E.tensor_scalar(
                    out=s, in0=s, scalar1=64.0, scalar2=MAGIC,
                    op0=ALU.mult, op1=ALU.add), inc="vd")
                wait("V", "vd", v, selfdep=True)
                v = op("V", lambda E, s=s: E.tensor_scalar(
                    out=s, in0=s, scalar1=MAGIC, scalar2=127.0,
                    op0=ALU.subtract, op1=ALU.min), inc="vd")
                wait("V", "vd", v, selfdep=True)
                v = op("V", lambda E, s=s, o=woutq[si][:, :], pm=pm:
                       E.tensor_scalar(out=o, in0=s, scalar1=-128.0, scalar2=pm,
                                       op0=ALU.max, op1=ALU.mult), inc="vd")
                wqs["vv"].append(v)
                wqs["oq"].append(dma("Y", "qws%d" % si, dstap, woutq[si][:, :],
                                     pre=[("vd", v)]))

        wq_issue(4)

        for it in range(NTA + 1):
            t = it
            tp = it - 1
            if t < NTA:
                col = slice(t * TA, (t + 1) * TA)
                par = t % 2
                xpar = t % 3
                if debug and t == 1:
                    for _e in ("T", "S", "V", "G"):
                        wait(_e, "qd", cnt["qd"], force=True)
                if debug and t == 2:
                    for _e in ("T", "S", "V", "G"):
                        wait(_e, "qd", dbgA_qs[0])
                wait("T", "qw", qw_all)
                wait("T", "qx%d" % (t % 3), xb_load[t])
                u_copy_done = None
                for e in range(8):
                    mlist = [(winT_sb[:, k, e * 128:(e + 1) * 128], xb[:, xpar, k, :])
                             for k in range(DC)]
                    b, tv = mm_group(mlist, full)
                    wait("S", "tm", tv)
                    sv = op("S", lambda E, b=b, e=e, p=par: E.copy(
                        out=u_bf[:, p, e, :], in_=full(b)), inc="sd")
                    free_bank(b, "sd", sv)
                    u_copy_done = sv
                if debug and t == 0:
                    dma("Y", "qd", dbg["u0e"].rearrange("(c p) t -> p c t", p=128),
                        u_bf[:, 0, :, :], pre=[("sd", u_copy_done)])
                mlist = [(wstT_sb[:, k, :], u_bf[:, par, k, :]) for k in range(DC)]
                bt_b, bt_tv = mm_group(mlist, full, pre_waits=[("sd", u_copy_done)])
                mlist = [(wdtT_sb[:, k, :], u_bf[:, par, k, :]) for k in range(DC)]
                dt_b, dt_tv = mm_group(mlist, full)
                wait("S", "tm", dt_tv)
                sv = op("S", lambda E, b=dt_b: E.activation(
                    out=e1_t[:], in_=full(b), func=AF.Exp,
                    bias=vecs_sb[:, 0:1]), inc="sd")
                free_bank(dt_b, "sd", sv)
                drn("S")
                sv = op("S", lambda E: E.activation(
                    out=sp_t[:], in_=e1_t[:], func=AF.Ln, bias=cb_one[:]), inc="sd")
                wait("V", "sd", sv)
                wait("V", "vd", negrate_vd, selfdep=True)
                vv = op("V", lambda E: E.tensor_scalar(
                    out=ld_t[:], in0=sp_t[:], scalar1=DT_MAX, scalar2=negrate[:],
                    op0=ALU.min, op1=ALU.mult), inc="vd")
                wait("S", "vd", vv)
                sv = op("S", lambda E, p=par: E.activation(
                    out=dec_t[:, p, :], in_=ld_t[:], func=AF.Exp), inc="sd")
                wait("V", "sd", sv)
                omd_vv = op("V", lambda E, p=par: E.tensor_scalar(
                    out=omd_t[:], in0=dec_t[:, p, :], scalar1=1.0, scalar2=-1.0,
                    op0=ALU.subtract, op1=ALU.mult), inc="vd")
                wait("V", "tm", bt_tv)
                wait("V", "vd", omd_vv, selfdep=True)
                c_vv = op("V", lambda E, b=bt_b: E.tensor_tensor(
                    out=c_t[:], in0=omd_t[:], in1=full(b), op=ALU.mult), inc="vd")
                free_bank(bt_b, "vd", c_vv)
                wait("V", "vd", c_vv, selfdep=True)
                if t % (L // TA) == 0:
                    init = 0.0
                else:
                    init = h_t[:, (t - 1) % 2, TA - 1:TA]
                scan_vd[t] = op("V", lambda E, p=par, i=init: E.tensor_tensor_scan(
                    out=h_t[:, p, :], data0=dec_t[:, p, :], data1=c_t[:],
                    initial=i, op0=ALU.mult, op1=ALU.add), inc="vd")
                wait("V", "vd", scan_vd[t], selfdep=True)
                hb_vv = op("V", lambda E, p=par: E.tensor_copy(
                    out=hb_t[:, p, :], in_=h_t[:, p, :]), inc="vd")
                for e in range(8):
                    mlist = [(winT_sb[:, k, (8 + e) * 128:(9 + e) * 128],
                              xb[:, xpar, k, :]) for k in range(DC)]
                    gb, gtv = mm_group(mlist, full)
                    wait("S", "tm", gtv)
                    ssv = op("S", lambda E, b=gb, e=e, p=par: E.activation(
                        out=sg[:, p, e, :], in_=full(b), func=AF.Silu), inc="sd")
                    free_bank(gb, "sd", ssv)
                xb_free_T[t] = gtv
                silu_done[t] = ssv

            if tp >= 0:
                colp = slice(tp * TA, (tp + 1) * TA)
                parp = tp % 2
                for e in range(DC):
                    mlist = [(woutT_sb[:, k, e * 128:(e + 1) * 128], z_sb[:, k, :])
                             for k in range(DC)]
                    b, tv = mm_group(mlist, full,
                                     pre_waits=[("vd", z_done[tp])])
                    wait("V", "tm", tv)
                    if tp >= 2 and r1_store[tp - 2] is not None:
                        wait("V", "qa%d" % (tp % 2), r1_store[tp - 2])
                    vv = op("V", lambda E, b=b, e=e, p=parp, xp=tp % 3: E.tensor_tensor(
                        out=r1_sb[:, p, e, :], in0=full(b), in1=xb[:, xp, e, :],
                        op=ALU.add), inc="vd")
                    free_bank(b, "vd", vv)
                r1_done[tp] = vv
                xb_free_V[tp] = vv
                dst = r1s.rearrange("(c p) t -> p c t", p=128)[:, :, colp]
                r1_store[tp] = dma("S", "qa%d" % (tp % 2), dst,
                                   r1_sb[:, parp, :, :],
                                   pre=[("vd", r1_done[tp])])
                if tp + 3 < NTA:
                    load_xb(tp + 3)

            if t < NTA:
                if debug and t == 1:
                    wait("V", "qd", dbgA_qs[0])
                wait("V", "sd", silu_done[t])
                for e in range(DC):
                    b, tv = mm_group(
                        [(wcT_sb[:, e * 128:(e + 1) * 128], hb_t[:, t % 2, :])],
                        full, pre_waits=[("vd", hb_vv)])
                    wait("V", "tm", tv)
                    vv = op("V", lambda E, b=b, e=e, p=t % 2: E.tensor_tensor(
                        out=ztmp8[:, e, :], in0=full(b), in1=sg[:, p, e, :],
                        op=ALU.mult), inc="vd")
                    free_bank(b, "vd", vv)
                wait("V", "vd", vv, selfdep=True)
                for e in range(DC):
                    vv = op("V", lambda E, e=e, p=t % 3: E.scalar_tensor_tensor(
                        out=z_sb[:, e, :], in0=xb[:, p, e, :],
                        scalar=vecs_sb[:, 2 + e:3 + e], in1=ztmp8[:, e, :],
                        op0=ALU.mult, op1=ALU.add), inc="vd")
                z_done[t] = vv
                for _wq in range(8):
                    wq_quant(1)
                    wq_issue(1)
                if debug and t == 0:
                    dma("Y", "qd", dbg["xb0"].rearrange("(c p) t -> p c t", p=128),
                        xb[:, 0, :, :], pre=[("vd", z_done[0])])
                    dma("Y", "qd", dbg["w0"][:, :], winT_sb[:, 0, :],
                        pre=[("vd", z_done[0])])
                    dma("Y", "qd", dbg["u0"].rearrange("(c p) t -> p c t", p=128),
                        u_bf[:, 0, :, :], pre=[("sd", u_copy_done)])
                    dma("Y", "qd", dbg["dec0"][:, :], dec_t[:, 0, :],
                        pre=[("vd", scan_vd[0])])
                    dma("Y", "qd", dbg["h0"][:, :], h_t[:, 0, :],
                        pre=[("vd", scan_vd[0])])
                    dma("Y", "qd", dbg["z0"].rearrange("(c p) t -> p c t", p=128),
                        z_sb[:, :, :], pre=[("vd", z_done[0])])
                    dbgA_qs[0] = cnt["qd"]


        while wqs["done"] < len(wq_stages):
            wq_quant(1)
            wq_issue(1)
        endA = [("tm", cnt["tm"]), ("vd", cnt["vd"]), ("sd", cnt["sd"]),
                ("qa0", cnt["qa0"]), ("qa1", cnt["qa1"])]
        spill_done = [("qws%d" % i, cnt["qws%d" % i]) for i in range(4)]
        w1_rl = dma("G", "qwr", w1q[:, :, :], w1qs[:, :, :],
                    pre=endA + spill_done)
        w2_rl = dma("Y", "qw2r", w2q[:, :, :], w2qs[:, :, :],
                    pre=endA + spill_done)

        r1_load = [None] * NTB
        r1ld_free_T = [None] * NTB
        r1ld_free_V = [None] * NTB
        out_store = [None] * NTB
        state = [dict() for _ in range(NTB)]

        def load_r1(tb):
            assert r1_load[tb] is None, tb
            pre = []
            if tb < 3:
                pre += endA
            if tb >= 2:
                pre += [("tm", r1ld_free_T[tb - 2]), ("vd", r1ld_free_V[tb - 2])]
            src = r1s.rearrange("(c p) t -> p c t", p=128)[:, :, tb * TB:(tb + 1) * TB]
            r1_load[tb] = dma("Y", "qy%d" % (tb % 2), r1ld[:, tb % 2, :, :], src, pre=pre)

        load_r1(0)
        load_r1(1)
        for _i in range(4):
            wait("V", "qws%d" % _i, cnt["qws%d" % _i])
        prev_htmp_v = [cnt["vd"]]
        prev_htmp_s = [0]

        def emit_sq1(n):
            p = n % 2
            wait("S", "qy%d" % p, r1_load[n])
            sqv = None
            for c in range(DC):
                sqv = op("S", lambda E, c=c, p=p: E.activation(
                    out=sqb1[:, c, :], in_=r1ld[:, p, c, :], func=AF.Square),
                    inc="sd")
            state[n]["sq1_sd"] = sqv

        def rows_and_s(bm, bv, tv_m, tv_v, rows, eps_ap=None, exp_bias=None):
            if eps_ap is None:
                eps_ap = cb_eps
            """V rows chain + S ln/exp; returns (rows_vd, sv, free info)."""
            mr, tr, vr, rr = rows
            wait("V", "tm", tv_m)
            vv = op("V", lambda E, b=bm, mr=mr: E.tensor_scalar(
                out=mr[0:1, :], in0=row_slice(b, TB), scalar1=1.0 / D,
                scalar2=None, op0=ALU.mult, op1=ALU.bypass), inc="vd")
            free_bank(bm, "vd", vv)
            wait("V", "vd", vv, selfdep=True)
            vv = op("V", lambda E, mr=mr, tr=tr: E.tensor_tensor(
                out=tr[0:1, :], in0=mr[0:1, :], in1=mr[0:1, :],
                op=ALU.mult), inc="vd")
            wait("V", "tm", tv_v)
            wait("V", "vd", vv, selfdep=True)
            vv = op("V", lambda E, b=bv, tr=tr, vr=vr: E.scalar_tensor_tensor(
                out=vr[0:1, :], in0=row_slice(b, TB), scalar=1.0 / D,
                in1=tr[0:1, :], op0=ALU.mult, op1=ALU.subtract), inc="vd")
            free_bank(bv, "vd", vv)
            rows_vd = vv
            wait("S", "vd", rows_vd)
            op("S", lambda E, tr=tr, vr=vr, ep=eps_ap: E.activation(
                out=tr[0:1, :], in_=vr[0:1, :], func=AF.Ln, bias=ep[0:1, :]))
            drn("S")
            if exp_bias is None:
                sv = op("S", lambda E, tr=tr, rr=rr: E.activation(
                    out=rr[0:1, :], in_=tr[0:1, :], func=AF.Exp,
                    scale=-0.5), inc="sd")
            else:
                sv = op("S", lambda E, tr=tr, rr=rr, eb=exp_bias: E.activation(
                    out=rr[0:1, :], in_=tr[0:1, :], func=AF.Exp,
                    scale=-0.5, bias=eb[0:1, :]), inc="sd")
            return rows_vd, sv

        def emit_bcasts(rows, rows_vd, sv):
            mr, tr, vr, rr = rows
            bmb = take_rbank()
            wait("T", "vd", rows_vd)
            op("T", lambda E, b=bmb, mr=mr: E.matmul(
                ln_slice(b, TB), ones_row[:], mr[0:1, :], start=True, stop=True),
                inc="tm")
            brs = take_rbank()
            wait("T", "sd", sv)
            tv_rs = op("T", lambda E, b=brs, rr=rr: E.matmul(
                ln_slice(b, TB), ones_row[:], rr[0:1, :], start=True, stop=True),
                inc="tm")
            return bmb, brs, tv_rs

        def emit_apply(src_chunks, dst_chunks, bmb, brs, tv_rs):
            wait("V", "tm", tv_rs)
            for c in range(DC):
                vv1 = op("V", lambda E, c=c, b=bmb, s=src_chunks[c], d=dst_chunks[c]:
                         E.tensor_tensor(out=d, in0=s, in1=ln_slice(b, TB),
                                         op=ALU.subtract), inc="vd")
            free_bank(bmb, "vd", vv1)
            wait("V", "vd", vv1, selfdep=True)
            for c in range(DC):
                vv2 = op("V", lambda E, c=c, b=brs, d=dst_chunks[c]:
                         E.tensor_tensor(out=d, in0=d, in1=ln_slice(b, TB),
                                         op=ALU.mult), inc="vd")
            free_bank(brs, "vd", vv2)
            return vv2

        for it in range(-1, NTB + 1):
            t3, t2, t1 = it - 1, it, it + 1
            do1 = 0 <= t1 < NTB
            do2 = 0 <= t2 < NTB
            do3 = 0 <= t3 < NTB
            if do1:
                p1 = t1 % 2
                wait("T", "qy%d" % p1, r1_load[t1])
                wait("T", "qwr", w1_rl)
                wait("T", "qw2r", w2_rl)
                if "sq1_sd" not in state[t1]:
                    emit_sq1(t1)
                src1 = [r1ld[:, p1, c, :] for c in range(DC)]
                dst1 = [x1_sb[:, p1, c, :] for c in range(DC)]
                ap1_state = {"last": None}

                def ap1_chunk(c, guard=None, _s1=src1, _d1=dst1,
                              _st=ap1_state):
                    bmb1_, brs1_, tvrs1_ = ap1_state["bc"]
                    wait("V", "tm", tvrs1_)
                    if guard:
                        wait("V", "vd", guard, selfdep=True)
                    v1 = op("V", lambda E, c=c, b=bmb1_, s=_s1[c], d=_d1[c]:
                            E.tensor_tensor(out=d, in0=s, in1=ln_slice(b, TB),
                                            op=ALU.subtract), inc="vd")
                    wait("V", "vd", v1, selfdep=True)
                    v2 = op("V", lambda E, c=c, b=brs1_, d=_d1[c]:
                            E.tensor_tensor(out=d, in0=d, in1=ln_slice(b, TB),
                                            op=ALU.mult), inc="vd")
                    _st["last"] = (v1, v2)
                bm1, tvm1 = mm_group([(ones_bf[:], src1[c]) for c in range(DC)],
                                     lambda b: row_slice(b, TB))
                bv1, tvv1 = mm_group([(ones_bf[:], sqb1[:, c, :]) for c in range(DC)],
                                     lambda b: row_slice(b, TB),
                                     pre_waits=[("sd", state[t1]["sq1_sd"])])
                r1ld_free_T[t1] = max(tvm1, tvv1)
                rows1_vd, sv1 = rows_and_s(bm1, bv1, tvm1, tvv1, rows1,
                                           exp_bias=cb_ln16)
            if do3:
                p3 = t3 % 2
                st3 = state[t3]
                r2_vds = []
                sq2_sds = []
                for e in range(DC):
                    mlist = [(w2q[:, k, e * 128:(e + 1) * 128], hq_sb[:, k, :])
                             for k in range(HC)]
                    b, tv = mm_group(mlist, lambda bb: ln_slice(bb, TB),
                                     pre_waits=[("vd", st3["hq_vd"])])
                    wait("V", "tm", tv)
                    r2_vd = op("V", lambda E, b=b, e=e, p=p3: E.scalar_tensor_tensor(
                        out=r2_sb[:, e, :], in0=ln_slice(b, TB),
                        scalar=b2q[:, e:e + 1], in1=x1_sb[:, p, e, :],
                        op0=ALU.add, op1=ALU.add), inc="vd")
                    free_bank(b, "vd", r2_vd)
                    r2_vds.append(r2_vd)
                    wait("S", "vd", r2_vd)
                    sq2_sds.append(op("S", lambda E, e=e: E.activation(
                        out=sqb2[:, e, :], in_=r2_sb[:, e, :], func=AF.Square),
                        inc="sd"))
                    if e == 3 and do1:
                        ap1_state["bc"] = emit_bcasts(rows1, rows1_vd, sv1)
                    if e >= 4 and do1:
                        ap1_chunk(e - 4, guard=r2_vds[e - 4])
                bm2 = take_bank()
                for e in range(DC):
                    wait("T", "vd", r2_vds[e])
                    kw = dict(start=(e == 0), stop=(e == DC - 1))
                    if e == DC - 1:
                        tvm2 = op("T", lambda E, e=e, b=bm2, kw=kw: E.matmul(
                            row_slice(b, TB), ones_bf[:], r2_sb[:, e, :], **kw),
                            inc="tm")
                    else:
                        op("T", lambda E, e=e, b=bm2, kw=kw: E.matmul(
                            row_slice(b, TB), ones_bf[:], r2_sb[:, e, :], **kw))
                bv2 = take_bank()
                for e in range(DC):
                    wait("T", "sd", sq2_sds[e])
                    kw = dict(start=(e == 0), stop=(e == DC - 1))
                    if e == DC - 1:
                        tvv2 = op("T", lambda E, e=e, b=bv2, kw=kw: E.matmul(
                            row_slice(b, TB), ones_bf[:], sqb2[:, e, :], **kw),
                            inc="tm")
                    else:
                        op("T", lambda E, e=e, b=bv2, kw=kw: E.matmul(
                            row_slice(b, TB), ones_bf[:], sqb2[:, e, :], **kw))
            elif do1:
                ap1_state["bc"] = emit_bcasts(rows1, rows1_vd, sv1)
            if do1:
                first = 4 if do3 else 0
                for c in range(first, DC):
                    ap1_chunk(c, guard=(r2_vds[c] if do3 else None))
                bmb1_, brs1_, _ = ap1_state["bc"]
                free_bank(bmb1_, "vd", ap1_state["last"][0])
                free_bank(brs1_, "vd", ap1_state["last"][1])
                apply1_vd = ap1_state["last"][1]
                r1ld_free_V[t1] = apply1_vd
                if debug and t1 == 0:
                    dma("Y", "qd", dbg["x10"].rearrange("(c p) t -> p c t", p=128),
                        x1_sb[:, 0, :, :], pre=[("vd", apply1_vd)])
                if debug and t1 == 1:
                    wait("V", "qd", cnt["qd"])
                wait("V", "vd", apply1_vd, selfdep=True)
                for half in range(2):
                    wait("V", "vd", prev_htmp_v[0], selfdep=True)
                    wait("V", "sd", prev_htmp_s[0])
                    for j in range(4):
                        c = half * 4 + j
                        v = op("V", lambda E, c=c, j=j, p=p1: E.tensor_scalar(
                            out=htmp8b[:, j, :], in0=x1_sb[:, p, c, :],
                            scalar1=MAGIC, scalar2=MAGIC,
                            op0=ALU.add, op1=ALU.subtract), inc="vd")
                    wait("V", "vd", v, selfdep=True)
                    for j in range(4):
                        c = half * 4 + j
                        xq_vd = op("V", lambda E, c=c, j=j, p=p1: E.tensor_scalar(
                            out=xq_sb[:, p, c, :], in0=htmp8b[:, j, :],
                            scalar1=127.0, scalar2=-128.0,
                            op0=ALU.min, op1=ALU.max), inc="vd")
                    prev_htmp_v[0] = xq_vd
                state[t1]["xq_vd"] = xq_vd
                if t1 + 2 < NTB:
                    load_r1(t1 + 2)
            ln2_rows = [None]
            if do3:
                ln2_rows[0] = rows_and_s(bm2, bv2, tvm2, tvv2, rows2,
                                         eps_ap=cb_eps256)

            def emit_ln2_rows():
                rows2_vd, sv2 = ln2_rows[0]
                return emit_bcasts(rows2, rows2_vd, sv2)

            def emit_ln2_apply(bcasts):
                bmb2, brs2, tvrs2 = bcasts
                if t3 >= 1 and out_store[t3 - 1] is not None:
                    wait("V", "qo", out_store[t3 - 1])
                src2 = [r2_sb[:, c, :] for c in range(DC)]
                dst2 = [ob[:, c, :] for c in range(DC)]
                apply2_vd = emit_apply(src2, dst2, bmb2, brs2, tvrs2)
                if debug and t3 == 0:
                    dma("Y", "qd", dbg["hq0"].rearrange("(c p) t -> p c t", p=128),
                        hq_sb[:, :, :], pre=[("vd", st3["hq_vd"])])
                colp = slice(t3 * TB, (t3 + 1) * TB)
                dstap = outT.rearrange("(c p) t -> p c t", p=128)[:, :, colp]
                out_store[t3] = dma("Y", "qo", dstap, ob[:, :, :],
                                    pre=[("vd", apply2_vd)])

            if do3 and not do2:
                emit_ln2_apply(emit_ln2_rows())
            if do2:
                p2 = t2 % 2
                hq_vd = None
                for gsub in range(HC // 4):
                    groups = []
                    for j in range(4):
                        hc = gsub * 4 + j
                        mlist = [(w1q[:, k, hc * 128:(hc + 1) * 128],
                                  xq_sb[:, p2, k, :]) for k in range(DC)]
                        groups.append(mm_group(mlist, lambda bb: ln_slice(bb, TB),
                                               pre_waits=[("vd", state[t2]["xq_vd"])]))
                    wait("V", "vd", prev_htmp_v[0], selfdep=True)
                    wait("V", "sd", prev_htmp_s[0])
                    for j in range(4):
                        b, tv = groups[j]
                        hc = gsub * 4 + j
                        wait("V", "tm", tv)
                        v = op("V", lambda E, b=b, hc=hc, j=j: E.tensor_scalar(
                            out=htmp8b[:, j, :], in0=ln_slice(b, TB),
                            scalar1=b1q16[:, hc:hc + 1], scalar2=16.0,
                            op0=ALU.add, op1=ALU.mult), inc="vd")
                        free_bank(b, "vd", v)
                    wait("V", "vd", v, selfdep=True)
                    for j in range(4):
                        v = op("V", lambda E, j=j: E.tensor_scalar(
                            out=htmp8b[:, j, :], in0=htmp8b[:, j, :],
                            scalar1=MAGIC, scalar2=MAGIC,
                            op0=ALU.add, op1=ALU.subtract), inc="vd")
                    prev_htmp_v[0] = v
                    wait("V", "vd", v, selfdep=True)
                    for j in range(4):
                        hc = gsub * 4 + j
                        v = op("V", lambda E, j=j, hc=hc: E.tensor_scalar(
                            out=hq_sb[:, hc, :], in0=htmp8b[:, j, :],
                            scalar1=127.0, scalar2=0.0,
                            op0=ALU.min, op1=ALU.max), inc="vd")
                    hq_sd = None
                    prev_htmp_v[0] = v
                    if gsub == 6 and do3:
                        ln2_bcasts = emit_ln2_rows()
                state[t2]["hq_vd"] = v
                if do3:
                    emit_ln2_apply(ln2_bcasts)
            if 0 <= it + 2 < NTB and "sq1_sd" not in state[it + 2]:
                emit_sq1(it + 2)

        wait("S", "qo", cnt["qo"], force=True)
        wait("S", "qa0", cnt["qa0"], force=True)
        wait("S", "qa1", cnt["qa1"], force=True)
        for q in ("qy0", "qy1", "qt0", "qt1", "qd"):
            if cnt[q]:
                wait("Y", q, cnt[q], force=True)
        for q in ("qw", "qx0", "qx1", "qx2"):
            wait("G", q, cnt[q], force=True)

        block = en(nc.Block())

        @block.tensor
        def _(E):
            for f in prog["T"]:
                f(E)

        @block.vector
        def _(E):
            for f in prog["V"]:
                f(E)

        @block.scalar
        def _(E):
            for f in prog["S"]:
                f(E)

        @block.gpsimd
        def _(E):
            for f in prog["G"]:
                f(E)

        @block.sync
        def _(E):
            for f in prog["Y"]:
                f(E)

    return nc


_NC_CACHE = {}


def _get_nc():
    if "nc" not in _NC_CACHE:
        _NC_CACHE["nc"] = build_nc()
    return _NC_CACHE["nc"]


def kernel(L0_attn_x, L0_attn_W_in, L0_attn_W_state, L0_attn_W_dt, L0_attn_b_dt,
           L0_attn_A_log, L0_attn_W_C, L0_attn_D, L0_attn_W_out,
           L0_ffn_w1, L0_ffn_b1, L0_ffn_w2, L0_ffn_b2):
    x = np.asarray(L0_attn_x, np.float32)
    B, Lq, Dm = x.shape
    assert (B, Lq, Dm) == (16, L, D)

    winT = np.ascontiguousarray(np.asarray(L0_attn_W_in, np.float32).T)
    wstT = np.ascontiguousarray(np.asarray(L0_attn_W_state, np.float32).T)
    wdtT = np.ascontiguousarray(np.asarray(L0_attn_W_dt, np.float32).T)
    wcT = np.ascontiguousarray(np.asarray(L0_attn_W_C, np.float32).T)
    woutT = np.ascontiguousarray(np.asarray(L0_attn_W_out, np.float32).T)
    w1T = np.ascontiguousarray(np.asarray(L0_ffn_w1, np.float32).T)
    w2T = np.ascontiguousarray(np.asarray(L0_ffn_w2, np.float32).T)

    vecs = np.zeros((128, 64), np.float32)
    vecs[:, 0] = np.asarray(L0_attn_b_dt, np.float32)
    vecs[:, 1] = np.asarray(L0_attn_A_log, np.float32)
    vecs[:, 2:10] = np.asarray(L0_attn_D, np.float32).reshape(8, 128).T
    vecs[:, 10:42] = np.asarray(L0_ffn_b1, np.float32).reshape(32, 128).T
    vecs[:, 42:50] = np.asarray(L0_ffn_b2, np.float32).reshape(8, 128).T

    nc = _get_nc()
    in_maps = []
    for c in range(8):
        xc = np.ascontiguousarray(
            x[2 * c:2 * c + 2].reshape(NTOK, D).T)
        in_maps.append({
            "xT": xc, "winT": winT, "wstT": wstT, "wdtT": wdtT, "wcT": wcT,
            "woutT": woutT, "w1T": w1T, "w2T": w2T, "vecs": vecs,
        })
    res = run_bass_kernel_spmd(nc, in_maps, core_ids=list(range(8)))
    out = np.empty((16, L, D), np.float32)
    for c in range(8):
        o = np.asarray(res.results[c]["outT"], np.float32)
        out[2 * c:2 * c + 2] = o.T.reshape(BLOC, L, D)
    return out


# version 19
# speedup vs baseline: 1.1949x; 1.0025x over previous
import contextlib
import numpy as np
import ml_dtypes
import concourse.bass as bass
import concourse.mybir as mybir
from concourse.bass_utils import run_bass_kernel_spmd

F32 = mybir.dt.float32
BF16 = mybir.dt.bfloat16
AF = mybir.ActivationFunctionType
ALU = mybir.AluOpType
BF = ml_dtypes.bfloat16

D = 1024
DC = 8
E2 = 2048
NST = 128
HID = 4096
HC = 32
BLOC = 2
L = 4096
NTOK = BLOC * L
TA = 512
NTA = NTOK // TA
TB = 256
NTB = NTOK // TB
MAGIC = 12582912.0
EPS = 1e-5
DT_MAX = 10.0
RATE_EPS = 1e-4

ENGS = ("T", "V", "S", "G", "Y")


def build_nc(debug=False):
    nc = bass.Bass("TRN2", target_bir_lowering=False)

    xT = nc.dram_tensor("xT", [D, NTOK], F32, kind="ExternalInput").ap()
    winT = nc.dram_tensor("winT", [D, E2], F32, kind="ExternalInput").ap()
    wstT = nc.dram_tensor("wstT", [D, NST], F32, kind="ExternalInput").ap()
    wdtT = nc.dram_tensor("wdtT", [D, NST], F32, kind="ExternalInput").ap()
    wcT = nc.dram_tensor("wcT", [NST, D], F32, kind="ExternalInput").ap()
    woutT = nc.dram_tensor("woutT", [D, D], F32, kind="ExternalInput").ap()
    w1T = nc.dram_tensor("w1T", [D, HID], F32, kind="ExternalInput").ap()
    w2T = nc.dram_tensor("w2T", [HID, D], F32, kind="ExternalInput").ap()
    vecs = nc.dram_tensor("vecs", [128, 64], F32, kind="ExternalInput").ap()
    outT = nc.dram_tensor("outT", [D, NTOK], F32, kind="ExternalOutput").ap()
    r1s = nc.dram_tensor("r1s", [D, NTOK], BF16, kind="Internal").ap()
    w1qs = nc.dram_tensor("w1qs", [128, DC, HID], BF16, kind="Internal").ap()
    w2qs = nc.dram_tensor("w2qs", [128, HC, D], BF16, kind="Internal").ap()

    dbg = {}
    if debug:
        dbg["u0"] = nc.dram_tensor("dbg_u0", [D, TA], BF16, kind="ExternalOutput").ap()
        dbg["dec0"] = nc.dram_tensor("dbg_dec0", [NST, TA], F32, kind="ExternalOutput").ap()
        dbg["h0"] = nc.dram_tensor("dbg_h0", [NST, TA], F32, kind="ExternalOutput").ap()
        dbg["z0"] = nc.dram_tensor("dbg_z0", [D, TA], BF16, kind="ExternalOutput").ap()
        dbg["x10"] = nc.dram_tensor("dbg_x10", [D, TB], F32, kind="ExternalOutput").ap()
        dbg["hq0"] = nc.dram_tensor("dbg_hq0", [HID, TB], BF16, kind="ExternalOutput").ap()
        dbg["xb0"] = nc.dram_tensor("dbg_xb0", [D, TA], BF16, kind="ExternalOutput").ap()
        dbg["u0e"] = nc.dram_tensor("dbg_u0e", [D, TA], BF16, kind="ExternalOutput").ap()
        dbg["w0"] = nc.dram_tensor("dbg_w0", [128, 2048], BF16, kind="ExternalOutput").ap()

    SB_TOP = 229344
    P_BASE = 16512
    ptr = {"P": P_BASE}
    handles = {}

    def nbytes_pp(shape, dtype):
        n = 1
        for s in shape[1:]:
            n *= s
        return n * (2 if dtype == BF16 else 4)

    def sb(name, shape, dtype, region, at=None):
        if at is None:
            off = (ptr[region] + 31) // 32 * 32
            ptr[region] = off + nbytes_pp(shape, dtype)
        else:
            off = at
        assert off + nbytes_pp(shape, dtype) <= SB_TOP, (name, off)
        h = nc.alloc_sbuf_tensor_at(name, shape, dtype, offset=off)
        handles[name] = (h, off)
        return h

    vecs_sb = sb("vecs_sb", [128, 64], F32, "P")
    negrate = sb("negrate", [128, 1], F32, "P")
    tmpc1 = sb("tmpc1", [128, 1], F32, "P")
    tmpc2 = sb("tmpc2", [128, 1], F32, "P")
    b1q16 = sb("b1q16", [128, 32], F32, "P")
    b2q = sb("b2q", [128, 8], F32, "P")
    ones_bf = sb("ones_bf", [128, 1], BF16, "P")
    cb_one = sb("cb_one", [128, 1], F32, "P")
    cb_eps = sb("cb_eps", [128, 1], F32, "P")
    cb_eps256 = sb("cb_eps256", [128, 1], F32, "P")
    cb_ln16 = sb("cb_ln16", [128, 1], F32, "P")
    ones_row = sb("ones_row", [1, 128], F32, "P")
    rows1 = tuple(sb("row1_%d" % i, [1, 256], F32, "P") for i in range(4))
    rows2 = rows1

    overlay = (ptr["P"] + 31) // 32 * 32
    ptr["A"] = overlay
    ptr["B"] = overlay

    winT_sb = sb("winT_sb", [128, DC, E2], BF16, "A")
    wstT_sb = sb("wstT_sb", [128, DC, NST], BF16, "A")
    wdtT_sb = sb("wdtT_sb", [128, DC, NST], BF16, "A")
    wcT_sb = sb("wcT_sb", [128, D], BF16, "A")
    woutT_sb = sb("woutT_sb", [128, DC, D], BF16, "A")
    xb = sb("xb", [128, 3, DC, TA], BF16, "A")
    u_bf = sb("u_bf", [128, 2, DC, TA], BF16, "A")
    sg = sb("sg", [128, 2, DC, TA], BF16, "A")
    z_sb = sb("z_sb", [128, DC, TA], BF16, "A")
    ztmp8 = sb("ztmp8", [128, DC, TA], F32, "A")
    r1_sb = sb("r1_sb", [128, 2, DC, TA], BF16, "A")
    sp_t = sb("sp_t", [128, TA], F32, "A")
    e1_t = sb("e1_t", [128, TA], F32, "A")
    ld_t = sb("ld_t", [128, TA], F32, "A")
    dec_t = sb("dec_t", [128, 2, TA], F32, "A")
    omd_t = sb("omd_t", [128, TA], F32, "A")
    c_t = sb("c_t", [128, TA], F32, "A")
    h_t = sb("h_t", [128, 2, TA], F32, "A")
    hb_t = sb("hb_t", [128, 2, TA], BF16, "A")

    w1q = sb("w1q", [128, DC, HID], BF16, "B")
    w2q = sb("w2q", [128, HC, D], BF16, "B")
    r1ld = sb("r1ld", [128, 2, DC, TB], BF16, "B")
    x1_sb = sb("x1_sb", [128, 2, DC, TB], F32, "B")
    xq_sb = sb("xq_sb", [128, 2, DC, TB], BF16, "B")
    hq_sb = sb("hq_sb", [128, HC, TB], BF16, "B")
    r2_sb = sb("r2_sb", [128, DC, TB], BF16, "B")
    ob = sb("ob", [128, DC, TB], F32, "B")
    htmp8b = sb("htmp8b", [128, 4, TB], F32, "B")
    sqb1 = sb("sqb1", [128, DC, TB], BF16, "B")
    sqb2 = sb("sqb2", [128, DC, TB], BF16, "B")

    wstg = [sb("wstg_%d" % i, [128, 512], F32, "A",
               at=handles["sqb1" if i < 2 else "sqb2"][1] + (i % 2) * 2048)
            for i in range(4)]
    woutq = [sb("woutq_%d" % i, [128, 512], BF16, "A",
                at=handles["htmp8b"][1] + i * 1024) for i in range(4)]
    assert ptr["A"] <= SB_TOP and ptr["B"] <= SB_TOP

    with contextlib.ExitStack() as st:
        en = st.enter_context
        pb = [en(nc.psum_tensor(f"pb{i}", [128, 512], F32)) for i in range(8)]
        sems = {k: en(nc.semaphore(k)) for k in (
            "qw", "qx0", "qx1", "qx2", "qa0", "qa1", "qy0", "qy1", "qo",
            "qt0", "qt1", "qt2", "qt3", "qws0", "qws1", "qws2", "qws3",
            "qwr", "qw2r", "qd", "tm", "sd", "vd")}

        prog = {e: [] for e in ENGS}
        cnt = {k: 0 for k in sems}
        waited = {e: {} for e in ENGS}
        own_sem = {"T": "tm", "V": "vd", "S": "sd"}

        def emit(eng, fn):
            prog[eng].append(fn)

        def wait(eng, sem, val, force=False, selfdep=False):
            if val is None or val <= 0:
                return
            if not force and not selfdep and own_sem.get(eng) == sem:
                return
            if not force and waited[eng].get(sem, 0) >= val:
                return
            waited[eng][sem] = max(waited[eng].get(sem, 0), val)
            emit(eng, lambda E, s=sems[sem], v=val: E.wait_ge(s, v))

        def drn(eng):
            emit(eng, lambda E: E.drain())

        def op(eng, fn, inc=None, inck=1):
            if inc is None:
                emit(eng, fn)
                return None
            cnt[inc] += inck
            emit(eng, lambda E, f=fn, s=sems[inc], k=inck: f(E).then_inc(s, k))
            return cnt[inc]

        bank_owner = [None] * 8
        bank_rr = [0]
        rbank_rr = [0]

        def take_rbank():
            b = 6 + rbank_rr[0] % 2
            rbank_rr[0] += 1
            if bank_owner[b] is not None:
                assert bank_owner[b][0] != "__taken__"
                wait("T", *bank_owner[b])
            bank_owner[b] = ("__taken__", 0)
            return b

        def take_bank():
            b = bank_rr[0] % 6
            bank_rr[0] += 1
            if bank_owner[b] is not None:
                assert bank_owner[b][0] != "__taken__", f"bank {b} re-taken before consumer recorded"
                wait("T", *bank_owner[b])
            bank_owner[b] = ("__taken__", 0)
            return b

        def free_bank(b, sem, val):
            bank_owner[b] = (sem, val)

        def mm_group(lhs_rhs, out_ap, pre_waits=()):
            b = take_bank()
            for s_name, v in pre_waits:
                wait("T", s_name, v)
            n = len(lhs_rhs)
            for i, (lhs, rhs) in enumerate(lhs_rhs):
                kw = dict(start=(i == 0), stop=(i == n - 1))
                if i == n - 1:
                    tv = op("T", lambda E, l=lhs, r=rhs, o=out_ap(b), kw=kw:
                            E.matmul(o, l, r, **kw), inc="tm")
                else:
                    op("T", lambda E, l=lhs, r=rhs, o=out_ap(b), kw=kw:
                       E.matmul(o, l, r, **kw))
            return b, tv

        def full(b):
            return pb[b][:, :]

        def ln_slice(b, T):
            return pb[b][:, 0:T]

        def row_slice(b, T):
            return pb[b][0:1, 0:T]

        def dma(eng, q, out_ap, in_ap, pre=()):
            for s_name, v in pre:
                wait(eng, s_name, v)
            return op(eng, lambda E, o=out_ap, i=in_ap: E.dma_start(out=o, in_=i),
                      inc=q, inck=16)

        dma("G", "qw", winT_sb[:, :, :], winT.rearrange("(k p) e -> p k e", p=128))
        dma("G", "qw", wstT_sb[:, :, :], wstT.rearrange("(k p) n -> p k n", p=128))
        dma("G", "qw", wdtT_sb[:, :, :], wdtT.rearrange("(k p) n -> p k n", p=128))
        dma("G", "qw", wcT_sb[:, :], wcT[:, :])
        dma("G", "qw", woutT_sb[:, :, :], woutT.rearrange("(k p) e -> p k e", p=128))
        qw_all = dma("G", "qw", vecs_sb[:, :], vecs[:, :])
        qs_vecs = None

        op("V", lambda E: E.memset(ones_bf[:], 1.0))
        op("V", lambda E: E.memset(ones_row[:], 1.0))
        op("V", lambda E: E.memset(cb_one[:], 1.0))
        op("V", lambda E: E.memset(cb_eps[:], EPS))
        op("V", lambda E: E.memset(cb_eps256[:], 256.0 * EPS))
        cv = op("V", lambda E: E.memset(cb_ln16[:], float(np.log(16.0))), inc="vd")
        wait("S", "qw", qw_all)
        wait("S", "vd", cv)
        op("S", lambda E: E.activation(out=tmpc1[:], in_=vecs_sb[:, 1:2], func=AF.Exp))
        drn("S")
        sv = op("S", lambda E: E.activation(out=tmpc2[:], in_=tmpc1[:], func=AF.Ln,
                                            bias=cb_one[:]), inc="sd")
        wait("V", "sd", sv)
        negrate_vd = op("V", lambda E: E.tensor_scalar(out=negrate[:], in0=tmpc2[:],
                                          scalar1=RATE_EPS, scalar2=-1.0,
                                          op0=ALU.add, op1=ALU.mult), inc="vd")
        wait("V", "qw", qw_all)

        def quant_cols(dst, src_ap, postmul):
            v = op("V", lambda E, s=src_ap: E.tensor_scalar(
                out=dst[:], in0=s, scalar1=64.0, scalar2=MAGIC,
                op0=ALU.mult, op1=ALU.add), inc="vd")
            wait("V", "vd", v, selfdep=True)
            v = op("V", lambda E: E.tensor_scalar(
                out=dst[:], in0=dst[:], scalar1=MAGIC, scalar2=127.0,
                op0=ALU.subtract, op1=ALU.min), inc="vd")
            wait("V", "vd", v, selfdep=True)
            v = op("V", lambda E, pm=postmul: E.tensor_scalar(
                out=dst[:], in0=dst[:], scalar1=-128.0, scalar2=pm,
                op0=ALU.max, op1=ALU.mult), inc="vd")
            return v

        quant_cols(b1q16, vecs_sb[:, 10:42], 1.0 / 64.0)
        quant_cols(b2q, vecs_sb[:, 42:50], 16.0 / 64.0)
        op("V", lambda E: E.memset(ztmp8[:, 0, 0:1], 0.0), inc="vd")

        dbgA_qs = [0]
        xb_load = [None] * NTA
        xb_free_T = [None] * NTA
        xb_free_V = [None] * NTA
        r1_done = [None] * NTA
        r1_store = [None] * NTA
        scan_vd = [None] * NTA
        silu_done = [None] * NTA
        z_done = [None] * NTA

        def load_xb(t):
            pre = []
            if t >= 3:
                pre = [("tm", xb_free_T[t - 3]), ("vd", xb_free_V[t - 3])]
            src = xT.rearrange("(c p) t -> p c t", p=128)[:, :, t * TA:(t + 1) * TA]
            xb_load[t] = dma("G", "qx%d" % (t % 3), xb[:, t % 3, :, :], src, pre=pre)

        load_xb(0)
        load_xb(1)
        load_xb(2)

        w1rr = w1T.rearrange("(k p) h -> p k h", p=128)
        w2rr = w2T.rearrange("(k p) e -> p k e", p=128)
        wq_stages = []
        for k in range(DC):
            for c0 in range(0, HID, 512):
                wq_stages.append((w1rr[:, k, c0:c0 + 512],
                                  w1qs[:, k, c0:c0 + 512], 1.0 / 1024.0))
        for k in range(HC):
            for c0 in range(0, D, 512):
                wq_stages.append((w2rr[:, k, c0:c0 + 512],
                                  w2qs[:, k, c0:c0 + 512], 16.0 / 1024.0))
        wqs = {"issued": 0, "done": 0, "qv": [], "vv": [], "oq": []}

        def wq_issue(n):
            for _ in range(n):
                i = wqs["issued"]
                if i >= len(wq_stages):
                    return
                wqs["issued"] += 1
                si = i % 4
                srcap, _, _ = wq_stages[i]
                pre = []
                if i >= 4:
                    pre.append(("vd", wqs["vv"][i - 4]))
                wqs["qv"].append(dma("Y", "qt%d" % si, wstg[si][:, :], srcap,
                                     pre=pre))

        def wq_quant(n):
            for _ in range(n):
                i = wqs["done"]
                if i >= min(wqs["issued"], len(wq_stages)):
                    return
                wqs["done"] += 1
                si = i % 4
                _, dstap, pm = wq_stages[i]
                wait("V", "qt%d" % si, wqs["qv"][i])
                if i >= 4:
                    wait("V", "qws%d" % si, wqs["oq"][i - 4])
                s = wstg[si][:, :]
                v = op("V", lambda E, s=s: E.tensor_scalar(
                    out=s, in0=s, scalar1=64.0, scalar2=MAGIC,
                    op0=ALU.mult, op1=ALU.add), inc="vd")
                wait("V", "vd", v, selfdep=True)
                v = op("V", lambda E, s=s: E.tensor_scalar(
                    out=s, in0=s, scalar1=MAGIC, scalar2=127.0,
                    op0=ALU.subtract, op1=ALU.min), inc="vd")
                wait("V", "vd", v, selfdep=True)
                v = op("V", lambda E, s=s, o=woutq[si][:, :], pm=pm:
                       E.tensor_scalar(out=o, in0=s, scalar1=-128.0, scalar2=pm,
                                       op0=ALU.max, op1=ALU.mult), inc="vd")
                wqs["vv"].append(v)
                wqs["oq"].append(dma("Y", "qws%d" % si, dstap, woutq[si][:, :],
                                     pre=[("vd", v)]))

        wq_issue(4)

        for it in range(NTA + 1):
            t = it
            tp = it - 1
            if t < NTA:
                col = slice(t * TA, (t + 1) * TA)
                par = t % 2
                xpar = t % 3
                if debug and t == 1:
                    for _e in ("T", "S", "V", "G"):
                        wait(_e, "qd", cnt["qd"], force=True)
                if debug and t == 2:
                    for _e in ("T", "S", "V", "G"):
                        wait(_e, "qd", dbgA_qs[0])
                wait("T", "qw", qw_all)
                wait("T", "qx%d" % (t % 3), xb_load[t])
                u_copy_done = None
                for e in range(8):
                    mlist = [(winT_sb[:, k, e * 128:(e + 1) * 128], xb[:, xpar, k, :])
                             for k in range(DC)]
                    b, tv = mm_group(mlist, full)
                    wait("S", "tm", tv)
                    sv = op("S", lambda E, b=b, e=e, p=par: E.copy(
                        out=u_bf[:, p, e, :], in_=full(b)), inc="sd")
                    free_bank(b, "sd", sv)
                    u_copy_done = sv
                if debug and t == 0:
                    dma("Y", "qd", dbg["u0e"].rearrange("(c p) t -> p c t", p=128),
                        u_bf[:, 0, :, :], pre=[("sd", u_copy_done)])
                mlist = [(wstT_sb[:, k, :], u_bf[:, par, k, :]) for k in range(DC)]
                bt_b, bt_tv = mm_group(mlist, full, pre_waits=[("sd", u_copy_done)])
                mlist = [(wdtT_sb[:, k, :], u_bf[:, par, k, :]) for k in range(DC)]
                dt_b, dt_tv = mm_group(mlist, full)
                wait("S", "tm", dt_tv)
                sv = op("S", lambda E, b=dt_b: E.activation(
                    out=e1_t[:], in_=full(b), func=AF.Exp,
                    bias=vecs_sb[:, 0:1]), inc="sd")
                free_bank(dt_b, "sd", sv)
                drn("S")
                sv = op("S", lambda E: E.activation(
                    out=sp_t[:], in_=e1_t[:], func=AF.Ln, bias=cb_one[:]), inc="sd")
                wait("V", "sd", sv)
                wait("V", "vd", negrate_vd, selfdep=True)
                vv = op("V", lambda E: E.tensor_scalar(
                    out=ld_t[:], in0=sp_t[:], scalar1=DT_MAX, scalar2=negrate[:],
                    op0=ALU.min, op1=ALU.mult), inc="vd")
                wait("S", "vd", vv)
                sv = op("S", lambda E, p=par: E.activation(
                    out=dec_t[:, p, :], in_=ld_t[:], func=AF.Exp), inc="sd")
                wait("V", "sd", sv)
                omd_vv = op("V", lambda E, p=par: E.tensor_scalar(
                    out=omd_t[:], in0=dec_t[:, p, :], scalar1=1.0, scalar2=-1.0,
                    op0=ALU.subtract, op1=ALU.mult), inc="vd")
                wait("V", "tm", bt_tv)
                wait("V", "vd", omd_vv, selfdep=True)
                c_vv = op("V", lambda E, b=bt_b: E.tensor_tensor(
                    out=c_t[:], in0=omd_t[:], in1=full(b), op=ALU.mult), inc="vd")
                free_bank(bt_b, "vd", c_vv)
                wait("V", "vd", c_vv, selfdep=True)
                if t % (L // TA) == 0:
                    init = 0.0
                else:
                    init = h_t[:, (t - 1) % 2, TA - 1:TA]
                scan_vd[t] = op("V", lambda E, p=par, i=init: E.tensor_tensor_scan(
                    out=h_t[:, p, :], data0=dec_t[:, p, :], data1=c_t[:],
                    initial=i, op0=ALU.mult, op1=ALU.add), inc="vd")
                wait("V", "vd", scan_vd[t], selfdep=True)
                hb_vv = op("V", lambda E, p=par: E.tensor_copy(
                    out=hb_t[:, p, :], in_=h_t[:, p, :]), inc="vd")
                for e in range(8):
                    mlist = [(winT_sb[:, k, (8 + e) * 128:(9 + e) * 128],
                              xb[:, xpar, k, :]) for k in range(DC)]
                    gb, gtv = mm_group(mlist, full)
                    wait("S", "tm", gtv)
                    ssv = op("S", lambda E, b=gb, e=e, p=par: E.activation(
                        out=sg[:, p, e, :], in_=full(b), func=AF.Silu), inc="sd")
                    free_bank(gb, "sd", ssv)
                xb_free_T[t] = gtv
                silu_done[t] = ssv

            if tp >= 0:
                colp = slice(tp * TA, (tp + 1) * TA)
                parp = tp % 2
                for e in range(DC):
                    mlist = [(woutT_sb[:, k, e * 128:(e + 1) * 128], z_sb[:, k, :])
                             for k in range(DC)]
                    b, tv = mm_group(mlist, full,
                                     pre_waits=[("vd", z_done[tp])])
                    wait("V", "tm", tv)
                    if tp >= 2 and r1_store[tp - 2] is not None:
                        wait("V", "qa%d" % (tp % 2), r1_store[tp - 2])
                    vv = op("V", lambda E, b=b, e=e, p=parp, xp=tp % 3: E.tensor_tensor(
                        out=r1_sb[:, p, e, :], in0=full(b), in1=xb[:, xp, e, :],
                        op=ALU.add), inc="vd")
                    free_bank(b, "vd", vv)
                r1_done[tp] = vv
                xb_free_V[tp] = vv
                dst = r1s.rearrange("(c p) t -> p c t", p=128)[:, :, colp]
                r1_store[tp] = dma("S", "qa%d" % (tp % 2), dst,
                                   r1_sb[:, parp, :, :],
                                   pre=[("vd", r1_done[tp])])
                if tp + 3 < NTA:
                    load_xb(tp + 3)

            if t < NTA:
                if debug and t == 1:
                    wait("V", "qd", dbgA_qs[0])
                wait("V", "sd", silu_done[t])
                for e in range(DC):
                    b, tv = mm_group(
                        [(wcT_sb[:, e * 128:(e + 1) * 128], hb_t[:, t % 2, :])],
                        full, pre_waits=[("vd", hb_vv)])
                    wait("V", "tm", tv)
                    vv = op("V", lambda E, b=b, e=e, p=t % 2: E.tensor_tensor(
                        out=ztmp8[:, e, :], in0=full(b), in1=sg[:, p, e, :],
                        op=ALU.mult), inc="vd")
                    free_bank(b, "vd", vv)
                wait("V", "vd", vv, selfdep=True)
                for e in range(DC):
                    vv = op("V", lambda E, e=e, p=t % 3: E.scalar_tensor_tensor(
                        out=z_sb[:, e, :], in0=xb[:, p, e, :],
                        scalar=vecs_sb[:, 2 + e:3 + e], in1=ztmp8[:, e, :],
                        op0=ALU.mult, op1=ALU.add), inc="vd")
                z_done[t] = vv
                for _wq in range(8):
                    wq_quant(1)
                    wq_issue(1)
                if debug and t == 0:
                    dma("Y", "qd", dbg["xb0"].rearrange("(c p) t -> p c t", p=128),
                        xb[:, 0, :, :], pre=[("vd", z_done[0])])
                    dma("Y", "qd", dbg["w0"][:, :], winT_sb[:, 0, :],
                        pre=[("vd", z_done[0])])
                    dma("Y", "qd", dbg["u0"].rearrange("(c p) t -> p c t", p=128),
                        u_bf[:, 0, :, :], pre=[("sd", u_copy_done)])
                    dma("Y", "qd", dbg["dec0"][:, :], dec_t[:, 0, :],
                        pre=[("vd", scan_vd[0])])
                    dma("Y", "qd", dbg["h0"][:, :], h_t[:, 0, :],
                        pre=[("vd", scan_vd[0])])
                    dma("Y", "qd", dbg["z0"].rearrange("(c p) t -> p c t", p=128),
                        z_sb[:, :, :], pre=[("vd", z_done[0])])
                    dbgA_qs[0] = cnt["qd"]


        while wqs["done"] < len(wq_stages):
            wq_quant(1)
            wq_issue(1)
        endA = [("tm", cnt["tm"]), ("vd", cnt["vd"]), ("sd", cnt["sd"]),
                ("qa0", cnt["qa0"]), ("qa1", cnt["qa1"])]
        spill_done = [("qws%d" % i, cnt["qws%d" % i]) for i in range(4)]
        w1_rl = dma("G", "qwr", w1q[:, :, :], w1qs[:, :, :],
                    pre=endA + spill_done)
        w2_rl = dma("Y", "qw2r", w2q[:, :, :], w2qs[:, :, :],
                    pre=endA + spill_done)

        r1_load = [None] * NTB
        r1ld_free_T = [None] * NTB
        r1ld_free_V = [None] * NTB
        out_store = [None] * NTB
        state = [dict() for _ in range(NTB)]

        def load_r1(tb):
            assert r1_load[tb] is None, tb
            pre = []
            if tb < 3:
                pre += endA
            if tb >= 2:
                pre += [("tm", r1ld_free_T[tb - 2]), ("vd", r1ld_free_V[tb - 2])]
            src = r1s.rearrange("(c p) t -> p c t", p=128)[:, :, tb * TB:(tb + 1) * TB]
            r1_load[tb] = dma("Y", "qy%d" % (tb % 2), r1ld[:, tb % 2, :, :], src, pre=pre)

        load_r1(0)
        load_r1(1)
        for _i in range(4):
            wait("V", "qws%d" % _i, cnt["qws%d" % _i])
        prev_htmp_v = [cnt["vd"]]
        prev_htmp_s = [0]

        def emit_sq1(n):
            p = n % 2
            wait("S", "qy%d" % p, r1_load[n])
            sqv = None
            for c in range(DC):
                sqv = op("S", lambda E, c=c, p=p: E.activation(
                    out=sqb1[:, c, :], in_=r1ld[:, p, c, :], func=AF.Square),
                    inc="sd")
            state[n]["sq1_sd"] = sqv

        def rows_and_s(bm, bv, tv_m, tv_v, rows, eps_ap=None, exp_bias=None):
            if eps_ap is None:
                eps_ap = cb_eps
            """V rows chain + S ln/exp; returns (rows_vd, sv, free info)."""
            mr, tr, vr, rr = rows
            wait("V", "tm", tv_m)
            vv = op("V", lambda E, b=bm, mr=mr: E.tensor_scalar(
                out=mr[0:1, :], in0=row_slice(b, TB), scalar1=1.0 / D,
                scalar2=None, op0=ALU.mult, op1=ALU.bypass), inc="vd")
            free_bank(bm, "vd", vv)
            wait("V", "vd", vv, selfdep=True)
            vv = op("V", lambda E, mr=mr, tr=tr: E.tensor_tensor(
                out=tr[0:1, :], in0=mr[0:1, :], in1=mr[0:1, :],
                op=ALU.mult), inc="vd")
            wait("V", "tm", tv_v)
            wait("V", "vd", vv, selfdep=True)
            vv = op("V", lambda E, b=bv, tr=tr, vr=vr: E.scalar_tensor_tensor(
                out=vr[0:1, :], in0=row_slice(b, TB), scalar=1.0 / D,
                in1=tr[0:1, :], op0=ALU.mult, op1=ALU.subtract), inc="vd")
            free_bank(bv, "vd", vv)
            rows_vd = vv
            wait("S", "vd", rows_vd)
            op("S", lambda E, tr=tr, vr=vr, ep=eps_ap: E.activation(
                out=tr[0:1, :], in_=vr[0:1, :], func=AF.Ln, bias=ep[0:1, :]))
            drn("S")
            if exp_bias is None:
                sv = op("S", lambda E, tr=tr, rr=rr: E.activation(
                    out=rr[0:1, :], in_=tr[0:1, :], func=AF.Exp,
                    scale=-0.5), inc="sd")
            else:
                sv = op("S", lambda E, tr=tr, rr=rr, eb=exp_bias: E.activation(
                    out=rr[0:1, :], in_=tr[0:1, :], func=AF.Exp,
                    scale=-0.5, bias=eb[0:1, :]), inc="sd")
            return rows_vd, sv

        def emit_bcasts(rows, rows_vd, sv):
            mr, tr, vr, rr = rows
            bmb = take_rbank()
            wait("T", "vd", rows_vd)
            op("T", lambda E, b=bmb, mr=mr: E.matmul(
                ln_slice(b, TB), ones_row[:], mr[0:1, :], start=True, stop=True),
                inc="tm")
            brs = take_rbank()
            wait("T", "sd", sv)
            tv_rs = op("T", lambda E, b=brs, rr=rr: E.matmul(
                ln_slice(b, TB), ones_row[:], rr[0:1, :], start=True, stop=True),
                inc="tm")
            return bmb, brs, tv_rs

        def emit_apply(src_chunks, dst_chunks, bmb, brs, tv_rs):
            wait("V", "tm", tv_rs)
            for c in range(DC):
                vv1 = op("V", lambda E, c=c, b=bmb, s=src_chunks[c], d=dst_chunks[c]:
                         E.tensor_tensor(out=d, in0=s, in1=ln_slice(b, TB),
                                         op=ALU.subtract), inc="vd")
            free_bank(bmb, "vd", vv1)
            wait("V", "vd", vv1, selfdep=True)
            for c in range(DC):
                vv2 = op("V", lambda E, c=c, b=brs, d=dst_chunks[c]:
                         E.tensor_tensor(out=d, in0=d, in1=ln_slice(b, TB),
                                         op=ALU.mult), inc="vd")
            free_bank(brs, "vd", vv2)
            return vv2

        for it in range(-1, NTB + 1):
            t3, t2, t1 = it - 1, it, it + 1
            do1 = 0 <= t1 < NTB
            do2 = 0 <= t2 < NTB
            do3 = 0 <= t3 < NTB
            if do1:
                p1 = t1 % 2
                wait("T", "qy%d" % p1, r1_load[t1])
                wait("T", "qwr", w1_rl)
                wait("T", "qw2r", w2_rl)
                if "sq1_sd" not in state[t1]:
                    emit_sq1(t1)
                src1 = [r1ld[:, p1, c, :] for c in range(DC)]
                dst1 = [x1_sb[:, p1, c, :] for c in range(DC)]
                ap1_state = {"last": None}

                def ap1_chunk(c, guard=None, _s1=src1, _d1=dst1,
                              _st=ap1_state):
                    bmb1_, brs1_, tvrs1_ = ap1_state["bc"]
                    wait("V", "tm", tvrs1_)
                    if guard:
                        wait("V", "vd", guard, selfdep=True)
                    v1 = op("V", lambda E, c=c, b=bmb1_, s=_s1[c], d=_d1[c]:
                            E.tensor_tensor(out=d, in0=s, in1=ln_slice(b, TB),
                                            op=ALU.subtract), inc="vd")
                    wait("V", "vd", v1, selfdep=True)
                    v2 = op("V", lambda E, c=c, b=brs1_, d=_d1[c]:
                            E.tensor_tensor(out=d, in0=d, in1=ln_slice(b, TB),
                                            op=ALU.mult), inc="vd")
                    _st["last"] = (v1, v2)
                bm1, tvm1 = mm_group([(ones_bf[:], src1[c]) for c in range(DC)],
                                     lambda b: row_slice(b, TB))
                bv1, tvv1 = mm_group([(ones_bf[:], sqb1[:, c, :]) for c in range(DC)],
                                     lambda b: row_slice(b, TB),
                                     pre_waits=[("sd", state[t1]["sq1_sd"])])
                r1ld_free_T[t1] = max(tvm1, tvv1)
                rows1_vd, sv1 = rows_and_s(bm1, bv1, tvm1, tvv1, rows1,
                                           exp_bias=cb_ln16)
            if do3:
                p3 = t3 % 2
                st3 = state[t3]
                r2_vds = []
                sq2_sds = []
                for e in range(DC):
                    mlist = [(w2q[:, k, e * 128:(e + 1) * 128], hq_sb[:, k, :])
                             for k in range(HC)]
                    b, tv = mm_group(mlist, lambda bb: ln_slice(bb, TB),
                                     pre_waits=[("vd", st3["hq_vd"])])
                    wait("V", "tm", tv)
                    r2_vd = op("V", lambda E, b=b, e=e, p=p3: E.scalar_tensor_tensor(
                        out=r2_sb[:, e, :], in0=ln_slice(b, TB),
                        scalar=b2q[:, e:e + 1], in1=x1_sb[:, p, e, :],
                        op0=ALU.add, op1=ALU.add), inc="vd")
                    free_bank(b, "vd", r2_vd)
                    r2_vds.append(r2_vd)
                    wait("S", "vd", r2_vd)
                    sq2_sds.append(op("S", lambda E, e=e: E.activation(
                        out=sqb2[:, e, :], in_=r2_sb[:, e, :], func=AF.Square),
                        inc="sd"))
                    if e == 3 and do1:
                        ap1_state["bc"] = emit_bcasts(rows1, rows1_vd, sv1)
                    if e >= 4 and do1:
                        ap1_chunk(e - 4, guard=r2_vds[e - 4])
                bm2 = take_bank()
                for e in range(DC):
                    wait("T", "vd", r2_vds[e])
                    kw = dict(start=(e == 0), stop=(e == DC - 1))
                    if e == DC - 1:
                        tvm2 = op("T", lambda E, e=e, b=bm2, kw=kw: E.matmul(
                            row_slice(b, TB), ones_bf[:], r2_sb[:, e, :], **kw),
                            inc="tm")
                    else:
                        op("T", lambda E, e=e, b=bm2, kw=kw: E.matmul(
                            row_slice(b, TB), ones_bf[:], r2_sb[:, e, :], **kw))
                bv2 = take_bank()
                for e in range(DC):
                    wait("T", "sd", sq2_sds[e])
                    kw = dict(start=(e == 0), stop=(e == DC - 1))
                    if e == DC - 1:
                        tvv2 = op("T", lambda E, e=e, b=bv2, kw=kw: E.matmul(
                            row_slice(b, TB), ones_bf[:], sqb2[:, e, :], **kw),
                            inc="tm")
                    else:
                        op("T", lambda E, e=e, b=bv2, kw=kw: E.matmul(
                            row_slice(b, TB), ones_bf[:], sqb2[:, e, :], **kw))
            elif do1:
                ap1_state["bc"] = emit_bcasts(rows1, rows1_vd, sv1)
            if do1:
                first = 4 if do3 else 0
                for c in range(first, DC):
                    ap1_chunk(c, guard=(r2_vds[c] if do3 else None))
                bmb1_, brs1_, _ = ap1_state["bc"]
                free_bank(bmb1_, "vd", ap1_state["last"][0])
                free_bank(brs1_, "vd", ap1_state["last"][1])
                apply1_vd = ap1_state["last"][1]
                r1ld_free_V[t1] = apply1_vd
                if debug and t1 == 0:
                    dma("Y", "qd", dbg["x10"].rearrange("(c p) t -> p c t", p=128),
                        x1_sb[:, 0, :, :], pre=[("vd", apply1_vd)])
                if debug and t1 == 1:
                    wait("V", "qd", cnt["qd"])

                def emit_xq(_av=apply1_vd, _t1=t1, _p1=p1):
                    wait("V", "vd", _av, selfdep=True)
                    xq_vd = None
                    for half in range(2):
                        wait("V", "vd", prev_htmp_v[0], selfdep=True)
                        wait("V", "sd", prev_htmp_s[0])
                        for j in range(4):
                            c = half * 4 + j
                            v = op("V", lambda E, c=c, j=j, p=_p1: E.tensor_scalar(
                                out=htmp8b[:, j, :], in0=x1_sb[:, p, c, :],
                                scalar1=MAGIC, scalar2=MAGIC,
                                op0=ALU.add, op1=ALU.subtract), inc="vd")
                        wait("V", "vd", v, selfdep=True)
                        for j in range(4):
                            c = half * 4 + j
                            xq_vd = op("V", lambda E, c=c, j=j, p=_p1: E.tensor_scalar(
                                out=xq_sb[:, p, c, :], in0=htmp8b[:, j, :],
                                scalar1=127.0, scalar2=-128.0,
                                op0=ALU.min, op1=ALU.max), inc="vd")
                        prev_htmp_v[0] = xq_vd
                    state[_t1]["xq_vd"] = xq_vd
                if not do2:
                    emit_xq()
                if t1 + 2 < NTB:
                    load_r1(t1 + 2)
            ln2_rows = [None]
            if do3:
                ln2_rows[0] = rows_and_s(bm2, bv2, tvm2, tvv2, rows2,
                                         eps_ap=cb_eps256)

            def emit_ln2_rows():
                rows2_vd, sv2 = ln2_rows[0]
                return emit_bcasts(rows2, rows2_vd, sv2)

            def emit_ln2_apply(bcasts):
                bmb2, brs2, tvrs2 = bcasts
                if t3 >= 1 and out_store[t3 - 1] is not None:
                    wait("V", "qo", out_store[t3 - 1])
                src2 = [r2_sb[:, c, :] for c in range(DC)]
                dst2 = [ob[:, c, :] for c in range(DC)]
                apply2_vd = emit_apply(src2, dst2, bmb2, brs2, tvrs2)
                if debug and t3 == 0:
                    dma("Y", "qd", dbg["hq0"].rearrange("(c p) t -> p c t", p=128),
                        hq_sb[:, :, :], pre=[("vd", st3["hq_vd"])])
                colp = slice(t3 * TB, (t3 + 1) * TB)
                dstap = outT.rearrange("(c p) t -> p c t", p=128)[:, :, colp]
                out_store[t3] = dma("Y", "qo", dstap, ob[:, :, :],
                                    pre=[("vd", apply2_vd)])

            if do3 and not do2:
                emit_ln2_apply(emit_ln2_rows())
            if do2:
                p2 = t2 % 2
                hq_vd = None
                for gsub in range(HC // 4):
                    groups = []
                    for j in range(4):
                        hc = gsub * 4 + j
                        mlist = [(w1q[:, k, hc * 128:(hc + 1) * 128],
                                  xq_sb[:, p2, k, :]) for k in range(DC)]
                        groups.append(mm_group(mlist, lambda bb: ln_slice(bb, TB),
                                               pre_waits=[("vd", state[t2]["xq_vd"])]))
                    wait("V", "vd", prev_htmp_v[0], selfdep=True)
                    wait("V", "sd", prev_htmp_s[0])
                    for j in range(4):
                        b, tv = groups[j]
                        hc = gsub * 4 + j
                        wait("V", "tm", tv)
                        v = op("V", lambda E, b=b, hc=hc, j=j: E.tensor_scalar(
                            out=htmp8b[:, j, :], in0=ln_slice(b, TB),
                            scalar1=b1q16[:, hc:hc + 1], scalar2=16.0,
                            op0=ALU.add, op1=ALU.mult), inc="vd")
                        free_bank(b, "vd", v)
                    wait("V", "vd", v, selfdep=True)
                    for j in range(4):
                        v = op("V", lambda E, j=j: E.tensor_scalar(
                            out=htmp8b[:, j, :], in0=htmp8b[:, j, :],
                            scalar1=MAGIC, scalar2=MAGIC,
                            op0=ALU.add, op1=ALU.subtract), inc="vd")
                    prev_htmp_v[0] = v
                    wait("V", "vd", v, selfdep=True)
                    for j in range(4):
                        hc = gsub * 4 + j
                        v = op("V", lambda E, j=j, hc=hc: E.tensor_scalar(
                            out=hq_sb[:, hc, :], in0=htmp8b[:, j, :],
                            scalar1=127.0, scalar2=0.0,
                            op0=ALU.min, op1=ALU.max), inc="vd")
                    hq_sd = None
                    prev_htmp_v[0] = v
                    if gsub == 0 and do1:
                        emit_xq()
                    if gsub == 6 and do3:
                        ln2_bcasts = emit_ln2_rows()
                state[t2]["hq_vd"] = v
                if do3:
                    emit_ln2_apply(ln2_bcasts)
            if 0 <= it + 2 < NTB and "sq1_sd" not in state[it + 2]:
                emit_sq1(it + 2)

        wait("S", "qo", cnt["qo"], force=True)
        wait("S", "qa0", cnt["qa0"], force=True)
        wait("S", "qa1", cnt["qa1"], force=True)
        for q in ("qy0", "qy1", "qt0", "qt1", "qd"):
            if cnt[q]:
                wait("Y", q, cnt[q], force=True)
        for q in ("qw", "qx0", "qx1", "qx2"):
            wait("G", q, cnt[q], force=True)

        block = en(nc.Block())

        @block.tensor
        def _(E):
            for f in prog["T"]:
                f(E)

        @block.vector
        def _(E):
            for f in prog["V"]:
                f(E)

        @block.scalar
        def _(E):
            for f in prog["S"]:
                f(E)

        @block.gpsimd
        def _(E):
            for f in prog["G"]:
                f(E)

        @block.sync
        def _(E):
            for f in prog["Y"]:
                f(E)

    return nc


_NC_CACHE = {}


def _get_nc():
    if "nc" not in _NC_CACHE:
        _NC_CACHE["nc"] = build_nc()
    return _NC_CACHE["nc"]


def kernel(L0_attn_x, L0_attn_W_in, L0_attn_W_state, L0_attn_W_dt, L0_attn_b_dt,
           L0_attn_A_log, L0_attn_W_C, L0_attn_D, L0_attn_W_out,
           L0_ffn_w1, L0_ffn_b1, L0_ffn_w2, L0_ffn_b2):
    x = np.asarray(L0_attn_x, np.float32)
    B, Lq, Dm = x.shape
    assert (B, Lq, Dm) == (16, L, D)

    winT = np.ascontiguousarray(np.asarray(L0_attn_W_in, np.float32).T)
    wstT = np.ascontiguousarray(np.asarray(L0_attn_W_state, np.float32).T)
    wdtT = np.ascontiguousarray(np.asarray(L0_attn_W_dt, np.float32).T)
    wcT = np.ascontiguousarray(np.asarray(L0_attn_W_C, np.float32).T)
    woutT = np.ascontiguousarray(np.asarray(L0_attn_W_out, np.float32).T)
    w1T = np.ascontiguousarray(np.asarray(L0_ffn_w1, np.float32).T)
    w2T = np.ascontiguousarray(np.asarray(L0_ffn_w2, np.float32).T)

    vecs = np.zeros((128, 64), np.float32)
    vecs[:, 0] = np.asarray(L0_attn_b_dt, np.float32)
    vecs[:, 1] = np.asarray(L0_attn_A_log, np.float32)
    vecs[:, 2:10] = np.asarray(L0_attn_D, np.float32).reshape(8, 128).T
    vecs[:, 10:42] = np.asarray(L0_ffn_b1, np.float32).reshape(32, 128).T
    vecs[:, 42:50] = np.asarray(L0_ffn_b2, np.float32).reshape(8, 128).T

    nc = _get_nc()
    in_maps = []
    for c in range(8):
        xc = np.ascontiguousarray(
            x[2 * c:2 * c + 2].reshape(NTOK, D).T)
        in_maps.append({
            "xT": xc, "winT": winT, "wstT": wstT, "wdtT": wdtT, "wcT": wcT,
            "woutT": woutT, "w1T": w1T, "w2T": w2T, "vecs": vecs,
        })
    res = run_bass_kernel_spmd(nc, in_maps, core_ids=list(range(8)))
    out = np.empty((16, L, D), np.float32)
    for c in range(8):
        o = np.asarray(res.results[c]["outT"], np.float32)
        out[2 * c:2 * c + 2] = o.T.reshape(BLOC, L, D)
    return out


# version 20 (final)
# speedup vs baseline: 1.2354x; 1.0339x over previous
import contextlib
import numpy as np
import ml_dtypes
import concourse.bass as bass
import concourse.mybir as mybir
from concourse.bass_utils import run_bass_kernel_spmd

F32 = mybir.dt.float32
BF16 = mybir.dt.bfloat16
F16 = mybir.dt.float16
AF = mybir.ActivationFunctionType
ALU = mybir.AluOpType
BF = ml_dtypes.bfloat16

D = 1024
DC = 8
E2 = 2048
NST = 128
HID = 4096
HC = 32
BLOC = 2
L = 4096
NTOK = BLOC * L
TA = 512
NTA = NTOK // TA
TB = 256
NTB = NTOK // TB
MAGIC = 12582912.0
FOFF = 1536.0
EPS = 1e-5
DT_MAX = 10.0
RATE_EPS = 1e-4

ENGS = ("T", "V", "S", "G", "Y")


def build_nc(debug=False):
    nc = bass.Bass("TRN2", target_bir_lowering=False)

    xT = nc.dram_tensor("xT", [D, NTOK], F32, kind="ExternalInput").ap()
    winT = nc.dram_tensor("winT", [D, E2], F32, kind="ExternalInput").ap()
    wstT = nc.dram_tensor("wstT", [D, NST], F32, kind="ExternalInput").ap()
    wdtT = nc.dram_tensor("wdtT", [D, NST], F32, kind="ExternalInput").ap()
    wcT = nc.dram_tensor("wcT", [NST, D], F32, kind="ExternalInput").ap()
    woutT = nc.dram_tensor("woutT", [D, D], F32, kind="ExternalInput").ap()
    w1T = nc.dram_tensor("w1T", [D, HID], F32, kind="ExternalInput").ap()
    w2T = nc.dram_tensor("w2T", [HID, D], F32, kind="ExternalInput").ap()
    vecs = nc.dram_tensor("vecs", [128, 64], F32, kind="ExternalInput").ap()
    outT = nc.dram_tensor("outT", [D, NTOK], F32, kind="ExternalOutput").ap()
    r1s = nc.dram_tensor("r1s", [D, NTOK], BF16, kind="Internal").ap()
    w1qs = nc.dram_tensor("w1qs", [128, DC, HID], BF16, kind="Internal").ap()
    w2qs = nc.dram_tensor("w2qs", [128, HC, D], BF16, kind="Internal").ap()

    dbg = {}
    if debug:
        dbg["u0"] = nc.dram_tensor("dbg_u0", [D, TA], BF16, kind="ExternalOutput").ap()
        dbg["dec0"] = nc.dram_tensor("dbg_dec0", [NST, TA], F32, kind="ExternalOutput").ap()
        dbg["h0"] = nc.dram_tensor("dbg_h0", [NST, TA], F32, kind="ExternalOutput").ap()
        dbg["z0"] = nc.dram_tensor("dbg_z0", [D, TA], BF16, kind="ExternalOutput").ap()
        dbg["x10"] = nc.dram_tensor("dbg_x10", [D, TB], F32, kind="ExternalOutput").ap()
        dbg["hq0"] = nc.dram_tensor("dbg_hq0", [HID, TB], BF16, kind="ExternalOutput").ap()
        dbg["xb0"] = nc.dram_tensor("dbg_xb0", [D, TA], BF16, kind="ExternalOutput").ap()
        dbg["u0e"] = nc.dram_tensor("dbg_u0e", [D, TA], BF16, kind="ExternalOutput").ap()
        dbg["w0"] = nc.dram_tensor("dbg_w0", [128, 2048], BF16, kind="ExternalOutput").ap()

    SB_TOP = 229344
    P_BASE = 16512
    ptr = {"P": P_BASE}
    handles = {}

    def nbytes_pp(shape, dtype):
        n = 1
        for s in shape[1:]:
            n *= s
        return n * (2 if dtype == BF16 else 4)

    def sb(name, shape, dtype, region, at=None):
        if at is None:
            off = (ptr[region] + 31) // 32 * 32
            ptr[region] = off + nbytes_pp(shape, dtype)
        else:
            off = at
        assert off + nbytes_pp(shape, dtype) <= SB_TOP, (name, off)
        h = nc.alloc_sbuf_tensor_at(name, shape, dtype, offset=off)
        handles[name] = (h, off)
        return h

    vecs_sb = sb("vecs_sb", [128, 64], F32, "P")
    negrate = sb("negrate", [128, 1], F32, "P")
    tmpc1 = sb("tmpc1", [128, 1], F32, "P")
    tmpc2 = sb("tmpc2", [128, 1], F32, "P")
    b1q16 = sb("b1q16", [128, 32], F32, "P")
    b2q = sb("b2q", [128, 8], F32, "P")
    ones_bf = sb("ones_bf", [128, 1], BF16, "P")
    cb_one = sb("cb_one", [128, 1], F32, "P")
    cb_eps = sb("cb_eps", [128, 1], F32, "P")
    cb_eps256 = sb("cb_eps256", [128, 1], F32, "P")
    cb_ln16 = sb("cb_ln16", [128, 1], F32, "P")
    ones_row = sb("ones_row", [1, 128], F32, "P")
    rows1 = tuple(sb("row1_%d" % i, [1, 256], F32, "P") for i in range(4))
    rows2 = rows1

    overlay = (ptr["P"] + 31) // 32 * 32
    ptr["A"] = overlay
    ptr["B"] = overlay

    winT_sb = sb("winT_sb", [128, DC, E2], BF16, "A")
    wstT_sb = sb("wstT_sb", [128, DC, NST], BF16, "A")
    wdtT_sb = sb("wdtT_sb", [128, DC, NST], BF16, "A")
    wcT_sb = sb("wcT_sb", [128, D], BF16, "A")
    woutT_sb = sb("woutT_sb", [128, DC, D], BF16, "A")
    xb = sb("xb", [128, 3, DC, TA], BF16, "A")
    u_bf = sb("u_bf", [128, 2, DC, TA], BF16, "A")
    sg = sb("sg", [128, 2, DC, TA], BF16, "A")
    z_sb = sb("z_sb", [128, DC, TA], BF16, "A")
    ztmp8 = sb("ztmp8", [128, DC, TA], F32, "A")
    r1_sb = sb("r1_sb", [128, 2, DC, TA], BF16, "A")
    sp_t = sb("sp_t", [128, TA], F32, "A")
    e1_t = sb("e1_t", [128, TA], F32, "A")
    ld_t = sb("ld_t", [128, TA], F32, "A")
    dec_t = sb("dec_t", [128, 2, TA], F32, "A")
    omd_t = sb("omd_t", [128, TA], F32, "A")
    c_t = sb("c_t", [128, TA], F32, "A")
    h_t = sb("h_t", [128, 2, TA], F32, "A")
    hb_t = sb("hb_t", [128, 2, TA], BF16, "A")

    w1q = sb("w1q", [128, DC, HID], BF16, "B")
    w2q = sb("w2q", [128, HC, D], BF16, "B")
    r1ld = sb("r1ld", [128, 2, DC, TB], BF16, "B")
    x1_sb = sb("x1_sb", [128, 2, DC, TB], F32, "B")
    xq_sb = sb("xq_sb", [128, 2, DC, TB], BF16, "B")
    hq_sb = sb("hq_sb", [128, HC, TB], BF16, "B")
    r2_sb = sb("r2_sb", [128, DC, TB], BF16, "B")
    ob = sb("ob", [128, DC, TB], F32, "B")
    htmp8b = sb("htmp8b", [128, 4, TB], F32, "B")
    sqb1 = sb("sqb1", [128, DC, TB], BF16, "B")
    sqb2 = sb("sqb2", [128, DC, TB], BF16, "B")

    wstg = [sb("wstg_%d" % i, [128, 512], F32, "A",
               at=handles["sqb1" if i < 2 else "sqb2"][1] + (i % 2) * 2048)
            for i in range(4)]
    woutq = [sb("woutq_%d" % i, [128, 512], BF16, "A",
                at=handles["htmp8b"][1] + i * 1024) for i in range(4)]
    wq16 = [sb("wq16_%d" % i, [128, 512], F16, "A",
               at=handles["woutq_%d" % i][1]) for i in range(4)]
    qc16 = sb("qc16", [128, 32], F16, "A", at=handles["woutq_0"][1])
    htmp16 = sb("htmp16", [128, 4, TB], F16, "B", at=handles["htmp8b"][1])
    hq16 = sb("hq16", [128, HC, TB], F16, "B", at=handles["hq_sb"][1])
    assert ptr["A"] <= SB_TOP and ptr["B"] <= SB_TOP

    with contextlib.ExitStack() as st:
        en = st.enter_context
        pb = [en(nc.psum_tensor(f"pb{i}", [128, 512], F32)) for i in range(8)]
        sems = {k: en(nc.semaphore(k)) for k in (
            "qw", "qx0", "qx1", "qx2", "qa0", "qa1", "qy0", "qy1", "qo",
            "qt0", "qt1", "qt2", "qt3", "qws0", "qws1", "qws2", "qws3",
            "qwr", "qw2r", "qd", "tm", "sd", "vd")}

        prog = {e: [] for e in ENGS}
        cnt = {k: 0 for k in sems}
        waited = {e: {} for e in ENGS}
        own_sem = {"T": "tm", "V": "vd", "S": "sd"}

        def emit(eng, fn):
            prog[eng].append(fn)

        def wait(eng, sem, val, force=False, selfdep=False):
            if val is None or val <= 0:
                return
            if not force and not selfdep and own_sem.get(eng) == sem:
                return
            if not force and waited[eng].get(sem, 0) >= val:
                return
            waited[eng][sem] = max(waited[eng].get(sem, 0), val)
            emit(eng, lambda E, s=sems[sem], v=val: E.wait_ge(s, v))

        def drn(eng):
            emit(eng, lambda E: E.drain())

        def op(eng, fn, inc=None, inck=1):
            if inc is None:
                emit(eng, fn)
                return None
            cnt[inc] += inck
            emit(eng, lambda E, f=fn, s=sems[inc], k=inck: f(E).then_inc(s, k))
            return cnt[inc]

        bank_owner = [None] * 8
        bank_rr = [0]
        rbank_rr = [0]

        def take_rbank():
            b = 6 + rbank_rr[0] % 2
            rbank_rr[0] += 1
            if bank_owner[b] is not None:
                assert bank_owner[b][0] != "__taken__"
                wait("T", *bank_owner[b])
            bank_owner[b] = ("__taken__", 0)
            return b

        def take_bank():
            b = bank_rr[0] % 6
            bank_rr[0] += 1
            if bank_owner[b] is not None:
                assert bank_owner[b][0] != "__taken__", f"bank {b} re-taken before consumer recorded"
                wait("T", *bank_owner[b])
            bank_owner[b] = ("__taken__", 0)
            return b

        def free_bank(b, sem, val):
            bank_owner[b] = (sem, val)

        def mm_group(lhs_rhs, out_ap, pre_waits=()):
            b = take_bank()
            for s_name, v in pre_waits:
                wait("T", s_name, v)
            n = len(lhs_rhs)
            for i, (lhs, rhs) in enumerate(lhs_rhs):
                kw = dict(start=(i == 0), stop=(i == n - 1))
                if i == n - 1:
                    tv = op("T", lambda E, l=lhs, r=rhs, o=out_ap(b), kw=kw:
                            E.matmul(o, l, r, **kw), inc="tm")
                else:
                    op("T", lambda E, l=lhs, r=rhs, o=out_ap(b), kw=kw:
                       E.matmul(o, l, r, **kw))
            return b, tv

        def full(b):
            return pb[b][:, :]

        def ln_slice(b, T):
            return pb[b][:, 0:T]

        def row_slice(b, T):
            return pb[b][0:1, 0:T]

        def dma(eng, q, out_ap, in_ap, pre=()):
            for s_name, v in pre:
                wait(eng, s_name, v)
            return op(eng, lambda E, o=out_ap, i=in_ap: E.dma_start(out=o, in_=i),
                      inc=q, inck=16)

        dma("G", "qw", winT_sb[:, :, :], winT.rearrange("(k p) e -> p k e", p=128))
        dma("G", "qw", wstT_sb[:, :, :], wstT.rearrange("(k p) n -> p k n", p=128))
        dma("G", "qw", wdtT_sb[:, :, :], wdtT.rearrange("(k p) n -> p k n", p=128))
        dma("G", "qw", wcT_sb[:, :], wcT[:, :])
        dma("G", "qw", woutT_sb[:, :, :], woutT.rearrange("(k p) e -> p k e", p=128))
        qw_all = dma("G", "qw", vecs_sb[:, :], vecs[:, :])
        qs_vecs = None

        op("V", lambda E: E.memset(ones_bf[:], 1.0))
        op("V", lambda E: E.memset(ones_row[:], 1.0))
        op("V", lambda E: E.memset(cb_one[:], 1.0))
        op("V", lambda E: E.memset(cb_eps[:], EPS))
        op("V", lambda E: E.memset(cb_eps256[:], 256.0 * EPS))
        cv = op("V", lambda E: E.memset(cb_ln16[:], float(np.log(16.0))), inc="vd")
        wait("S", "qw", qw_all)
        wait("S", "vd", cv)
        op("S", lambda E: E.activation(out=tmpc1[:], in_=vecs_sb[:, 1:2], func=AF.Exp))
        drn("S")
        sv = op("S", lambda E: E.activation(out=tmpc2[:], in_=tmpc1[:], func=AF.Ln,
                                            bias=cb_one[:]), inc="sd")
        wait("V", "sd", sv)
        negrate_vd = op("V", lambda E: E.tensor_scalar(out=negrate[:], in0=tmpc2[:],
                                          scalar1=RATE_EPS, scalar2=-1.0,
                                          op0=ALU.add, op1=ALU.mult), inc="vd")
        wait("V", "qw", qw_all)

        def quant_cols(dst, src_ap, postmul, ncol):
            t16 = qc16[:, 0:ncol]
            wait("V", "vd", cnt["vd"], selfdep=True)
            v = op("V", lambda E, s=src_ap, t=t16: E.tensor_scalar(
                out=t, in0=s, scalar1=64.0, scalar2=FOFF,
                op0=ALU.mult, op1=ALU.add), inc="vd")
            wait("V", "vd", v, selfdep=True)
            v = op("V", lambda E, t=t16: E.tensor_scalar(
                out=t, in0=t, scalar1=FOFF, scalar2=127.0,
                op0=ALU.subtract, op1=ALU.min), inc="vd")
            wait("V", "vd", v, selfdep=True)
            v = op("V", lambda E, t=t16, pm=postmul: E.tensor_scalar(
                out=dst[:], in0=t, scalar1=-128.0, scalar2=pm,
                op0=ALU.max, op1=ALU.mult), inc="vd")
            return v

        quant_cols(b1q16, vecs_sb[:, 10:42], 1.0 / 64.0, 32)
        quant_cols(b2q, vecs_sb[:, 42:50], 16.0 / 64.0, 8)
        op("V", lambda E: E.memset(ztmp8[:, 0, 0:1], 0.0), inc="vd")

        dbgA_qs = [0]
        xb_load = [None] * NTA
        xb_free_T = [None] * NTA
        xb_free_V = [None] * NTA
        r1_done = [None] * NTA
        r1_store = [None] * NTA
        scan_vd = [None] * NTA
        silu_done = [None] * NTA
        z_done = [None] * NTA

        def load_xb(t):
            pre = []
            if t >= 3:
                pre = [("tm", xb_free_T[t - 3]), ("vd", xb_free_V[t - 3])]
            src = xT.rearrange("(c p) t -> p c t", p=128)[:, :, t * TA:(t + 1) * TA]
            xb_load[t] = dma("G", "qx%d" % (t % 3), xb[:, t % 3, :, :], src, pre=pre)

        load_xb(0)
        load_xb(1)
        load_xb(2)

        w1rr = w1T.rearrange("(k p) h -> p k h", p=128)
        w2rr = w2T.rearrange("(k p) e -> p k e", p=128)
        wq_stages = []
        for k in range(DC):
            for c0 in range(0, HID, 512):
                wq_stages.append((w1rr[:, k, c0:c0 + 512],
                                  w1qs[:, k, c0:c0 + 512], 1.0 / 1024.0))
        for k in range(HC):
            for c0 in range(0, D, 512):
                wq_stages.append((w2rr[:, k, c0:c0 + 512],
                                  w2qs[:, k, c0:c0 + 512], 16.0 / 1024.0))
        wqs = {"issued": 0, "done": 0, "qv": [], "vv": [], "oq": []}

        def wq_issue(n):
            for _ in range(n):
                i = wqs["issued"]
                if i >= len(wq_stages):
                    return
                wqs["issued"] += 1
                si = i % 4
                srcap, _, _ = wq_stages[i]
                pre = []
                if i >= 4:
                    pre.append(("vd", wqs["vv"][i - 4]))
                wqs["qv"].append(dma("Y", "qt%d" % si, wstg[si][:, :], srcap,
                                     pre=pre))

        def wq_quant(n):
            for _ in range(n):
                i = wqs["done"]
                if i >= min(wqs["issued"], len(wq_stages)):
                    return
                wqs["done"] += 1
                si = i % 4
                _, dstap, pm = wq_stages[i]
                wait("V", "qt%d" % si, wqs["qv"][i])
                if i >= 4:
                    wait("V", "qws%d" % si, wqs["oq"][i - 4])
                s = wstg[si][:, :]
                t = wq16[si][:, :]
                v = op("V", lambda E, s=s, t=t: E.tensor_scalar(
                    out=t, in0=s, scalar1=64.0, scalar2=FOFF,
                    op0=ALU.mult, op1=ALU.add), inc="vd")
                wait("V", "vd", v, selfdep=True)
                v = op("V", lambda E, t=t: E.tensor_scalar(
                    out=t, in0=t, scalar1=FOFF, scalar2=127.0,
                    op0=ALU.subtract, op1=ALU.min), inc="vd")
                wait("V", "vd", v, selfdep=True)
                v = op("V", lambda E, t=t, o=woutq[si][:, :], pm=pm:
                       E.tensor_scalar(out=o, in0=t, scalar1=-128.0, scalar2=pm,
                                       op0=ALU.max, op1=ALU.mult), inc="vd")
                wqs["vv"].append(v)
                wqs["oq"].append(dma("Y", "qws%d" % si, dstap, woutq[si][:, :],
                                     pre=[("vd", v)]))

        wq_issue(4)

        for it in range(NTA + 1):
            t = it
            tp = it - 1
            if t < NTA:
                col = slice(t * TA, (t + 1) * TA)
                par = t % 2
                xpar = t % 3
                if debug and t == 1:
                    for _e in ("T", "S", "V", "G"):
                        wait(_e, "qd", cnt["qd"], force=True)
                if debug and t == 2:
                    for _e in ("T", "S", "V", "G"):
                        wait(_e, "qd", dbgA_qs[0])
                wait("T", "qw", qw_all)
                wait("T", "qx%d" % (t % 3), xb_load[t])
                u_copy_done = None
                for e in range(8):
                    mlist = [(winT_sb[:, k, e * 128:(e + 1) * 128], xb[:, xpar, k, :])
                             for k in range(DC)]
                    b, tv = mm_group(mlist, full)
                    wait("S", "tm", tv)
                    sv = op("S", lambda E, b=b, e=e, p=par: E.copy(
                        out=u_bf[:, p, e, :], in_=full(b)), inc="sd")
                    free_bank(b, "sd", sv)
                    u_copy_done = sv
                if debug and t == 0:
                    dma("Y", "qd", dbg["u0e"].rearrange("(c p) t -> p c t", p=128),
                        u_bf[:, 0, :, :], pre=[("sd", u_copy_done)])
                mlist = [(wstT_sb[:, k, :], u_bf[:, par, k, :]) for k in range(DC)]
                bt_b, bt_tv = mm_group(mlist, full, pre_waits=[("sd", u_copy_done)])
                mlist = [(wdtT_sb[:, k, :], u_bf[:, par, k, :]) for k in range(DC)]
                dt_b, dt_tv = mm_group(mlist, full)
                wait("S", "tm", dt_tv)
                sv = op("S", lambda E, b=dt_b: E.activation(
                    out=e1_t[:], in_=full(b), func=AF.Exp,
                    bias=vecs_sb[:, 0:1]), inc="sd")
                free_bank(dt_b, "sd", sv)
                drn("S")
                sv = op("S", lambda E: E.activation(
                    out=sp_t[:], in_=e1_t[:], func=AF.Ln, bias=cb_one[:]), inc="sd")
                wait("V", "sd", sv)
                wait("V", "vd", negrate_vd, selfdep=True)
                vv = op("V", lambda E: E.tensor_scalar(
                    out=ld_t[:], in0=sp_t[:], scalar1=DT_MAX, scalar2=negrate[:],
                    op0=ALU.min, op1=ALU.mult), inc="vd")
                wait("S", "vd", vv)
                sv = op("S", lambda E, p=par: E.activation(
                    out=dec_t[:, p, :], in_=ld_t[:], func=AF.Exp), inc="sd")
                wait("V", "sd", sv)
                omd_vv = op("V", lambda E, p=par: E.tensor_scalar(
                    out=omd_t[:], in0=dec_t[:, p, :], scalar1=1.0, scalar2=-1.0,
                    op0=ALU.subtract, op1=ALU.mult), inc="vd")
                wait("V", "tm", bt_tv)
                wait("V", "vd", omd_vv, selfdep=True)
                c_vv = op("V", lambda E, b=bt_b: E.tensor_tensor(
                    out=c_t[:], in0=omd_t[:], in1=full(b), op=ALU.mult), inc="vd")
                free_bank(bt_b, "vd", c_vv)
                wait("V", "vd", c_vv, selfdep=True)
                if t % (L // TA) == 0:
                    init = 0.0
                else:
                    init = h_t[:, (t - 1) % 2, TA - 1:TA]
                scan_vd[t] = op("V", lambda E, p=par, i=init: E.tensor_tensor_scan(
                    out=h_t[:, p, :], data0=dec_t[:, p, :], data1=c_t[:],
                    initial=i, op0=ALU.mult, op1=ALU.add), inc="vd")
                wait("V", "vd", scan_vd[t], selfdep=True)
                hb_vv = op("V", lambda E, p=par: E.tensor_copy(
                    out=hb_t[:, p, :], in_=h_t[:, p, :]), inc="vd")
                for e in range(8):
                    mlist = [(winT_sb[:, k, (8 + e) * 128:(9 + e) * 128],
                              xb[:, xpar, k, :]) for k in range(DC)]
                    gb, gtv = mm_group(mlist, full)
                    wait("S", "tm", gtv)
                    ssv = op("S", lambda E, b=gb, e=e, p=par: E.activation(
                        out=sg[:, p, e, :], in_=full(b), func=AF.Silu), inc="sd")
                    free_bank(gb, "sd", ssv)
                xb_free_T[t] = gtv
                silu_done[t] = ssv

            if tp >= 0:
                colp = slice(tp * TA, (tp + 1) * TA)
                parp = tp % 2
                for e in range(DC):
                    mlist = [(woutT_sb[:, k, e * 128:(e + 1) * 128], z_sb[:, k, :])
                             for k in range(DC)]
                    b, tv = mm_group(mlist, full,
                                     pre_waits=[("vd", z_done[tp])])
                    wait("V", "tm", tv)
                    if tp >= 2 and r1_store[tp - 2] is not None:
                        wait("V", "qa%d" % (tp % 2), r1_store[tp - 2])
                    vv = op("V", lambda E, b=b, e=e, p=parp, xp=tp % 3: E.tensor_tensor(
                        out=r1_sb[:, p, e, :], in0=full(b), in1=xb[:, xp, e, :],
                        op=ALU.add), inc="vd")
                    free_bank(b, "vd", vv)
                r1_done[tp] = vv
                xb_free_V[tp] = vv
                dst = r1s.rearrange("(c p) t -> p c t", p=128)[:, :, colp]
                r1_store[tp] = dma("S", "qa%d" % (tp % 2), dst,
                                   r1_sb[:, parp, :, :],
                                   pre=[("vd", r1_done[tp])])
                if tp + 3 < NTA:
                    load_xb(tp + 3)

            if t < NTA:
                if debug and t == 1:
                    wait("V", "qd", dbgA_qs[0])
                wait("V", "sd", silu_done[t])
                for e in range(DC):
                    b, tv = mm_group(
                        [(wcT_sb[:, e * 128:(e + 1) * 128], hb_t[:, t % 2, :])],
                        full, pre_waits=[("vd", hb_vv)])
                    wait("V", "tm", tv)
                    vv = op("V", lambda E, b=b, e=e, p=t % 2: E.tensor_tensor(
                        out=ztmp8[:, e, :], in0=full(b), in1=sg[:, p, e, :],
                        op=ALU.mult), inc="vd")
                    free_bank(b, "vd", vv)
                wait("V", "vd", vv, selfdep=True)
                for e in range(DC):
                    vv = op("V", lambda E, e=e, p=t % 3: E.scalar_tensor_tensor(
                        out=z_sb[:, e, :], in0=xb[:, p, e, :],
                        scalar=vecs_sb[:, 2 + e:3 + e], in1=ztmp8[:, e, :],
                        op0=ALU.mult, op1=ALU.add), inc="vd")
                z_done[t] = vv
                for _wq in range(8):
                    wq_quant(1)
                    wq_issue(1)
                if debug and t == 0:
                    dma("Y", "qd", dbg["xb0"].rearrange("(c p) t -> p c t", p=128),
                        xb[:, 0, :, :], pre=[("vd", z_done[0])])
                    dma("Y", "qd", dbg["w0"][:, :], winT_sb[:, 0, :],
                        pre=[("vd", z_done[0])])
                    dma("Y", "qd", dbg["u0"].rearrange("(c p) t -> p c t", p=128),
                        u_bf[:, 0, :, :], pre=[("sd", u_copy_done)])
                    dma("Y", "qd", dbg["dec0"][:, :], dec_t[:, 0, :],
                        pre=[("vd", scan_vd[0])])
                    dma("Y", "qd", dbg["h0"][:, :], h_t[:, 0, :],
                        pre=[("vd", scan_vd[0])])
                    dma("Y", "qd", dbg["z0"].rearrange("(c p) t -> p c t", p=128),
                        z_sb[:, :, :], pre=[("vd", z_done[0])])
                    dbgA_qs[0] = cnt["qd"]


        while wqs["done"] < len(wq_stages):
            wq_quant(1)
            wq_issue(1)
        endA = [("tm", cnt["tm"]), ("vd", cnt["vd"]), ("sd", cnt["sd"]),
                ("qa0", cnt["qa0"]), ("qa1", cnt["qa1"])]
        spill_done = [("qws%d" % i, cnt["qws%d" % i]) for i in range(4)]
        w1_rl = dma("G", "qwr", w1q[:, :, :], w1qs[:, :, :],
                    pre=endA + spill_done)
        w2_rl = dma("Y", "qw2r", w2q[:, :, :], w2qs[:, :, :],
                    pre=endA + spill_done)

        r1_load = [None] * NTB
        r1ld_free_T = [None] * NTB
        r1ld_free_V = [None] * NTB
        out_store = [None] * NTB
        state = [dict() for _ in range(NTB)]

        def load_r1(tb):
            assert r1_load[tb] is None, tb
            pre = []
            if tb < 3:
                pre += endA
            if tb >= 2:
                pre += [("tm", r1ld_free_T[tb - 2]), ("vd", r1ld_free_V[tb - 2])]
            src = r1s.rearrange("(c p) t -> p c t", p=128)[:, :, tb * TB:(tb + 1) * TB]
            r1_load[tb] = dma("Y", "qy%d" % (tb % 2), r1ld[:, tb % 2, :, :], src, pre=pre)

        load_r1(0)
        load_r1(1)
        for _i in range(4):
            wait("V", "qws%d" % _i, cnt["qws%d" % _i])
        prev_htmp_v = [cnt["vd"]]
        prev_htmp_s = [0]

        def emit_sq1(n):
            p = n % 2
            wait("S", "qy%d" % p, r1_load[n])
            sqv = None
            for c in range(DC):
                sqv = op("S", lambda E, c=c, p=p: E.activation(
                    out=sqb1[:, c, :], in_=r1ld[:, p, c, :], func=AF.Square),
                    inc="sd")
            state[n]["sq1_sd"] = sqv

        def rows_and_s(bm, bv, tv_m, tv_v, rows, eps_ap=None, exp_bias=None):
            if eps_ap is None:
                eps_ap = cb_eps
            """V rows chain + S ln/exp; returns (rows_vd, sv, free info)."""
            mr, tr, vr, rr = rows
            wait("V", "tm", tv_m)
            vv = op("V", lambda E, b=bm, mr=mr: E.tensor_scalar(
                out=mr[0:1, :], in0=row_slice(b, TB), scalar1=1.0 / D,
                scalar2=None, op0=ALU.mult, op1=ALU.bypass), inc="vd")
            free_bank(bm, "vd", vv)
            wait("V", "vd", vv, selfdep=True)
            vv = op("V", lambda E, mr=mr, tr=tr: E.tensor_tensor(
                out=tr[0:1, :], in0=mr[0:1, :], in1=mr[0:1, :],
                op=ALU.mult), inc="vd")
            wait("V", "tm", tv_v)
            wait("V", "vd", vv, selfdep=True)
            vv = op("V", lambda E, b=bv, tr=tr, vr=vr: E.scalar_tensor_tensor(
                out=vr[0:1, :], in0=row_slice(b, TB), scalar=1.0 / D,
                in1=tr[0:1, :], op0=ALU.mult, op1=ALU.subtract), inc="vd")
            free_bank(bv, "vd", vv)
            rows_vd = vv
            wait("S", "vd", rows_vd)
            op("S", lambda E, tr=tr, vr=vr, ep=eps_ap: E.activation(
                out=tr[0:1, :], in_=vr[0:1, :], func=AF.Ln, bias=ep[0:1, :]))
            drn("S")
            if exp_bias is None:
                sv = op("S", lambda E, tr=tr, rr=rr: E.activation(
                    out=rr[0:1, :], in_=tr[0:1, :], func=AF.Exp,
                    scale=-0.5), inc="sd")
            else:
                sv = op("S", lambda E, tr=tr, rr=rr, eb=exp_bias: E.activation(
                    out=rr[0:1, :], in_=tr[0:1, :], func=AF.Exp,
                    scale=-0.5, bias=eb[0:1, :]), inc="sd")
            return rows_vd, sv

        def emit_bcasts(rows, rows_vd, sv):
            mr, tr, vr, rr = rows
            bmb = take_rbank()
            wait("T", "vd", rows_vd)
            op("T", lambda E, b=bmb, mr=mr: E.matmul(
                ln_slice(b, TB), ones_row[:], mr[0:1, :], start=True, stop=True),
                inc="tm")
            brs = take_rbank()
            wait("T", "sd", sv)
            tv_rs = op("T", lambda E, b=brs, rr=rr: E.matmul(
                ln_slice(b, TB), ones_row[:], rr[0:1, :], start=True, stop=True),
                inc="tm")
            return bmb, brs, tv_rs

        def emit_apply(src_chunks, dst_chunks, bmb, brs, tv_rs):
            wait("V", "tm", tv_rs)
            for c in range(DC):
                vv1 = op("V", lambda E, c=c, b=bmb, s=src_chunks[c], d=dst_chunks[c]:
                         E.tensor_tensor(out=d, in0=s, in1=ln_slice(b, TB),
                                         op=ALU.subtract), inc="vd")
            free_bank(bmb, "vd", vv1)
            wait("V", "vd", vv1, selfdep=True)
            for c in range(DC):
                vv2 = op("V", lambda E, c=c, b=brs, d=dst_chunks[c]:
                         E.tensor_tensor(out=d, in0=d, in1=ln_slice(b, TB),
                                         op=ALU.mult), inc="vd")
            free_bank(brs, "vd", vv2)
            return vv2

        for it in range(-1, NTB + 1):
            t3, t2, t1 = it - 1, it, it + 1
            do1 = 0 <= t1 < NTB
            do2 = 0 <= t2 < NTB
            do3 = 0 <= t3 < NTB
            if do1:
                p1 = t1 % 2
                wait("T", "qy%d" % p1, r1_load[t1])
                wait("T", "qwr", w1_rl)
                wait("T", "qw2r", w2_rl)
                if "sq1_sd" not in state[t1]:
                    emit_sq1(t1)
                src1 = [r1ld[:, p1, c, :] for c in range(DC)]
                dst1 = [x1_sb[:, p1, c, :] for c in range(DC)]
                ap1_state = {"last": None}

                def ap1_chunk(c, guard=None, _s1=src1, _d1=dst1,
                              _st=ap1_state):
                    bmb1_, brs1_, tvrs1_ = ap1_state["bc"]
                    wait("V", "tm", tvrs1_)
                    if guard:
                        wait("V", "vd", guard, selfdep=True)
                    v1 = op("V", lambda E, c=c, b=bmb1_, s=_s1[c], d=_d1[c]:
                            E.tensor_tensor(out=d, in0=s, in1=ln_slice(b, TB),
                                            op=ALU.subtract), inc="vd")
                    wait("V", "vd", v1, selfdep=True)
                    v2 = op("V", lambda E, c=c, b=brs1_, d=_d1[c]:
                            E.tensor_tensor(out=d, in0=d, in1=ln_slice(b, TB),
                                            op=ALU.mult), inc="vd")
                    _st["last"] = (v1, v2)
                bm1, tvm1 = mm_group([(ones_bf[:], src1[c]) for c in range(DC)],
                                     lambda b: row_slice(b, TB))
                bv1, tvv1 = mm_group([(ones_bf[:], sqb1[:, c, :]) for c in range(DC)],
                                     lambda b: row_slice(b, TB),
                                     pre_waits=[("sd", state[t1]["sq1_sd"])])
                r1ld_free_T[t1] = max(tvm1, tvv1)
                rows1_vd, sv1 = rows_and_s(bm1, bv1, tvm1, tvv1, rows1,
                                           exp_bias=cb_ln16)
            if do3:
                p3 = t3 % 2
                st3 = state[t3]
                r2_vds = []
                sq2_sds = []
                for e in range(DC):
                    mlist = [(w2q[:, k, e * 128:(e + 1) * 128], hq_sb[:, k, :])
                             for k in range(HC)]
                    b, tv = mm_group(mlist, lambda bb: ln_slice(bb, TB),
                                     pre_waits=[("vd", st3["hq_vd"])])
                    wait("V", "tm", tv)
                    r2_vd = op("V", lambda E, b=b, e=e, p=p3: E.scalar_tensor_tensor(
                        out=r2_sb[:, e, :], in0=ln_slice(b, TB),
                        scalar=b2q[:, e:e + 1], in1=x1_sb[:, p, e, :],
                        op0=ALU.add, op1=ALU.add), inc="vd")
                    free_bank(b, "vd", r2_vd)
                    r2_vds.append(r2_vd)
                    wait("S", "vd", r2_vd)
                    sq2_sds.append(op("S", lambda E, e=e: E.activation(
                        out=sqb2[:, e, :], in_=r2_sb[:, e, :], func=AF.Square),
                        inc="sd"))
                    if e == 3 and do1:
                        ap1_state["bc"] = emit_bcasts(rows1, rows1_vd, sv1)
                    if e >= 4 and do1:
                        ap1_chunk(e - 4, guard=r2_vds[e - 4])
                bm2 = take_bank()
                for e in range(DC):
                    wait("T", "vd", r2_vds[e])
                    kw = dict(start=(e == 0), stop=(e == DC - 1))
                    if e == DC - 1:
                        tvm2 = op("T", lambda E, e=e, b=bm2, kw=kw: E.matmul(
                            row_slice(b, TB), ones_bf[:], r2_sb[:, e, :], **kw),
                            inc="tm")
                    else:
                        op("T", lambda E, e=e, b=bm2, kw=kw: E.matmul(
                            row_slice(b, TB), ones_bf[:], r2_sb[:, e, :], **kw))
                bv2 = take_bank()
                for e in range(DC):
                    wait("T", "sd", sq2_sds[e])
                    kw = dict(start=(e == 0), stop=(e == DC - 1))
                    if e == DC - 1:
                        tvv2 = op("T", lambda E, e=e, b=bv2, kw=kw: E.matmul(
                            row_slice(b, TB), ones_bf[:], sqb2[:, e, :], **kw),
                            inc="tm")
                    else:
                        op("T", lambda E, e=e, b=bv2, kw=kw: E.matmul(
                            row_slice(b, TB), ones_bf[:], sqb2[:, e, :], **kw))
            elif do1:
                ap1_state["bc"] = emit_bcasts(rows1, rows1_vd, sv1)
            if do1:
                first = 4 if do3 else 0
                for c in range(first, DC):
                    ap1_chunk(c, guard=(r2_vds[c] if do3 else None))
                bmb1_, brs1_, _ = ap1_state["bc"]
                free_bank(bmb1_, "vd", ap1_state["last"][0])
                free_bank(brs1_, "vd", ap1_state["last"][1])
                apply1_vd = ap1_state["last"][1]
                r1ld_free_V[t1] = apply1_vd
                if debug and t1 == 0:
                    dma("Y", "qd", dbg["x10"].rearrange("(c p) t -> p c t", p=128),
                        x1_sb[:, 0, :, :], pre=[("vd", apply1_vd)])
                if debug and t1 == 1:
                    wait("V", "qd", cnt["qd"])
                wait("V", "vd", apply1_vd, selfdep=True)
                for half in range(2):
                    wait("V", "vd", prev_htmp_v[0], selfdep=True)
                    wait("V", "sd", prev_htmp_s[0])
                    for j in range(4):
                        c = half * 4 + j
                        v = op("V", lambda E, c=c, j=j, p=p1: E.tensor_scalar(
                            out=htmp16[:, j, :], in0=x1_sb[:, p, c, :],
                            scalar1=-128.0, scalar2=FOFF,
                            op0=ALU.max, op1=ALU.add), inc="vd")
                    wait("V", "vd", v, selfdep=True)
                    for j in range(4):
                        c = half * 4 + j
                        xq_vd = op("V", lambda E, c=c, j=j, p=p1: E.tensor_scalar(
                            out=xq_sb[:, p, c, :], in0=htmp16[:, j, :],
                            scalar1=FOFF, scalar2=127.0,
                            op0=ALU.subtract, op1=ALU.min), inc="vd")
                    prev_htmp_v[0] = xq_vd
                state[t1]["xq_vd"] = xq_vd
                if t1 + 2 < NTB:
                    load_r1(t1 + 2)
            ln2_rows = [None]
            if do3:
                ln2_rows[0] = rows_and_s(bm2, bv2, tvm2, tvv2, rows2,
                                         eps_ap=cb_eps256)

            def emit_ln2_rows():
                rows2_vd, sv2 = ln2_rows[0]
                return emit_bcasts(rows2, rows2_vd, sv2)

            def emit_ln2_apply(bcasts):
                bmb2, brs2, tvrs2 = bcasts
                if t3 >= 1 and out_store[t3 - 1] is not None:
                    wait("V", "qo", out_store[t3 - 1])
                src2 = [r2_sb[:, c, :] for c in range(DC)]
                dst2 = [ob[:, c, :] for c in range(DC)]
                apply2_vd = emit_apply(src2, dst2, bmb2, brs2, tvrs2)
                if debug and t3 == 0:
                    dma("Y", "qd", dbg["hq0"].rearrange("(c p) t -> p c t", p=128),
                        hq_sb[:, :, :], pre=[("vd", st3["hq_vd"])])
                colp = slice(t3 * TB, (t3 + 1) * TB)
                dstap = outT.rearrange("(c p) t -> p c t", p=128)[:, :, colp]
                out_store[t3] = dma("Y", "qo", dstap, ob[:, :, :],
                                    pre=[("vd", apply2_vd)])

            if do3 and not do2:
                emit_ln2_apply(emit_ln2_rows())
            if do2:
                p2 = t2 % 2
                hq_vd = None
                for gsub in range(HC // 4):
                    groups = []
                    for j in range(4):
                        hc = gsub * 4 + j
                        mlist = [(w1q[:, k, hc * 128:(hc + 1) * 128],
                                  xq_sb[:, p2, k, :]) for k in range(DC)]
                        groups.append(mm_group(mlist, lambda bb: ln_slice(bb, TB),
                                               pre_waits=[("vd", state[t2]["xq_vd"])]))
                    wait("V", "vd", prev_htmp_v[0], selfdep=True)
                    wait("V", "sd", prev_htmp_s[0])
                    for j in range(4):
                        b, tv = groups[j]
                        hc = gsub * 4 + j
                        wait("V", "tm", tv)
                        v = op("V", lambda E, b=b, hc=hc, j=j: E.tensor_scalar(
                            out=htmp8b[:, j, :], in0=ln_slice(b, TB),
                            scalar1=b1q16[:, hc:hc + 1], scalar2=16.0,
                            op0=ALU.add, op1=ALU.mult), inc="vd")
                        free_bank(b, "vd", v)
                    wait("V", "vd", v, selfdep=True)
                    for j in range(4):
                        hc = gsub * 4 + j
                        v = op("V", lambda E, j=j, hc=hc: E.tensor_scalar(
                            out=hq16[:, hc, :], in0=htmp8b[:, j, :],
                            scalar1=0.0, scalar2=FOFF,
                            op0=ALU.max, op1=ALU.add), inc="vd")
                    prev_htmp_v[0] = v
                    wait("V", "vd", v, selfdep=True)
                    for j in range(4):
                        hc = gsub * 4 + j
                        v = op("V", lambda E, j=j, hc=hc: E.tensor_scalar(
                            out=hq_sb[:, hc, :], in0=hq16[:, hc, :],
                            scalar1=FOFF, scalar2=127.0,
                            op0=ALU.subtract, op1=ALU.min), inc="vd")
                    hq_sd = None
                    prev_htmp_v[0] = v
                    if gsub == 6 and do3:
                        ln2_bcasts = emit_ln2_rows()
                state[t2]["hq_vd"] = v
                if do3:
                    emit_ln2_apply(ln2_bcasts)
            if 0 <= it + 2 < NTB and "sq1_sd" not in state[it + 2]:
                emit_sq1(it + 2)

        wait("S", "qo", cnt["qo"], force=True)
        wait("S", "qa0", cnt["qa0"], force=True)
        wait("S", "qa1", cnt["qa1"], force=True)
        for q in ("qy0", "qy1", "qt0", "qt1", "qd"):
            if cnt[q]:
                wait("Y", q, cnt[q], force=True)
        for q in ("qw", "qx0", "qx1", "qx2"):
            wait("G", q, cnt[q], force=True)

        block = en(nc.Block())

        @block.tensor
        def _(E):
            for f in prog["T"]:
                f(E)

        @block.vector
        def _(E):
            for f in prog["V"]:
                f(E)

        @block.scalar
        def _(E):
            for f in prog["S"]:
                f(E)

        @block.gpsimd
        def _(E):
            for f in prog["G"]:
                f(E)

        @block.sync
        def _(E):
            for f in prog["Y"]:
                f(E)

    return nc


_NC_CACHE = {}


def _get_nc():
    if "nc" not in _NC_CACHE:
        _NC_CACHE["nc"] = build_nc()
    return _NC_CACHE["nc"]


def kernel(L0_attn_x, L0_attn_W_in, L0_attn_W_state, L0_attn_W_dt, L0_attn_b_dt,
           L0_attn_A_log, L0_attn_W_C, L0_attn_D, L0_attn_W_out,
           L0_ffn_w1, L0_ffn_b1, L0_ffn_w2, L0_ffn_b2):
    x = np.asarray(L0_attn_x, np.float32)
    B, Lq, Dm = x.shape
    assert (B, Lq, Dm) == (16, L, D)

    winT = np.ascontiguousarray(np.asarray(L0_attn_W_in, np.float32).T)
    wstT = np.ascontiguousarray(np.asarray(L0_attn_W_state, np.float32).T)
    wdtT = np.ascontiguousarray(np.asarray(L0_attn_W_dt, np.float32).T)
    wcT = np.ascontiguousarray(np.asarray(L0_attn_W_C, np.float32).T)
    woutT = np.ascontiguousarray(np.asarray(L0_attn_W_out, np.float32).T)
    w1T = np.ascontiguousarray(np.asarray(L0_ffn_w1, np.float32).T)
    w2T = np.ascontiguousarray(np.asarray(L0_ffn_w2, np.float32).T)

    vecs = np.zeros((128, 64), np.float32)
    vecs[:, 0] = np.asarray(L0_attn_b_dt, np.float32)
    vecs[:, 1] = np.asarray(L0_attn_A_log, np.float32)
    vecs[:, 2:10] = np.asarray(L0_attn_D, np.float32).reshape(8, 128).T
    vecs[:, 10:42] = np.asarray(L0_ffn_b1, np.float32).reshape(32, 128).T
    vecs[:, 42:50] = np.asarray(L0_ffn_b2, np.float32).reshape(8, 128).T

    nc = _get_nc()
    in_maps = []
    for c in range(8):
        xc = np.ascontiguousarray(
            x[2 * c:2 * c + 2].reshape(NTOK, D).T)
        in_maps.append({
            "xT": xc, "winT": winT, "wstT": wstT, "wdtT": wdtT, "wcT": wcT,
            "woutT": woutT, "w1T": w1T, "w2T": w2T, "vecs": vecs,
        })
    res = run_bass_kernel_spmd(nc, in_maps, core_ids=list(range(8)))
    out = np.empty((16, L, D), np.float32)
    for c in range(8):
        o = np.asarray(res.results[c]["outT"], np.float32)
        out[2 * c:2 * c + 2] = o.T.reshape(BLOC, L, D)
    return out
